# version 05
# baseline (speedup 1.0000x reference)
import numpy as np
import ml_dtypes
import concourse.bass as bass
import concourse.mybir as mybir
from concourse.bass_utils import run_bass_kernel_spmd

F32 = mybir.dt.float32
BF16 = mybir.dt.bfloat16
F32R = mybir.dt.float32r
U8 = mybir.dt.uint8
AF = mybir.ActivationFunctionType
ALU = mybir.AluOpType
AX = mybir.AxisListType

NCORES = 8
SEQS = 4
N = 2048
D = 1024
H = 16
HD = 64
NT = N // 128
NKT = N // 128
NCH = D // 128
EPS_LN = 1e-5
EPS_RMS = 1e-8
ENGS = ("sync", "gpsimd", "tensor", "vector", "scalar")
ARENA = 22528
DEBUG = False


class Sem:
    def __init__(self, handle):
        self.h = handle
        self.n = 0


class Emitter:
    def __init__(self):
        self.events = {e: [] for e in ENGS}
        self.lastw = {e: {} for e in ENGS}

    def op(self, eng, method, *args, waits=(), inc=None, inck=1, **kw):
        real = []
        for sem, val in waits:
            if val <= 0:
                continue
            if self.lastw[eng].get(id(sem), -1) >= val:
                continue
            self.lastw[eng][id(sem)] = val
            real.append((sem.h, val))
        tgt = None
        if inc is not None:
            inc.n += inck
            tgt = (inc.h, inck)

        def fn(e, _m=method, _a=args, _k=kw, _w=real, _t=tgt):
            for h, v in _w:
                e.wait_ge(h, v)
            ins = getattr(e, _m)(*_a, **_k)
            if _t is not None:
                ins.then_inc(_t[0], _t[1])
        self.events[eng].append(fn)


def build_nc():
    nc = bass.Bass("TRN2", target_bir_lowering=False)

    x_d = nc.dram_tensor("x", [SEQS * N, D], F32, kind="ExternalInput").ap()
    wqkv_d = nc.dram_tensor("wqkv", [D, 3 * D], F32, kind="ExternalInput").ap()
    wproj_d = nc.dram_tensor("wproj", [D, D], F32, kind="ExternalInput").ap()
    wt_d = nc.dram_tensor("wt", [D, D], F32, kind="ExternalInput").ap()
    bproj_d = nc.dram_tensor("bproj", [D], F32, kind="ExternalInput").ap()
    fbias_d = nc.dram_tensor("fbias", [D], F32, kind="ExternalInput").ap()
    qs_d = nc.dram_tensor("qs", [1, HD], F32, kind="ExternalInput").ap()
    ks_d = nc.dram_tensor("ks", [1, HD], F32, kind="ExternalInput").ap()
    qspat_d = nc.dram_tensor("qspat", [128, 1], F32, kind="ExternalInput").ap()
    kspat_d = nc.dram_tensor("kspat", [128, 1], F32, kind="ExternalInput").ap()
    fscale_d = nc.dram_tensor("fscale", [128, D], F32, kind="ExternalInput").ap()
    ident_d = nc.dram_tensor("ident", [128, 128], BF16, kind="ExternalInput").ap()
    sstat_d = nc.dram_tensor("sstat", [128, 32], BF16, kind="ExternalInput").ap()
    fstat_d = nc.dram_tensor("fstat", [8, 512], BF16, kind="ExternalInput").ap()
    dsel_d = nc.dram_tensor("dsel", [16, 1024], BF16, kind="ExternalInput").ap()
    onesb_d = nc.dram_tensor("onesb", [1, 128], BF16, kind="ExternalInput").ap()
    y_d = nc.dram_tensor("y", [SEQS * N, D], F32, kind="ExternalOutput").ap()
    if DEBUG:
        dbg_xnf = nc.dram_tensor("dbg_xnf", [128, NCH * N], BF16, kind="ExternalOutput").ap()
        dbg_qh = nc.dram_tensor("dbg_qh", [128, NCH * N], BF16, kind="ExternalOutput").ap()
        dbg_kh = nc.dram_tensor("dbg_kh", [128, NCH * N], BF16, kind="ExternalOutput").ap()
        dbg_vt = nc.dram_tensor("dbg_vt", [128, NKT * 1040], BF16, kind="ExternalOutput").ap()
        dbg_dn = nc.dram_tensor("dbg_dn", [16, N], BF16, kind="ExternalOutput").ap()
        dbg_oun = nc.dram_tensor("dbg_oun", [128, NCH * N], BF16, kind="ExternalOutput").ap()
        dbg_onr = nc.dram_tensor("dbg_onr", [128, NCH * N], BF16, kind="ExternalOutput").ap()
        dbg_wbf = nc.dram_tensor("dbg_wbf", [128, D], BF16, kind="ExternalOutput").ap()
        dbg_q0 = nc.dram_tensor("dbg_q0", [128, N], BF16, kind="ExternalOutput").ap()

    import contextlib
    ctx = contextlib.ExitStack()
    with ctx:
        wproj_bf = ctx.enter_context(nc.sbuf_tensor("wproj_bf", [128, 8 * D], BF16))
        ws_bf = ctx.enter_context(nc.sbuf_tensor("ws_bf", [128, 8 * D], BF16))
        ident_bf = ctx.enter_context(nc.sbuf_tensor("ident_bf", [128, 128], BF16))
        sstat_s = ctx.enter_context(nc.sbuf_tensor("sstat_s", [128, 32], BF16))
        fstat_b = ctx.enter_context(nc.sbuf_tensor("fstat_b", [8, 512], BF16))
        dsel_s = ctx.enter_context(nc.sbuf_tensor("dsel_s", [16, 1024], BF16))
        onesb_s = ctx.enter_context(nc.sbuf_tensor("onesb_s", [1, 128], BF16))
        qspat_s = ctx.enter_context(nc.sbuf_tensor("qspat_s", [128, 1], F32))
        kspat_s = ctx.enter_context(nc.sbuf_tensor("kspat_s", [128, 1], F32))
        bprow = ctx.enter_context(nc.sbuf_tensor("bprow", [1, D], BF16))
        fbrow = ctx.enter_context(nc.sbuf_tensor("fbrow", [1, D], BF16))
        qsr = ctx.enter_context(nc.sbuf_tensor("qsr", [1, HD], F32))
        ksr = ctx.enter_context(nc.sbuf_tensor("ksr", [1, HD], F32))
        negb = ctx.enter_context(nc.sbuf_tensor("negb", [128, 1], F32))
        small = ctx.enter_context(nc.sbuf_tensor("small", [128, 16], F32))
        smallb = ctx.enter_context(nc.sbuf_tensor("smallb", [1, 16], BF16))
        qh = ctx.enter_context(nc.sbuf_tensor("qh", [128, NCH * N], BF16))
        kh = ctx.enter_context(nc.sbuf_tensor("kh", [128, NCH * N], BF16))
        vt = ctx.enter_context(nc.sbuf_tensor("vt", [128, NKT * 1040], BF16))
        xnf = ctx.enter_context(nc.sbuf_tensor("xnf", [128, NCH * N], BF16))
        dn = ctx.enter_context(nc.sbuf_tensor("dn", [16, N], BF16))
        xio = ctx.enter_context(nc.sbuf_tensor("xio", [128, 2 * D], F32))
        wbf = ctx.enter_context(nc.sbuf_tensor("wbf", [128, 2 * D], BF16))
        arena = ctx.enter_context(nc.sbuf_tensor("arena", [128, ARENA], U8))
        capm = ctx.enter_context(nc.sbuf_tensor("capm", [128, 1], F32))
        ps_a = ctx.enter_context(nc.psum_tensor("ps_a", [128, 2048], F32))
        ps_b = ctx.enter_context(nc.psum_tensor("ps_b", [128, 2048], F32))
        _SY = ctx.enter_context(nc.semaphore("SY"))
        _GP = ctx.enter_context(nc.semaphore("GP"))
        _PE = ctx.enter_context(nc.semaphore("PE"))
        _AC = ctx.enter_context(nc.semaphore("AC"))
        _DV = ctx.enter_context(nc.semaphore("DV"))
        _ST = ctx.enter_context(nc.semaphore("ST"))
        _BAR = ctx.enter_context(nc.semaphore("BAR"))
        block = ctx.enter_context(nc.Block())
        SY, GP, PE, AC, DV, STm, BAR = (Sem(_SY), Sem(_GP), Sem(_PE),
                                        Sem(_AC), Sem(_DV), Sem(_ST), Sem(_BAR))
        em = Emitter()
        nc.const_aps.register(F32, EPS_LN, capm[:])
        psa_bf = ps_a[:].bitcast(BF16)
        psb_bf = ps_b[:].bitcast(BF16)

        def av(off, nbytes, dt, parts=128):
            a = arena.ap()[:, off:off + nbytes].bitcast(dt)
            if parts != 128:
                a = a[0:parts, :]
            return a

        xntm = av(0, 4096, BF16)
        sqscrA = av(4096, 4096, F32)
        q2b = av(8192, 2048, BF16)
        fqsb = av(10240, 4096, F32)
        vfm = av(8192, 4096, BF16)
        fsb = av(14336, 2048, F32, 8)
        sqf = av(16384, 2048, F32, 8)
        fse = av(18432, 2048, F32, 8)
        fhi = av(20480, 1024, BF16, 8)
        flo = av(21504, 1024, BF16, 8)
        ptb = av(0, 8192, BF16)
        dscr = av(8192, 4096, BF16, 128)
        rdnf = av(0, 8192, F32, 16)
        rdnb = av(12288, 4096, BF16, 16)
        xntmE = av(0, 4096, BF16)
        xn2 = av(4096, 4096, BF16)
        gclip = av(8192, 4096, F32)
        gact = av(12288, 4096, F32)
        sqscrE = av(16384, 4096, F32)
        wstage = av(0, 4096, F32)
        wmask = av(4096, 1024, U8)
        scb = av(8192, 4096, F32)
        nscb = av(12288, 4096, F32)

        em.op("vector", "memset", inc=DV, ap=capm[:], constant=EPS_LN)
        for dst, src in ((ident_bf[:], ident_d), (sstat_s[:], sstat_d),
                         (fstat_b[:], fstat_d), (dsel_s[:], dsel_d),
                         (onesb_s[:], onesb_d), (qspat_s[:], qspat_d),
                         (kspat_s[:], kspat_d), (qsr[:], qs_d),
                         (ksr[:], ks_d), (scb, fscale_d)):
            em.op("sync", "dma_start", inc=SY, inck=16, out=dst, in_=src)
        ld_consts = SY.n
        for j in range(NCH):
            em.op("gpsimd", "dma_start", inc=GP, inck=16,
                  out=wproj_bf[:, j * D:(j + 1) * D],
                  in_=wproj_d[j * 128:(j + 1) * 128, :])
        em.op("gpsimd", "dma_start", inc=GP, inck=16,
              out=bprow[:], in_=bproj_d.rearrange("(a b) -> a b", a=1))
        em.op("gpsimd", "dma_start", inc=GP, inck=16,
              out=fbrow[:], in_=fbias_d.rearrange("(a b) -> a b", a=1))
        gp_wfix = GP.n

        dv_sel_prev = 0
        for j in range(NCH):
            em.op("sync", "dma_start", inc=SY, inck=16,
                  out=wstage, in_=wt_d[j * 128:(j + 1) * 128, :],
                  waits=[(DV, dv_sel_prev)])
            w = [(SY, SY.n), (SY, ld_consts)]
            em.op("vector", "tensor_single_scalar", waits=w, inc=DV,
                  out=wmask, in_=wstage, scalar=0.0, op=ALU.is_ge)
            if j == 0:
                em.op("vector", "tensor_scalar_mul", out=nscb,
                      in0=scb, scalar1=-1.0)
            em.op("vector", "select", inc=DV,
                  out=ws_bf[:, j * D:(j + 1) * D], mask=wmask,
                  on_true=scb, on_false=nscb)
            dv_sel_prev = DV.n
        dv_ws = DV.n
        em.op("vector", "reduce_max", waits=[(SY, ld_consts)],
              out=small[0:1, 0:1], in_=qsr[:], axis=AX.X,
              apply_absolute_value=True)
        em.op("vector", "reduce_max", inc=DV, out=small[0:1, 1:2], in_=ksr[:],
              axis=AX.X, apply_absolute_value=True)
        em.op("vector", "tensor_mul", waits=[(DV, DV.n)], inc=DV,
              out=small[0:1, 2:3],
              in0=small[0:1, 0:1], in1=small[0:1, 1:2])
        em.op("vector", "tensor_scalar_mul", waits=[(DV, DV.n)], inc=DV,
              out=small[0:1, 3:4],
              in0=small[0:1, 2:3], scalar1=-8.0)
        em.op("vector", "tensor_copy", waits=[(DV, DV.n)], inc=DV,
              out=smallb[0:1, 0:1], in_=small[0:1, 3:4])
        em.op("tensor", "matmul", waits=[(DV, DV.n)], inc=PE,
              out=ps_b[0:128, 0:1], lhsT=onesb_s[:],
              rhs=smallb[0:1, 0:1], start=True, stop=True)
        em.op("vector", "tensor_copy", waits=[(PE, PE.n)], inc=DV,
              out=negb[:], in_=ps_b[0:128, 0:1])
        dv_negb = DV.n

        st_total = [0]

        def emit_ln(t, s, sy_val, xt, out_tile, dv_var_prev, sqdump, extraw=()):
            def dve(*a, waits=(), **kw):
                em.op("vector", *a, waits=list(waits) + [(DV, DV.n)], inc=DV, **kw)
            w0 = list(extraw) + [(SY, sy_val)]
            em.op("vector", "reduce_sum", waits=w0, inc=DV, out=small[:, 0:1],
                  in_=xt, axis=AX.X)
            wa = list(w0)
            if dv_var_prev:
                wa.append((DV, dv_var_prev))
            em.op("scalar", "activation", waits=wa, inc=AC,
                  out=sqdump, in_=xt, func=AF.Square, accum_out=small[:, 1:2])
            ac_sq = AC.n
            dve("tensor_scalar_mul", waits=[(AC, ac_sq)],
                out=small[:, 2:3], in0=small[:, 0:1], scalar1=1.0 / D)
            dve("tensor_mul", out=small[:, 3:4],
                in0=small[:, 2:3], in1=small[:, 2:3])
            dve("scalar_tensor_tensor", out=small[:, 4:5], in0=small[:, 1:2],
                scalar=1.0 / D, in1=small[:, 3:4], op0=ALU.mult,
                op1=ALU.subtract)
            dve("tensor_scalar_add", out=small[:, 4:5], in0=small[:, 4:5],
                scalar1=EPS_LN)
            dv_var = DV.n
            em.op("scalar", "activation", waits=[(DV, dv_var)], inc=AC,
                  out=small[:, 5:6], in_=small[:, 4:5], func=AF.Sqrt,
                  bias=0.0, scale=1.0)
            dve("reciprocal", waits=[(AC, AC.n)],
                out=small[:, 6:7], in_=small[:, 5:6])
            dve("tensor_mul", out=small[:, 8:9],
                in0=small[:, 6:7], in1=small[:, 6:7])
            dve("tensor_mul", out=small[:, 8:9],
                in0=small[:, 8:9], in1=small[:, 4:5])
            dve("tensor_scalar", out=small[:, 8:9], in0=small[:, 8:9],
                scalar1=-0.5, scalar2=1.5, op0=ALU.mult, op1=ALU.add)
            dve("tensor_mul", out=small[:, 6:7],
                in0=small[:, 6:7], in1=small[:, 8:9])
            dve("scalar_tensor_tensor", out=small[:, 7:8], in0=small[:, 2:3],
                scalar=-1.0, in1=small[:, 6:7], op0=ALU.mult, op1=ALU.mult)
            dve("tensor_scalar", out=out_tile, in0=xt,
                scalar1=small[:, 6:7], scalar2=small[:, 7:8],
                op0=ALU.mult, op1=ALU.add)
            return dv_var, DV.n

        for s in range(SEQS):
            barw = [(BAR, 5 * s)]

            dv_apply = {}
            dv_trc = {}
            dv_varA = 0
            for t in range(NT):
                p = t % 2
                w = list(barw)
                if t >= 2:
                    w.append((DV, dv_apply[t - 2]))
                em.op("sync", "dma_start", waits=w, inc=SY, inck=16,
                      out=xio[:, p * D:(p + 1) * D],
                      in_=x_d[(s * NT + t) * 128:(s * NT + t + 1) * 128, :])
                xt = xio[:, p * D:(p + 1) * D]
                dv_varA, dv_apply[t] = emit_ln(
                    t, s, SY.n, xt, xntm[:, p * D:(p + 1) * D], dv_varA,
                    sqscrA, extraw=barw)
                w = [(DV, dv_apply[t])]
                if t >= 1:
                    w.append((DV, dv_trc[t - 1]))
                for j in range(NCH):
                    em.op("tensor", "transpose", waits=w if j == 0 else (),
                          inc=(PE if j == NCH - 1 else None),
                          out=psa_bf[:, j * 128:(j + 1) * 128],
                          in_=xntm[:, p * D + j * 128:p * D + (j + 1) * 128],
                          identity=ident_bf[:])
                em.op("vector", "tensor_copy", waits=[(PE, PE.n)], inc=DV,
                      out=xnf.ap().rearrange("p (c n) -> p c n", c=NCH)[
                          :, :, t * 128:(t + 1) * 128],
                      in_=psa_bf[:, 0:D].rearrange("p (c n) -> p c n", c=NCH))
                dv_trc[t] = DV.n
            dv_xnf = DV.n

            gp_w = {}
            pe_last_wbf = {}
            dv_stt = {}
            pe_qk = {}
            pe_ssm = {}
            ac_sqr = {}
            ac_sqrt = {}
            dv_f = {}
            pe_fm = {}
            dv_fev = {}
            pe_vm = {}
            dv_vev = {}
            dv_vtc = {}

            def load_wchunk(m):
                w = list(barw)
                if m >= 2:
                    w.append((PE, pe_last_wbf[m - 2]))
                em.op("gpsimd", "dma_start", waits=w, inc=GP, inck=16,
                      out=wbf[:, (m % 2) * D:(m % 2 + 1) * D].rearrange(
                          "p (j c) -> p j c", j=NCH),
                      in_=wqkv_d[:, m * 128:(m + 1) * 128].rearrange(
                          "(j p) c -> p j c", j=NCH))
                gp_w[m] = GP.n

            load_wchunk(0)
            load_wchunk(1)

            for m in range(16):
                isq = m < 8
                dst = qh if isq else kh
                ch = m % 8
                pat = qspat_s if isq else kspat_s
                for qtr in range(4):
                    w = [(GP, gp_w[m]), (DV, dv_xnf)]
                    if m >= 1:
                        w.append((DV, dv_stt[(m - 1, qtr)]))
                    for j in range(NCH):
                        em.op("tensor", "matmul", waits=w if j == 0 else (),
                              inc=(PE if j == NCH - 1 else None),
                              out=ps_a[:, qtr * 512:(qtr + 1) * 512],
                              lhsT=wbf[:, (m % 2) * D + j * 128:
                                       (m % 2) * D + (j + 1) * 128],
                              rhs=xnf[:, j * N + qtr * 512:
                                      j * N + (qtr + 1) * 512],
                              start=(j == 0), stop=(j == NCH - 1))
                    pe_qk[(m, qtr)] = PE.n
                    w = [(PE, pe_qk[(m, qtr)])]
                    if (m, qtr) != (0, 0):
                        pm, pq = (m, qtr - 2) if qtr >= 2 else (m - 1, qtr + 2)
                        if (pm, pq) in pe_ssm:
                            w.append((PE, pe_ssm[(pm, pq)]))
                    em.op("scalar", "activation", waits=w, inc=AC,
                          out=q2b[:, (qtr % 2) * 512:(qtr % 2 + 1) * 512],
                          in_=ps_a[:, qtr * 512:(qtr + 1) * 512],
                          func=AF.Square)
                    ac_sqr[(m, qtr)] = AC.n
                    w = [(AC, ac_sqr[(m, qtr)])]
                    if qtr == 0 and m >= 1:
                        w.append((AC, ac_sqrt[m - 1]))
                    em.op("tensor", "matmul", waits=w, inc=PE,
                          out=ps_b[0:8, 0:512],
                          lhsT=sstat_s[:, qtr * 8:(qtr + 1) * 8],
                          rhs=q2b[:, (qtr % 2) * 512:(qtr % 2 + 1) * 512],
                          start=(qtr == 0), stop=(qtr == 3))
                    pe_ssm[(m, qtr)] = PE.n
                if DEBUG and s == 0 and m == 0:
                    em.op("gpsimd", "dma_start", waits=[(GP, gp_w[0])], inc=GP,
                          inck=16, out=dbg_wbf, in_=wbf[:, 0:D])
                    em.op("vector", "tensor_copy", waits=[(PE, pe_qk[(0, 3)])],
                          inc=DV, out=xntm, in_=ps_a[:])
                    em.op("gpsimd", "dma_start", waits=[(DV, DV.n)], inc=GP,
                          inck=16, out=dbg_q0, in_=xntm)
                pe_last_wbf[m] = pe_qk[(m, 3)]
                if m + 2 < 24:
                    load_wchunk(m + 2)
                w = [(PE, pe_ssm[(m, 3)])]
                if m >= 1:
                    w.append((PE, pe_fm[(m - 1, 3)]))
                em.op("scalar", "activation", waits=w, inc=AC,
                      out=sqf, in_=ps_b[0:8, 0:512], func=AF.Sqrt,
                      bias=0.0, scale=1.0 / 64.0)
                em.op("scalar", "activation", inc=AC,
                      out=fse, in_=ps_b[0:8, 0:512], func=AF.Copy,
                      bias=0.0, scale=1.0 / 64.0)
                ac_sqrt[m] = AC.n
                w = [(AC, ac_sqrt[m])]
                if m >= 1:
                    w.append((PE, pe_fm[(m - 1, 3)]))
                def dvb(*a, waits=(), **kw):
                    em.op("vector", *a, waits=list(waits) + [(DV, DV.n)],
                          inc=DV, **kw)
                dvb("reciprocal", waits=w, out=fsb, in_=sqf)
                dvb("tensor_mul", out=fsb, in0=fsb, in1=fse)
                dvb("tensor_add", out=fsb, in0=fsb, in1=sqf)
                dvb("tensor_scalar", out=fsb, in0=fsb, scalar1=0.5,
                    scalar2=EPS_RMS, op0=ALU.mult, op1=ALU.add)
                dvb("reciprocal", out=sqf, in_=fsb)
                dvb("tensor_copy", out=fhi, in_=sqf)
                dvb("tensor_sub", out=fse, in0=sqf, in1=fhi)
                dvb("tensor_copy", out=flo, in_=fse)
                dv_f[m] = DV.n
                for qtr in range(4):
                    fb = 512 + (qtr % 2) * 512
                    w = [(DV, dv_f[m])]
                    if qtr >= 2:
                        w.append((DV, dv_fev[(m, qtr - 2)]))
                    elif m >= 1:
                        w.append((DV, dv_fev[(m - 1, qtr + 2)]))
                    em.op("tensor", "matmul", waits=w,
                          out=ps_b[:, fb:fb + 512],
                          lhsT=fstat_b[:, qtr * 128:(qtr + 1) * 128],
                          rhs=fhi, start=True, stop=False)
                    em.op("tensor", "matmul", inc=PE,
                          out=ps_b[:, fb:fb + 512],
                          lhsT=fstat_b[:, qtr * 128:(qtr + 1) * 128],
                          rhs=flo, start=False, stop=True)
                    pe_fm[(m, qtr)] = PE.n
                    em.op("vector", "tensor_copy", waits=[(PE, pe_fm[(m, qtr)])],
                          inc=DV,
                          out=fqsb[:, (qtr % 2) * 512:(qtr % 2 + 1) * 512],
                          in_=ps_b[:, fb:fb + 512])
                    dv_fev[(m, qtr)] = DV.n
                    em.op("vector", "scalar_tensor_tensor", inc=DV,
                          waits=[(DV, dv_fev[(m, qtr)])],
                          out=dst[:, ch * N + qtr * 512:ch * N + (qtr + 1) * 512],
                          in0=ps_a[:, qtr * 512:(qtr + 1) * 512], scalar=pat[:],
                          in1=fqsb[:, (qtr % 2) * 512:(qtr % 2 + 1) * 512],
                          op0=ALU.mult, op1=ALU.mult)
                    dv_stt[(m, qtr)] = DV.n

            if DEBUG and s == 0:
                em.op("gpsimd", "dma_start", waits=[(DV, dv_xnf)], inc=GP, inck=16,
                      out=dbg_xnf, in_=xnf[:])
                em.op("gpsimd", "dma_start", waits=[(DV, dv_stt[(15, 3)])],
                      inc=GP, inck=16, out=dbg_qh, in_=qh[:])
                em.op("gpsimd", "dma_start", waits=[(DV, dv_stt[(15, 3)])],
                      inc=GP, inck=16, out=dbg_kh, in_=kh[:])
            if s == 0:
                em.op("vector", "memset", inc=DV, ap=vt[:], constant=1.0)
            dv_vinit = DV.n
            for vm in range(8):
                m = 16 + vm
                for qtr in range(4):
                    w = [(GP, gp_w[m]), (DV, dv_stt[(15, 3)]), (DV, dv_vinit)]
                    if vm >= 1:
                        w.append((DV, dv_vev[vm - 1]))
                    for j in range(NCH):
                        em.op("tensor", "matmul", waits=w if j == 0 else (),
                              inc=(PE if j == NCH - 1 else None),
                              out=ps_a[:, qtr * 512:(qtr + 1) * 512],
                              lhsT=wbf[:, (m % 2) * D + j * 128:
                                       (m % 2) * D + (j + 1) * 128],
                              rhs=xnf[:, j * N + qtr * 512:
                                      j * N + (qtr + 1) * 512],
                              start=(j == 0), stop=(j == NCH - 1))
                pe_vm[m] = PE.n
                pe_last_wbf[m] = pe_vm[m]
                if m + 2 < 24:
                    load_wchunk(m + 2)
                w = [(PE, pe_vm[m])]
                if vm == 0:
                    w.append((DV, dv_stt[(15, 3)]))
                em.op("vector", "tensor_copy", waits=w, inc=DV,
                      out=vfm, in_=ps_a[:])
                dv_vev[vm] = DV.n
                for grp in range(4):
                    w = [(DV, dv_vev[vm])]
                    if (vm, grp) != (0, 0):
                        pv, pg = (vm, grp - 1) if grp >= 1 else (vm - 1, 3)
                        w.append((DV, dv_vtc[(pv, pg)]))
                    for kk in range(4):
                        kt = grp * 4 + kk
                        em.op("tensor", "transpose", waits=w if kk == 0 else (),
                              inc=(PE if kk == 3 else None),
                              out=psb_bf[:, 3072 + kk * 128:3072 + (kk + 1) * 128],
                              in_=vfm[:, kt * 128:(kt + 1) * 128],
                              identity=ident_bf[:])
                    em.op("vector", "tensor_copy", waits=[(PE, PE.n)], inc=DV,
                          out=vt.ap().rearrange("p (kt c) -> p kt c", kt=NKT)[
                              :, grp * 4:(grp + 1) * 4,
                              130 * vm:130 * vm + 130].rearrange(
                              "p kt (h2 c) -> p kt h2 c", c=65)[:, :, :, 0:64],
                          in_=psb_bf[:, 3072:3584].rearrange(
                              "p (kt c) -> p kt c", kt=4).rearrange(
                              "p kt (h2 c) -> p kt h2 c", c=64))
                    dv_vtc[(vm, grp)] = DV.n
            dv_vdone = DV.n

            if DEBUG and s == 0:
                em.op("gpsimd", "dma_start", waits=[(DV, dv_vdone)], inc=GP,
                      inck=16, out=dbg_vt, in_=vt[:])
            order = [jj // 2 + 8 * (jj % 2) for jj in range(16)]
            pe_s = {}
            ac_e = {}
            pe_pv = {}
            dv_oe = {}
            gp_dn = {}
            units = 2 * NKT
            npair = NKT

            for hi, h in enumerate(order):
                ch = h % 8
                odd = h >= 8
                rowo = 64 if odd else 0
                drow = 0 if odd else 64

                def s_unit(u, hi=hi, ch=ch, rowo=rowo):
                    qha, kt = u // NKT, u % NKT
                    pr = u // 2
                    w = []
                    if hi == 0 and u == 0:
                        w += [(DV, dv_vdone), (DV, dv_negb)]
                    if pr >= 1:
                        w.append((AC, ac_e.get((hi, pr - 1), 0)))
                    elif hi > 0:
                        w.append((AC, ac_e[(hi - 1, units // 2 - 1)]))
                    for qtr in range(2):
                        em.op("tensor", "matmul", waits=w if qtr == 0 else (),
                              inc=(PE if qtr == 1 else None),
                              out=ps_a[:, (u % 2) * 1024 + qtr * 512:
                                       (u % 2) * 1024 + (qtr + 1) * 512],
                              lhsT=kh[rowo:rowo + 64,
                                      ch * N + kt * 128:ch * N + (kt + 1) * 128],
                              rhs=qh[rowo:rowo + 64,
                                     ch * N + qha * 1024 + qtr * 512:
                                     ch * N + qha * 1024 + (qtr + 1) * 512],
                              start=True, stop=True)
                    pe_s[(hi, u)] = PE.n

                def e_pair(pr, hi=hi):
                    w = [(PE, pe_s[(hi, 2 * pr + 1)])]
                    if pr >= 2:
                        w.append((PE, pe_pv[(hi, 2 * pr - 3)]))
                    elif hi > 0:
                        w.append((PE, pe_pv[(hi - 1, units - 2 + 2 * pr - 1)]))
                    em.op("scalar", "activation", waits=w, inc=AC,
                          out=ptb[:, (pr % 2) * 2048:(pr % 2) * 2048 + 2048],
                          in_=ps_a[:],
                          func=AF.Exp, bias=negb[:], scale=0.125)
                    ac_e[(hi, pr)] = AC.n

                def pv_unit(u, hi=hi, h=h, odd=odd):
                    qha, kt = u // NKT, u % NKT
                    pr = u // 2
                    w = [(AC, ac_e[(hi, pr)])]
                    if kt == 0 and hi > 0:
                        w.append((DV, dv_oe[(hi - 1, qha)]))
                    vcol = kt * 1040 + 130 * (h % 8) + (65 if odd else 0)
                    for qtr in range(2):
                        ob = qha * 1024 + qtr * 512
                        pcol = (pr % 2) * 2048 + (u % 2) * 1024 + qtr * 512
                        if not odd:
                            em.op("tensor", "matmul", waits=w if qtr == 0 else (),
                                  inc=(PE if qtr == 1 else None),
                                  out=ps_b[0:65, ob:ob + 512],
                                  lhsT=vt[:, vcol:vcol + 65],
                                  rhs=ptb[:, pcol:pcol + 512],
                                  start=(kt == 0), stop=(kt == NKT - 1))
                        else:
                            em.op("tensor", "matmul", waits=w if qtr == 0 else (),
                                  out=ps_b[64:128, ob:ob + 512],
                                  lhsT=vt[:, vcol:vcol + 64],
                                  rhs=ptb[:, pcol:pcol + 512],
                                  start=(kt == 0), stop=(kt == NKT - 1),
                                  tile_position=(0, 64))
                            em.op("tensor", "matmul",
                                  inc=(PE if qtr == 1 else None),
                                  out=ps_b[0:1, ob:ob + 512],
                                  lhsT=vt[:, vcol + 64:vcol + 65],
                                  rhs=ptb[:, pcol:pcol + 512],
                                  start=(kt == 0), stop=(kt == NKT - 1),
                                  tile_position=(0, 0))
                    pe_pv[(hi, u)] = PE.n

                def evac_region(qha, hi=hi, h=h, odd=odd, rowo=rowo, drow=drow):
                    w = [(PE, pe_pv[(hi, qha * NKT + NKT - 1)])]
                    em.op("vector", "tensor_copy", waits=w, inc=DV,
                          out=qh[rowo:rowo + 64,
                                 (h % 8) * N + qha * 1024:(h % 8) * N + (qha + 1) * 1024],
                          in_=ps_b[rowo:rowo + 64, qha * 1024:(qha + 1) * 1024])
                    w2 = []
                    if hi >= 2:
                        w2 = [(GP, gp_dn[hi - 2])]
                    em.op("vector", "tensor_copy", waits=w2, inc=DV,
                          out=dscr[drow:drow + 1, qha * 1024:(qha + 1) * 1024],
                          in_=ps_b[drow:drow + 1, qha * 1024:(qha + 1) * 1024])
                    dv_oe[(hi, qha)] = DV.n

                s_unit(0)
                s_unit(1)
                for pr in range(units // 2):
                    e_pair(pr)
                    pv_unit(2 * pr)
                    pv_unit(2 * pr + 1)
                    if 2 * pr + 2 < units:
                        s_unit(2 * pr + 2)
                        s_unit(2 * pr + 3)
                    if 2 * pr + 1 == NKT - 1:
                        evac_region(0)
                evac_region(1)
                em.op("gpsimd", "dma_start",
                      waits=[(DV, dv_oe[(hi, 1)])], inc=GP, inck=16,
                      out=dn[h:h + 1, :], in_=dscr[drow:drow + 1, :])
                gp_dn[hi] = GP.n

            if DEBUG and s == 0:
                em.op("gpsimd", "dma_start", waits=[(GP, gp_dn[15]), (DV, dv_oe[15])],
                      inc=GP, inck=16, out=dbg_dn, in_=dn[:])
                em.op("gpsimd", "dma_start", waits=[(DV, dv_oe[15])], inc=GP,
                      inck=16, out=dbg_oun, in_=qh[:])
            em.op("vector", "reciprocal", waits=[(GP, gp_dn[15])], inc=DV,
                  out=rdnf, in_=dn[:])
            em.op("vector", "tensor_copy", waits=[(DV, DV.n)], inc=DV,
                  out=rdnb, in_=rdnf)
            dv_rdn = DV.n
            pe_fd = {}
            dv_on = {}
            for j in range(NCH):
                for ha in range(2):
                    w = [(DV, dv_rdn), (AC, ac_e[(15, units // 2 - 1)])]
                    if (j, ha) != (0, 0):
                        pj, ph = (j, 0) if ha == 1 else (j - 1, 1)
                        w.append((DV, dv_on[(pj, ph)]))
                    for qtr in range(2):
                        em.op("tensor", "matmul", waits=w if qtr == 0 else (),
                              inc=(PE if qtr == 1 else None),
                              out=ps_a[:, ha * 1024 + qtr * 512:
                                       ha * 1024 + (qtr + 1) * 512],
                              lhsT=dsel_s[:, j * 128:(j + 1) * 128],
                              rhs=rdnb[:, ha * 1024 + qtr * 512:
                                       ha * 1024 + (qtr + 1) * 512],
                              start=True, stop=True)
                    pe_fd[(j, ha)] = PE.n
                    em.op("vector", "tensor_tensor",
                          waits=[(PE, pe_fd[(j, ha)])], inc=DV,
                          out=kh[:, j * N + ha * 1024:j * N + (ha + 1) * 1024],
                          in0=qh[:, j * N + ha * 1024:j * N + (ha + 1) * 1024],
                          in1=ps_a[:, ha * 1024:(ha + 1) * 1024], op=ALU.mult)
                    dv_on[(j, ha)] = DV.n
            dv_onorm = DV.n

            if DEBUG and s == 0:
                em.op("gpsimd", "dma_start", waits=[(DV, dv_onorm)], inc=GP,
                      inck=16, out=dbg_onr, in_=kh[:])
            dv_tr2 = {}
            dv_clip = {}
            dv_epi = {}
            st_done = {}
            dv_varE = 0
            for t in range(NT):
                p = t % 2
                w = [(DV, dv_onorm)]
                if t >= 2:
                    w.append((DV, dv_epi[2 * (t - 2) + 1]))
                em.op("sync", "dma_start", waits=w, inc=SY, inck=16,
                      out=xio[:, p * D:(p + 1) * D],
                      in_=x_d[(s * NT + t) * 128:(s * NT + t + 1) * 128, :])
                xt = xio[:, p * D:(p + 1) * D]
                dv_varE, dv_ap2 = emit_ln(
                    t, s, SY.n, xt, xntmE[:, p * D:(p + 1) * D], dv_varE,
                    sqscrE, extraw=[(DV, dv_onorm)])
                w = [(DV, dv_ap2)]
                if t >= 1:
                    w.append((DV, dv_tr2[t - 1]))
                for j in range(NCH):
                    em.op("tensor", "transpose", waits=w if j == 0 else (),
                          inc=(PE if j == NCH - 1 else None),
                          out=psa_bf[:, j * 128:(j + 1) * 128],
                          in_=xntmE[:, p * D + j * 128:p * D + (j + 1) * 128],
                          identity=ident_bf[:])
                pe_t2 = PE.n
                em.op("vector", "tensor_copy", waits=[(PE, pe_t2)], inc=DV,
                      out=xn2[:, p * D:(p + 1) * D], in_=psa_bf[:, 0:D])
                dv_tr2[t] = DV.n
                for fh in range(2):
                    ci = 2 * t + fh
                    sb = (ci % 4) * 512
                    gb = (ci % 3 + 1) * 512
                    w = [(DV, dv_tr2[t]), (GP, gp_wfix)]
                    if ci >= 4:
                        w.append((DV, dv_epi[ci - 4]))
                    for j in range(NCH):
                        em.op("tensor", "matmul", waits=w if j == 0 else (),
                              out=ps_b[:, sb:sb + 512],
                              lhsT=kh[:, j * N + t * 128:j * N + (t + 1) * 128],
                              rhs=wproj_bf[:, j * D + fh * 512:
                                           j * D + fh * 512 + 512],
                              start=(j == 0), stop=False)
                    em.op("tensor", "matmul",
                          out=ps_b[:, sb:sb + 512], lhsT=onesb_s[:],
                          rhs=bprow[:, fh * 512:fh * 512 + 512],
                          start=False, stop=True)
                    w = []
                    if ci >= 3:
                        w.append((DV, dv_clip[ci - 3]))
                    for j in range(NCH):
                        em.op("tensor", "matmul", waits=w if j == 0 else (),
                              out=ps_a[:, gb:gb + 512],
                              lhsT=xn2[:, p * D + j * 128:p * D + (j + 1) * 128],
                              rhs=ws_bf[:, j * D + fh * 512:
                                        j * D + fh * 512 + 512],
                              start=(j == 0), stop=False)
                    em.op("tensor", "matmul", inc=PE,
                          out=ps_a[:, gb:gb + 512], lhsT=onesb_s[:],
                          rhs=fbrow[:, fh * 512:fh * 512 + 512],
                          start=False, stop=True)
                    pe_sg = PE.n
                    em.op("vector", "tensor_scalar", waits=[(PE, pe_sg)],
                          inc=DV,
                          out=gclip[:, fh * 512:(fh + 1) * 512],
                          in0=ps_a[:, gb:gb + 512], scalar1=-100.0,
                          scalar2=100.0, op0=ALU.max, op1=ALU.min)
                    dv_clip[ci] = DV.n
                    w = [(DV, dv_clip[ci])]
                    if ci >= 2:
                        w.append((STm, st_done[ci - 2]))
                    em.op("scalar", "activation", waits=w, inc=AC,
                          out=gact[:, fh * 512:(fh + 1) * 512],
                          in_=gclip[:, fh * 512:(fh + 1) * 512], func=AF.Silu)
                    em.op("vector", "scalar_tensor_tensor", waits=[(AC, AC.n)],
                          inc=DV,
                          out=gclip[:, fh * 512:(fh + 1) * 512],
                          in0=ps_b[:, sb:sb + 512], scalar=1.0,
                          in1=gact[:, fh * 512:(fh + 1) * 512],
                          op0=ALU.mult, op1=ALU.mult)
                    em.op("vector", "tensor_tensor", waits=[(DV, DV.n)], inc=DV,
                          out=gact[:, fh * 512:(fh + 1) * 512],
                          in0=gclip[:, fh * 512:(fh + 1) * 512],
                          in1=xt[:, fh * 512:(fh + 1) * 512], op=ALU.add)
                    dv_epi[ci] = DV.n
                    em.op("sync", "dma_start", waits=[(DV, dv_epi[ci])],
                          inc=STm, inck=16,
                          out=y_d[(s * NT + t) * 128:(s * NT + t + 1) * 128,
                                  fh * 512:(fh + 1) * 512],
                          in_=gact[:, fh * 512:(fh + 1) * 512])
                    st_done[ci] = STm.n
                    st_total[0] = STm.n

            em.op("sync", "nop", waits=[(STm, st_total[0])], inc=BAR)
            em.op("gpsimd", "nop", waits=[(GP, GP.n)], inc=BAR)
            em.op("tensor", "nop", inc=BAR)
            em.op("vector", "nop", inc=BAR)
            em.op("scalar", "nop", inc=BAR)

        em.op("sync", "nop", waits=[(STm, st_total[0])])

        @block.sync
        def _(sync):
            for fn in em.events["sync"]:
                fn(sync)

        @block.gpsimd
        def _(gpsimd):
            for fn in em.events["gpsimd"]:
                fn(gpsimd)

        @block.tensor
        def _(tensor):
            for fn in em.events["tensor"]:
                fn(tensor)

        @block.vector
        def _(vector):
            for fn in em.events["vector"]:
                fn(vector)

        @block.scalar
        def _(scalar):
            for fn in em.events["scalar"]:
                fn(scalar)

    return nc


_NC_CACHE = None


def _get_nc():
    global _NC_CACHE
    if _NC_CACHE is None:
        _NC_CACHE = build_nc()
    return _NC_CACHE


def kernel(L0_attn_x, L0_attn_Wqkv, L0_attn_Wproj, L0_attn_bproj,
           L0_attn_q_scale, L0_attn_k_scale, L0_ffn_weights, L0_ffn_scale,
           L0_ffn_bias):
    x = np.asarray(L0_attn_x, dtype=np.float32)
    wqkv = np.asarray(L0_attn_Wqkv, dtype=np.float32)
    wproj = np.asarray(L0_attn_Wproj, dtype=np.float32)
    bproj = np.asarray(L0_attn_bproj, dtype=np.float32)
    qs = np.asarray(L0_attn_q_scale, dtype=np.float32)
    ks = np.asarray(L0_attn_k_scale, dtype=np.float32)
    wf = np.asarray(L0_ffn_weights, dtype=np.float32)
    fsc = np.asarray(L0_ffn_scale, dtype=np.float32)
    fbi = np.asarray(L0_ffn_bias, dtype=np.float32)

    wq = wqkv[:, 0:D].reshape(D, H, HD)
    wk = wqkv[:, D:2 * D].reshape(D, H, HD)
    wv = wqkv[:, 2 * D:3 * D].reshape(D, H, HD)
    cols = []
    for part in (wq, wk, wv):
        for j in range(8):
            cols.append(part[:, j, :])
            cols.append(part[:, j + 8, :])
    wqkv_p = np.ascontiguousarray(np.concatenate(cols, axis=1))
    wp = wproj.reshape(H, HD, D)
    rows = []
    for j in range(8):
        rows.append(wp[j])
        rows.append(wp[j + 8])
    wproj_p = np.ascontiguousarray(np.concatenate(rows, axis=0))
    wt = np.ascontiguousarray(wf.T)
    qspat = np.ascontiguousarray(np.tile(qs, 2)[:, None])
    kspat = np.ascontiguousarray(np.tile(ks, 2)[:, None])
    fscale_b = np.ascontiguousarray(np.broadcast_to(fsc[None, :], (128, D)))

    bf = ml_dtypes.bfloat16
    ident = np.eye(128, dtype=np.float32).astype(bf)
    sstat = np.zeros((128, 32), np.float32)
    for i in range(4):
        sstat[0:64, 8 * i + 2 * i] = 1.0
        sstat[64:128, 8 * i + 2 * i + 1] = 1.0
    sstat = sstat.astype(bf)
    fstat = np.zeros((8, 512), np.float32)
    for i in range(4):
        fstat[2 * i, 128 * i:128 * i + 64] = 1.0
        fstat[2 * i + 1, 128 * i + 64:128 * i + 128] = 1.0
    fstat = fstat.astype(bf)
    dsel = np.zeros((16, 1024), np.float32)
    for j in range(8):
        dsel[j, j * 128:j * 128 + 64] = 1.0
        dsel[j + 8, j * 128 + 64:j * 128 + 128] = 1.0
    dsel = dsel.astype(bf)
    onesb = np.ones((1, 128), np.float32).astype(bf)

    nc = _get_nc()
    xr = x.reshape(32, N, D)
    in_maps = []
    for c in range(NCORES):
        in_maps.append({
            "x": np.ascontiguousarray(
                xr[SEQS * c:SEQS * (c + 1)].reshape(SEQS * N, D)),
            "wqkv": wqkv_p, "wproj": wproj_p, "wt": wt,
            "bproj": bproj, "fbias": fbi,
            "qs": qs[None, :], "ks": ks[None, :],
            "qspat": qspat, "kspat": kspat, "fscale": fscale_b,
            "ident": ident, "sstat": sstat, "fstat": fstat,
            "dsel": dsel, "onesb": onesb,
        })
    res = run_bass_kernel_spmd(nc, in_maps, core_ids=list(range(NCORES)))
    out = np.concatenate([r["y"].reshape(SEQS, N, D) for r in res.results],
                         axis=0)
    return out


# version 06
# speedup vs baseline: 1.2881x; 1.2881x over previous
import numpy as np
import ml_dtypes
import concourse.bass as bass
import concourse.mybir as mybir
from concourse.bass_utils import run_bass_kernel_spmd

F32 = mybir.dt.float32
BF16 = mybir.dt.bfloat16
F32R = mybir.dt.float32r
U8 = mybir.dt.uint8
AF = mybir.ActivationFunctionType
ALU = mybir.AluOpType
AX = mybir.AxisListType

NCORES = 8
SEQS = 4
N = 2048
D = 1024
H = 16
HD = 64
NT = N // 128
NKT = N // 128
NCH = D // 128
EPS_LN = 1e-5
EPS_RMS = 1e-8
ENGS = ("sync", "gpsimd", "tensor", "vector", "scalar")
ARENA = 22528
DEBUG = False


class Sem:
    def __init__(self, handle):
        self.h = handle
        self.n = 0


class Emitter:
    def __init__(self):
        self.events = {e: [] for e in ENGS}
        self.lastw = {e: {} for e in ENGS}

    def op(self, eng, method, *args, waits=(), inc=None, inck=1, **kw):
        real = []
        for sem, val in waits:
            if val <= 0:
                continue
            if self.lastw[eng].get(id(sem), -1) >= val:
                continue
            self.lastw[eng][id(sem)] = val
            real.append((sem.h, val))
        tgt = None
        if inc is not None:
            inc.n += inck
            tgt = (inc.h, inck)

        def fn(e, _m=method, _a=args, _k=kw, _w=real, _t=tgt):
            for h, v in _w:
                e.wait_ge(h, v)
            ins = getattr(e, _m)(*_a, **_k)
            if _t is not None:
                ins.then_inc(_t[0], _t[1])
        self.events[eng].append(fn)


def build_nc():
    nc = bass.Bass("TRN2", target_bir_lowering=False)

    x_d = nc.dram_tensor("x", [SEQS * N, D], F32, kind="ExternalInput").ap()
    wqkv_d = nc.dram_tensor("wqkv", [D, 3 * D], F32, kind="ExternalInput").ap()
    wproj_d = nc.dram_tensor("wproj", [D, D], F32, kind="ExternalInput").ap()
    wt_d = nc.dram_tensor("wt", [D, D], F32, kind="ExternalInput").ap()
    bproj_d = nc.dram_tensor("bproj", [D], F32, kind="ExternalInput").ap()
    fbias_d = nc.dram_tensor("fbias", [D], F32, kind="ExternalInput").ap()
    qs_d = nc.dram_tensor("qs", [1, HD], F32, kind="ExternalInput").ap()
    ks_d = nc.dram_tensor("ks", [1, HD], F32, kind="ExternalInput").ap()
    qspat_d = nc.dram_tensor("qspat", [128, 1], F32, kind="ExternalInput").ap()
    kspat_d = nc.dram_tensor("kspat", [128, 1], F32, kind="ExternalInput").ap()
    fscale_d = nc.dram_tensor("fscale", [128, D], F32, kind="ExternalInput").ap()
    ident_d = nc.dram_tensor("ident", [128, 128], BF16, kind="ExternalInput").ap()
    sstat_d = nc.dram_tensor("sstat", [128, 32], BF16, kind="ExternalInput").ap()
    fstat_d = nc.dram_tensor("fstat", [8, 512], BF16, kind="ExternalInput").ap()
    dsel_d = nc.dram_tensor("dsel", [16, 1024], BF16, kind="ExternalInput").ap()
    onesb_d = nc.dram_tensor("onesb", [1, 128], BF16, kind="ExternalInput").ap()
    y_d = nc.dram_tensor("y", [SEQS * N, D], F32, kind="ExternalOutput").ap()
    if DEBUG:
        dbg_xnf = nc.dram_tensor("dbg_xnf", [128, NCH * N], BF16, kind="ExternalOutput").ap()
        dbg_qh = nc.dram_tensor("dbg_qh", [128, NCH * N], BF16, kind="ExternalOutput").ap()
        dbg_kh = nc.dram_tensor("dbg_kh", [128, NCH * N], BF16, kind="ExternalOutput").ap()
        dbg_vt = nc.dram_tensor("dbg_vt", [128, NKT * 1040], BF16, kind="ExternalOutput").ap()
        dbg_dn = nc.dram_tensor("dbg_dn", [16, N], BF16, kind="ExternalOutput").ap()
        dbg_oun = nc.dram_tensor("dbg_oun", [128, NCH * N], BF16, kind="ExternalOutput").ap()
        dbg_onr = nc.dram_tensor("dbg_onr", [128, NCH * N], BF16, kind="ExternalOutput").ap()
        dbg_wbf = nc.dram_tensor("dbg_wbf", [128, D], BF16, kind="ExternalOutput").ap()
        dbg_q0 = nc.dram_tensor("dbg_q0", [128, N], BF16, kind="ExternalOutput").ap()

    import contextlib
    ctx = contextlib.ExitStack()
    with ctx:
        wproj_bf = ctx.enter_context(nc.sbuf_tensor("wproj_bf", [128, 8 * D], BF16))
        ws_bf = ctx.enter_context(nc.sbuf_tensor("ws_bf", [128, 8 * D], BF16))
        ident_bf = ctx.enter_context(nc.sbuf_tensor("ident_bf", [128, 128], BF16))
        sstat_s = ctx.enter_context(nc.sbuf_tensor("sstat_s", [128, 32], BF16))
        fstat_b = ctx.enter_context(nc.sbuf_tensor("fstat_b", [8, 512], BF16))
        dsel_s = ctx.enter_context(nc.sbuf_tensor("dsel_s", [16, 1024], BF16))
        onesb_s = ctx.enter_context(nc.sbuf_tensor("onesb_s", [1, 128], BF16))
        qspat_s = ctx.enter_context(nc.sbuf_tensor("qspat_s", [128, 1], F32))
        kspat_s = ctx.enter_context(nc.sbuf_tensor("kspat_s", [128, 1], F32))
        bprow = ctx.enter_context(nc.sbuf_tensor("bprow", [1, D], BF16))
        fbrow = ctx.enter_context(nc.sbuf_tensor("fbrow", [1, D], BF16))
        qsr = ctx.enter_context(nc.sbuf_tensor("qsr", [1, HD], F32))
        ksr = ctx.enter_context(nc.sbuf_tensor("ksr", [1, HD], F32))
        negb = ctx.enter_context(nc.sbuf_tensor("negb", [128, 1], F32))
        small = ctx.enter_context(nc.sbuf_tensor("small", [128, 16], F32))
        smallb = ctx.enter_context(nc.sbuf_tensor("smallb", [1, 16], BF16))
        qh = ctx.enter_context(nc.sbuf_tensor("qh", [128, NCH * N], BF16))
        kh = ctx.enter_context(nc.sbuf_tensor("kh", [128, NCH * N], BF16))
        vt = ctx.enter_context(nc.sbuf_tensor("vt", [128, NKT * 1040], BF16))
        xnf = ctx.enter_context(nc.sbuf_tensor("xnf", [128, NCH * N], BF16))
        dn = ctx.enter_context(nc.sbuf_tensor("dn", [16, N], BF16))
        xio = ctx.enter_context(nc.sbuf_tensor("xio", [128, 2 * D], F32))
        wbf = ctx.enter_context(nc.sbuf_tensor("wbf", [128, 2 * D], BF16))
        arena = ctx.enter_context(nc.sbuf_tensor("arena", [128, ARENA], U8))
        capm = ctx.enter_context(nc.sbuf_tensor("capm", [128, 1], F32))
        ps_a = ctx.enter_context(nc.psum_tensor("ps_a", [128, 2048], F32))
        ps_b = ctx.enter_context(nc.psum_tensor("ps_b", [128, 2048], F32))
        _SY = ctx.enter_context(nc.semaphore("SY"))
        _GP = ctx.enter_context(nc.semaphore("GP"))
        _PE = ctx.enter_context(nc.semaphore("PE"))
        _AC = ctx.enter_context(nc.semaphore("AC"))
        _DV = ctx.enter_context(nc.semaphore("DV"))
        _ST = ctx.enter_context(nc.semaphore("ST"))
        _BAR = ctx.enter_context(nc.semaphore("BAR"))
        block = ctx.enter_context(nc.Block())
        SY, GP, PE, AC, DV, STm, BAR = (Sem(_SY), Sem(_GP), Sem(_PE),
                                        Sem(_AC), Sem(_DV), Sem(_ST), Sem(_BAR))
        em = Emitter()
        nc.const_aps.register(F32, EPS_LN, capm[:])
        psa_bf = ps_a[:].bitcast(BF16)
        psb_bf = ps_b[:].bitcast(BF16)

        def av(off, nbytes, dt, parts=128):
            a = arena.ap()[:, off:off + nbytes].bitcast(dt)
            if parts != 128:
                a = a[0:parts, :]
            return a

        xntm = av(0, 4096, BF16)
        sqscrA = av(4096, 4096, F32)
        q2b = av(8192, 2048, BF16)
        fqsb = av(10240, 4096, F32)
        vfm = av(8192, 4096, BF16)
        fsb = av(14336, 2048, F32, 8)
        sqf = av(16384, 2048, F32, 8)
        fse = av(18432, 2048, F32, 8)
        fhi = av(20480, 1024, BF16, 8)
        flo = av(21504, 1024, BF16, 8)
        ptb = av(0, 8192, BF16)
        dscr = av(8192, 4096, BF16, 128)
        rdnf = av(0, 8192, F32, 16)
        rdnb = av(12288, 4096, BF16, 16)
        xntmE = av(0, 4096, BF16)
        xn2 = av(4096, 4096, BF16)
        gclip = av(8192, 4096, F32)
        gact = av(12288, 4096, F32)
        sqscrE = av(16384, 4096, F32)
        wstage = av(0, 4096, F32)
        wmask = av(4096, 1024, U8)
        scb = av(8192, 4096, F32)
        nscb = av(12288, 4096, F32)

        em.op("vector", "memset", inc=DV, ap=capm[:], constant=EPS_LN)
        for dst, src in ((ident_bf[:], ident_d), (sstat_s[:], sstat_d),
                         (fstat_b[:], fstat_d), (dsel_s[:], dsel_d),
                         (onesb_s[:], onesb_d), (qspat_s[:], qspat_d),
                         (kspat_s[:], kspat_d), (qsr[:], qs_d),
                         (ksr[:], ks_d), (scb, fscale_d)):
            em.op("sync", "dma_start", inc=SY, inck=16, out=dst, in_=src)
        ld_consts = SY.n
        for j in range(NCH):
            em.op("gpsimd", "dma_start", inc=GP, inck=16,
                  out=wproj_bf[:, j * D:(j + 1) * D],
                  in_=wproj_d[j * 128:(j + 1) * 128, :])
        em.op("gpsimd", "dma_start", inc=GP, inck=16,
              out=bprow[:], in_=bproj_d.rearrange("(a b) -> a b", a=1))
        em.op("gpsimd", "dma_start", inc=GP, inck=16,
              out=fbrow[:], in_=fbias_d.rearrange("(a b) -> a b", a=1))
        gp_wfix = GP.n

        dv_sel_prev = 0
        for j in range(NCH):
            em.op("sync", "dma_start", inc=SY, inck=16,
                  out=wstage, in_=wt_d[j * 128:(j + 1) * 128, :],
                  waits=[(DV, dv_sel_prev)])
            w = [(SY, SY.n), (SY, ld_consts)]
            em.op("vector", "tensor_single_scalar", waits=w, inc=DV,
                  out=wmask, in_=wstage, scalar=0.0, op=ALU.is_ge)
            if j == 0:
                em.op("vector", "tensor_scalar_mul", out=nscb,
                      in0=scb, scalar1=-1.0)
            em.op("vector", "select", inc=DV,
                  out=ws_bf[:, j * D:(j + 1) * D], mask=wmask,
                  on_true=scb, on_false=nscb)
            dv_sel_prev = DV.n
        dv_ws = DV.n
        em.op("vector", "reduce_max", waits=[(SY, ld_consts)],
              out=small[0:1, 0:1], in_=qsr[:], axis=AX.X,
              apply_absolute_value=True)
        em.op("vector", "reduce_max", inc=DV, out=small[0:1, 1:2], in_=ksr[:],
              axis=AX.X, apply_absolute_value=True)
        em.op("vector", "tensor_mul", waits=[(DV, DV.n)], inc=DV,
              out=small[0:1, 2:3],
              in0=small[0:1, 0:1], in1=small[0:1, 1:2])
        em.op("vector", "tensor_scalar_mul", waits=[(DV, DV.n)], inc=DV,
              out=small[0:1, 3:4],
              in0=small[0:1, 2:3], scalar1=-8.0)
        em.op("vector", "tensor_copy", waits=[(DV, DV.n)], inc=DV,
              out=smallb[0:1, 0:1], in_=small[0:1, 3:4])
        em.op("tensor", "matmul", waits=[(DV, DV.n)], inc=PE,
              out=ps_b[0:128, 0:1], lhsT=onesb_s[:],
              rhs=smallb[0:1, 0:1], start=True, stop=True)
        em.op("vector", "tensor_copy", waits=[(PE, PE.n)], inc=DV,
              out=negb[:], in_=ps_b[0:128, 0:1])
        dv_negb = DV.n

        st_total = [0]

        def emit_ln(t, s, sy_val, xt, out_tile, dv_var_prev, sqdump, extraw=()):
            def dve(*a, waits=(), **kw):
                em.op("vector", *a, waits=list(waits) + [(DV, DV.n)], inc=DV, **kw)
            w0 = list(extraw) + [(SY, sy_val)]
            em.op("vector", "reduce_sum", waits=w0, inc=DV, out=small[:, 0:1],
                  in_=xt, axis=AX.X)
            wa = list(w0)
            if dv_var_prev:
                wa.append((DV, dv_var_prev))
            em.op("scalar", "activation", waits=wa, inc=AC,
                  out=sqdump, in_=xt, func=AF.Square, accum_out=small[:, 1:2])
            ac_sq = AC.n
            dve("tensor_scalar_mul", waits=[(AC, ac_sq)],
                out=small[:, 2:3], in0=small[:, 0:1], scalar1=1.0 / D)
            dve("tensor_mul", out=small[:, 3:4],
                in0=small[:, 2:3], in1=small[:, 2:3])
            dve("scalar_tensor_tensor", out=small[:, 4:5], in0=small[:, 1:2],
                scalar=1.0 / D, in1=small[:, 3:4], op0=ALU.mult,
                op1=ALU.subtract)
            dve("tensor_scalar_add", out=small[:, 4:5], in0=small[:, 4:5],
                scalar1=EPS_LN)
            dv_var = DV.n
            em.op("scalar", "activation", waits=[(DV, dv_var)], inc=AC,
                  out=small[:, 5:6], in_=small[:, 4:5], func=AF.Sqrt,
                  bias=0.0, scale=1.0)
            dve("reciprocal", waits=[(AC, AC.n)],
                out=small[:, 6:7], in_=small[:, 5:6])
            dve("tensor_mul", out=small[:, 8:9],
                in0=small[:, 6:7], in1=small[:, 6:7])
            dve("tensor_mul", out=small[:, 8:9],
                in0=small[:, 8:9], in1=small[:, 4:5])
            dve("tensor_scalar", out=small[:, 8:9], in0=small[:, 8:9],
                scalar1=-0.5, scalar2=1.5, op0=ALU.mult, op1=ALU.add)
            dve("tensor_mul", out=small[:, 6:7],
                in0=small[:, 6:7], in1=small[:, 8:9])
            dve("scalar_tensor_tensor", out=small[:, 7:8], in0=small[:, 2:3],
                scalar=-1.0, in1=small[:, 6:7], op0=ALU.mult, op1=ALU.mult)
            dve("tensor_scalar", out=out_tile, in0=xt,
                scalar1=small[:, 6:7], scalar2=small[:, 7:8],
                op0=ALU.mult, op1=ALU.add)
            return dv_var, DV.n

        for s in range(SEQS):
            barw = [(BAR, 5 * s)]

            dv_apply = {}
            dv_trc = {}
            dv_varA = 0
            for t in range(NT):
                p = t % 2
                w = list(barw)
                if t >= 2:
                    w.append((DV, dv_apply[t - 2]))
                em.op("sync", "dma_start", waits=w, inc=SY, inck=16,
                      out=xio[:, p * D:(p + 1) * D],
                      in_=x_d[(s * NT + t) * 128:(s * NT + t + 1) * 128, :])
                xt = xio[:, p * D:(p + 1) * D]
                dv_varA, dv_apply[t] = emit_ln(
                    t, s, SY.n, xt, xntm[:, p * D:(p + 1) * D], dv_varA,
                    sqscrA, extraw=barw)
                w = [(DV, dv_apply[t])]
                if t >= 1:
                    w.append((DV, dv_trc[t - 1]))
                for j in range(NCH):
                    em.op("tensor", "transpose", waits=w if j == 0 else (),
                          inc=(PE if j == NCH - 1 else None),
                          out=psa_bf[:, j * 128:(j + 1) * 128],
                          in_=xntm[:, p * D + j * 128:p * D + (j + 1) * 128],
                          identity=ident_bf[:])
                em.op("vector", "tensor_copy", waits=[(PE, PE.n)], inc=DV,
                      out=xnf.ap().rearrange("p (c n) -> p c n", c=NCH)[
                          :, :, t * 128:(t + 1) * 128],
                      in_=psa_bf[:, 0:D].rearrange("p (c n) -> p c n", c=NCH))
                dv_trc[t] = DV.n
            dv_xnf = DV.n

            gp_w = {}
            pe_last_wbf = {}
            dv_stt = {}
            pe_qk = {}
            pe_ssm = {}
            ac_sqr = {}
            ac_sqrt = {}
            dv_f = {}
            pe_fm = {}
            dv_fev = {}
            pe_vm = {}
            dv_vev = {}
            dv_vtc = {}

            def load_wchunk(m):
                w = list(barw)
                if m >= 2:
                    w.append((PE, pe_last_wbf[m - 2]))
                em.op("gpsimd", "dma_start", waits=w, inc=GP, inck=16,
                      out=wbf[:, (m % 2) * D:(m % 2 + 1) * D].rearrange(
                          "p (j c) -> p j c", j=NCH),
                      in_=wqkv_d[:, m * 128:(m + 1) * 128].rearrange(
                          "(j p) c -> p j c", j=NCH))
                gp_w[m] = GP.n

            load_wchunk(0)
            load_wchunk(1)

            for m in range(16):
                isq = m < 8
                dst = qh if isq else kh
                ch = m % 8
                pat = qspat_s if isq else kspat_s
                for qtr in range(4):
                    w = [(GP, gp_w[m]), (DV, dv_xnf)]
                    if m >= 1:
                        w.append((DV, dv_stt[(m - 1, qtr)]))
                    for j in range(NCH):
                        em.op("tensor", "matmul", waits=w if j == 0 else (),
                              inc=(PE if j == NCH - 1 else None),
                              out=ps_a[:, qtr * 512:(qtr + 1) * 512],
                              lhsT=wbf[:, (m % 2) * D + j * 128:
                                       (m % 2) * D + (j + 1) * 128],
                              rhs=xnf[:, j * N + qtr * 512:
                                      j * N + (qtr + 1) * 512],
                              start=(j == 0), stop=(j == NCH - 1))
                    pe_qk[(m, qtr)] = PE.n
                    w = [(PE, pe_qk[(m, qtr)])]
                    if (m, qtr) != (0, 0):
                        pm, pq = (m, qtr - 2) if qtr >= 2 else (m - 1, qtr + 2)
                        if (pm, pq) in pe_ssm:
                            w.append((PE, pe_ssm[(pm, pq)]))
                    em.op("scalar", "activation", waits=w, inc=AC,
                          out=q2b[:, (qtr % 2) * 512:(qtr % 2 + 1) * 512],
                          in_=ps_a[:, qtr * 512:(qtr + 1) * 512],
                          func=AF.Square)
                    ac_sqr[(m, qtr)] = AC.n
                    w = [(AC, ac_sqr[(m, qtr)])]
                    if qtr == 0 and m >= 1:
                        w.append((AC, ac_sqrt[m - 1]))
                    em.op("tensor", "matmul", waits=w, inc=PE,
                          out=ps_b[0:8, 0:512],
                          lhsT=sstat_s[:, qtr * 8:(qtr + 1) * 8],
                          rhs=q2b[:, (qtr % 2) * 512:(qtr % 2 + 1) * 512],
                          start=(qtr == 0), stop=(qtr == 3))
                    pe_ssm[(m, qtr)] = PE.n
                if DEBUG and s == 0 and m == 0:
                    em.op("gpsimd", "dma_start", waits=[(GP, gp_w[0])], inc=GP,
                          inck=16, out=dbg_wbf, in_=wbf[:, 0:D])
                    em.op("vector", "tensor_copy", waits=[(PE, pe_qk[(0, 3)])],
                          inc=DV, out=xntm, in_=ps_a[:])
                    em.op("gpsimd", "dma_start", waits=[(DV, DV.n)], inc=GP,
                          inck=16, out=dbg_q0, in_=xntm)
                pe_last_wbf[m] = pe_qk[(m, 3)]
                if m + 2 < 24:
                    load_wchunk(m + 2)
                w = [(PE, pe_ssm[(m, 3)])]
                if m >= 1:
                    w.append((PE, pe_fm[(m - 1, 3)]))
                em.op("scalar", "activation", waits=w, inc=AC,
                      out=sqf, in_=ps_b[0:8, 0:512], func=AF.Sqrt,
                      bias=0.0, scale=1.0 / 64.0)
                em.op("scalar", "activation", inc=AC,
                      out=fse, in_=ps_b[0:8, 0:512], func=AF.Copy,
                      bias=0.0, scale=1.0 / 64.0)
                ac_sqrt[m] = AC.n
                w = [(AC, ac_sqrt[m])]
                if m >= 1:
                    w.append((PE, pe_fm[(m - 1, 3)]))
                def dvb(*a, waits=(), **kw):
                    em.op("vector", *a, waits=list(waits) + [(DV, DV.n)],
                          inc=DV, **kw)
                dvb("reciprocal", waits=w, out=fsb, in_=sqf)
                dvb("tensor_mul", out=fsb, in0=fsb, in1=fse)
                dvb("tensor_add", out=fsb, in0=fsb, in1=sqf)
                dvb("tensor_scalar", out=fsb, in0=fsb, scalar1=0.5,
                    scalar2=EPS_RMS, op0=ALU.mult, op1=ALU.add)
                dvb("reciprocal", out=sqf, in_=fsb)
                dvb("tensor_copy", out=fhi, in_=sqf)
                dvb("tensor_sub", out=fse, in0=sqf, in1=fhi)
                dvb("tensor_copy", out=flo, in_=fse)
                dv_f[m] = DV.n
                for qtr in range(4):
                    fb = 512 + (qtr % 2) * 512
                    w = [(DV, dv_f[m])]
                    if qtr >= 2:
                        w.append((DV, dv_fev[(m, qtr - 2)]))
                    elif m >= 1:
                        w.append((DV, dv_fev[(m - 1, qtr + 2)]))
                    em.op("tensor", "matmul", waits=w,
                          out=ps_b[:, fb:fb + 512],
                          lhsT=fstat_b[:, qtr * 128:(qtr + 1) * 128],
                          rhs=fhi, start=True, stop=False)
                    em.op("tensor", "matmul", inc=PE,
                          out=ps_b[:, fb:fb + 512],
                          lhsT=fstat_b[:, qtr * 128:(qtr + 1) * 128],
                          rhs=flo, start=False, stop=True)
                    pe_fm[(m, qtr)] = PE.n
                    em.op("vector", "tensor_copy", waits=[(PE, pe_fm[(m, qtr)])],
                          inc=DV,
                          out=fqsb[:, (qtr % 2) * 512:(qtr % 2 + 1) * 512],
                          in_=ps_b[:, fb:fb + 512])
                    dv_fev[(m, qtr)] = DV.n
                    em.op("vector", "scalar_tensor_tensor", inc=DV,
                          waits=[(DV, dv_fev[(m, qtr)])],
                          out=dst[:, ch * N + qtr * 512:ch * N + (qtr + 1) * 512],
                          in0=ps_a[:, qtr * 512:(qtr + 1) * 512], scalar=pat[:],
                          in1=fqsb[:, (qtr % 2) * 512:(qtr % 2 + 1) * 512],
                          op0=ALU.mult, op1=ALU.mult)
                    dv_stt[(m, qtr)] = DV.n

            if DEBUG and s == 0:
                em.op("gpsimd", "dma_start", waits=[(DV, dv_xnf)], inc=GP, inck=16,
                      out=dbg_xnf, in_=xnf[:])
                em.op("gpsimd", "dma_start", waits=[(DV, dv_stt[(15, 3)])],
                      inc=GP, inck=16, out=dbg_qh, in_=qh[:])
                em.op("gpsimd", "dma_start", waits=[(DV, dv_stt[(15, 3)])],
                      inc=GP, inck=16, out=dbg_kh, in_=kh[:])
            if s == 0:
                em.op("vector", "memset", inc=DV, ap=vt[:], constant=1.0)
            dv_vinit = DV.n
            for vm in range(8):
                m = 16 + vm
                for qtr in range(4):
                    w = [(GP, gp_w[m]), (DV, dv_stt[(15, 3)]), (DV, dv_vinit)]
                    if vm >= 1:
                        w.append((DV, dv_vev[vm - 1]))
                    for j in range(NCH):
                        em.op("tensor", "matmul", waits=w if j == 0 else (),
                              inc=(PE if j == NCH - 1 else None),
                              out=ps_a[:, qtr * 512:(qtr + 1) * 512],
                              lhsT=wbf[:, (m % 2) * D + j * 128:
                                       (m % 2) * D + (j + 1) * 128],
                              rhs=xnf[:, j * N + qtr * 512:
                                      j * N + (qtr + 1) * 512],
                              start=(j == 0), stop=(j == NCH - 1))
                pe_vm[m] = PE.n
                pe_last_wbf[m] = pe_vm[m]
                if m + 2 < 24:
                    load_wchunk(m + 2)
                w = [(PE, pe_vm[m])]
                if vm == 0:
                    w.append((DV, dv_stt[(15, 3)]))
                em.op("vector", "tensor_copy", waits=w, inc=DV,
                      out=vfm, in_=ps_a[:])
                dv_vev[vm] = DV.n
                for grp in range(4):
                    w = [(DV, dv_vev[vm])]
                    if (vm, grp) != (0, 0):
                        pv, pg = (vm, grp - 1) if grp >= 1 else (vm - 1, 3)
                        w.append((DV, dv_vtc[(pv, pg)]))
                    for kk in range(4):
                        kt = grp * 4 + kk
                        em.op("tensor", "transpose", waits=w if kk == 0 else (),
                              inc=(PE if kk == 3 else None),
                              out=psb_bf[:, 3072 + kk * 128:3072 + (kk + 1) * 128],
                              in_=vfm[:, kt * 128:(kt + 1) * 128],
                              identity=ident_bf[:])
                    em.op("vector", "tensor_copy", waits=[(PE, PE.n)], inc=DV,
                          out=vt.ap().rearrange("p (kt c) -> p kt c", kt=NKT)[
                              :, grp * 4:(grp + 1) * 4,
                              130 * vm:130 * vm + 130].rearrange(
                              "p kt (h2 c) -> p kt h2 c", c=65)[:, :, :, 0:64],
                          in_=psb_bf[:, 3072:3584].rearrange(
                              "p (kt c) -> p kt c", kt=4).rearrange(
                              "p kt (h2 c) -> p kt h2 c", c=64))
                    dv_vtc[(vm, grp)] = DV.n
            dv_vdone = DV.n

            if DEBUG and s == 0:
                em.op("gpsimd", "dma_start", waits=[(DV, dv_vdone)], inc=GP,
                      inck=16, out=dbg_vt, in_=vt[:])
            order = [jj // 2 + 8 * (jj % 2) for jj in range(16)]
            pe_s = {}
            ac_e = {}
            pe_pv = {}
            dv_oe = {}
            gp_dn = {}
            units = 2 * NKT
            npair = NKT

            for hi, h in enumerate(order):
                ch = h % 8
                odd = h >= 8
                rowo = 64 if odd else 0
                drow = 0 if odd else 64

                def s_unit(u, hi=hi, ch=ch, rowo=rowo):
                    qha, kt = u // NKT, u % NKT
                    w = []
                    if hi == 0 and u == 0:
                        w += [(DV, dv_vdone), (DV, dv_negb)]
                    if u >= 2:
                        w.append((AC, ac_e.get((hi, u - 2), 0)))
                    elif hi > 0:
                        w.append((AC, ac_e[(hi - 1, units - 2 + u)]))
                    for qtr in range(2):
                        em.op("tensor", "matmul", waits=w if qtr == 0 else (),
                              inc=(PE if qtr == 1 else None),
                              out=ps_a[:, (u % 2) * 1024 + qtr * 512:
                                       (u % 2) * 1024 + (qtr + 1) * 512],
                              lhsT=kh[rowo:rowo + 64,
                                      ch * N + kt * 128:ch * N + (kt + 1) * 128],
                              rhs=qh[rowo:rowo + 64,
                                     ch * N + qha * 1024 + qtr * 512:
                                     ch * N + qha * 1024 + (qtr + 1) * 512],
                              start=True, stop=True)
                    pe_s[(hi, u)] = PE.n

                def e_unit(u, hi=hi):
                    w = [(PE, pe_s[(hi, u)])]
                    if u >= 2:
                        w.append((PE, pe_pv[(hi, u - 2)]))
                    elif hi > 0:
                        w.append((PE, pe_pv[(hi - 1, units - 2 + u)]))
                    em.op("scalar", "activation", waits=w, inc=AC,
                          out=ptb[:, (u % 2) * 1024:(u % 2) * 1024 + 1024],
                          in_=ps_a[:, (u % 2) * 1024:(u % 2) * 1024 + 1024],
                          func=AF.Exp, bias=negb[:], scale=0.125)
                    ac_e[(hi, u)] = AC.n

                def pv_unit(u, hi=hi, h=h, odd=odd):
                    qha, kt = u // NKT, u % NKT
                    w = [(AC, ac_e[(hi, u)])]
                    if kt == 0 and hi > 0:
                        w.append((DV, dv_oe[(hi - 1, qha)]))
                    vcol = kt * 1040 + 130 * (h % 8) + (65 if odd else 0)
                    for qtr in range(2):
                        ob = qha * 1024 + qtr * 512
                        pcol = (u % 2) * 1024 + qtr * 512
                        if not odd:
                            em.op("tensor", "matmul", waits=w if qtr == 0 else (),
                                  inc=(PE if qtr == 1 else None),
                                  out=ps_b[0:65, ob:ob + 512],
                                  lhsT=vt[:, vcol:vcol + 65],
                                  rhs=ptb[:, pcol:pcol + 512],
                                  start=(kt == 0), stop=(kt == NKT - 1))
                        else:
                            em.op("tensor", "matmul", waits=w if qtr == 0 else (),
                                  out=ps_b[64:128, ob:ob + 512],
                                  lhsT=vt[:, vcol:vcol + 64],
                                  rhs=ptb[:, pcol:pcol + 512],
                                  start=(kt == 0), stop=(kt == NKT - 1),
                                  tile_position=(0, 64))
                            em.op("tensor", "matmul",
                                  inc=(PE if qtr == 1 else None),
                                  out=ps_b[0:1, ob:ob + 512],
                                  lhsT=vt[:, vcol + 64:vcol + 65],
                                  rhs=ptb[:, pcol:pcol + 512],
                                  start=(kt == 0), stop=(kt == NKT - 1),
                                  tile_position=(0, 0))
                    pe_pv[(hi, u)] = PE.n

                def evac_region(qha, hi=hi, h=h, odd=odd, rowo=rowo, drow=drow):
                    w = [(PE, pe_pv[(hi, qha * NKT + NKT - 1)])]
                    em.op("vector", "tensor_copy", waits=w, inc=DV,
                          out=qh[rowo:rowo + 64,
                                 (h % 8) * N + qha * 1024:(h % 8) * N + (qha + 1) * 1024],
                          in_=ps_b[rowo:rowo + 64, qha * 1024:(qha + 1) * 1024])
                    w2 = []
                    if hi >= 2:
                        w2 = [(GP, gp_dn[hi - 2])]
                    em.op("vector", "tensor_copy", waits=w2, inc=DV,
                          out=dscr[drow:drow + 1, qha * 1024:(qha + 1) * 1024],
                          in_=ps_b[drow:drow + 1, qha * 1024:(qha + 1) * 1024])
                    dv_oe[(hi, qha)] = DV.n

                s_unit(0)
                s_unit(1)
                for u in range(units):
                    e_unit(u)
                    pv_unit(u)
                    if u + 2 < units:
                        s_unit(u + 2)
                    if u == NKT - 1:
                        evac_region(0)
                evac_region(1)
                em.op("gpsimd", "dma_start",
                      waits=[(DV, dv_oe[(hi, 1)])], inc=GP, inck=16,
                      out=dn[h:h + 1, :], in_=dscr[drow:drow + 1, :])
                gp_dn[hi] = GP.n

            if DEBUG and s == 0:
                em.op("gpsimd", "dma_start", waits=[(GP, gp_dn[15]), (DV, dv_oe[15])],
                      inc=GP, inck=16, out=dbg_dn, in_=dn[:])
                em.op("gpsimd", "dma_start", waits=[(DV, dv_oe[15])], inc=GP,
                      inck=16, out=dbg_oun, in_=qh[:])
            em.op("vector", "reciprocal", waits=[(GP, gp_dn[15])], inc=DV,
                  out=rdnf, in_=dn[:])
            em.op("vector", "tensor_copy", waits=[(DV, DV.n)], inc=DV,
                  out=rdnb, in_=rdnf)
            dv_rdn = DV.n
            pe_fd = {}
            dv_on = {}
            for j in range(NCH):
                for ha in range(2):
                    w = [(DV, dv_rdn), (AC, ac_e[(15, units - 1)])]
                    if (j, ha) != (0, 0):
                        pj, ph = (j, 0) if ha == 1 else (j - 1, 1)
                        w.append((DV, dv_on[(pj, ph)]))
                    for qtr in range(2):
                        em.op("tensor", "matmul", waits=w if qtr == 0 else (),
                              inc=(PE if qtr == 1 else None),
                              out=ps_a[:, ha * 1024 + qtr * 512:
                                       ha * 1024 + (qtr + 1) * 512],
                              lhsT=dsel_s[:, j * 128:(j + 1) * 128],
                              rhs=rdnb[:, ha * 1024 + qtr * 512:
                                       ha * 1024 + (qtr + 1) * 512],
                              start=True, stop=True)
                    pe_fd[(j, ha)] = PE.n
                    em.op("vector", "tensor_tensor",
                          waits=[(PE, pe_fd[(j, ha)])], inc=DV,
                          out=kh[:, j * N + ha * 1024:j * N + (ha + 1) * 1024],
                          in0=qh[:, j * N + ha * 1024:j * N + (ha + 1) * 1024],
                          in1=ps_a[:, ha * 1024:(ha + 1) * 1024], op=ALU.mult)
                    dv_on[(j, ha)] = DV.n
            dv_onorm = DV.n

            if DEBUG and s == 0:
                em.op("gpsimd", "dma_start", waits=[(DV, dv_onorm)], inc=GP,
                      inck=16, out=dbg_onr, in_=kh[:])
            dv_tr2 = {}
            dv_clip = {}
            dv_epi = {}
            st_done = {}
            dv_varE = 0
            for t in range(NT):
                p = t % 2
                w = [(DV, dv_onorm)]
                if t >= 2:
                    w.append((DV, dv_epi[2 * (t - 2) + 1]))
                em.op("sync", "dma_start", waits=w, inc=SY, inck=16,
                      out=xio[:, p * D:(p + 1) * D],
                      in_=x_d[(s * NT + t) * 128:(s * NT + t + 1) * 128, :])
                xt = xio[:, p * D:(p + 1) * D]
                dv_varE, dv_ap2 = emit_ln(
                    t, s, SY.n, xt, xntmE[:, p * D:(p + 1) * D], dv_varE,
                    sqscrE, extraw=[(DV, dv_onorm)])
                w = [(DV, dv_ap2)]
                if t >= 1:
                    w.append((DV, dv_tr2[t - 1]))
                for j in range(NCH):
                    em.op("tensor", "transpose", waits=w if j == 0 else (),
                          inc=(PE if j == NCH - 1 else None),
                          out=psa_bf[:, j * 128:(j + 1) * 128],
                          in_=xntmE[:, p * D + j * 128:p * D + (j + 1) * 128],
                          identity=ident_bf[:])
                pe_t2 = PE.n
                em.op("vector", "tensor_copy", waits=[(PE, pe_t2)], inc=DV,
                      out=xn2[:, p * D:(p + 1) * D], in_=psa_bf[:, 0:D])
                dv_tr2[t] = DV.n
                for fh in range(2):
                    ci = 2 * t + fh
                    sb = (ci % 4) * 512
                    gb = (ci % 3 + 1) * 512
                    w = [(DV, dv_tr2[t]), (GP, gp_wfix)]
                    if ci >= 4:
                        w.append((DV, dv_epi[ci - 4]))
                    for j in range(NCH):
                        em.op("tensor", "matmul", waits=w if j == 0 else (),
                              out=ps_b[:, sb:sb + 512],
                              lhsT=kh[:, j * N + t * 128:j * N + (t + 1) * 128],
                              rhs=wproj_bf[:, j * D + fh * 512:
                                           j * D + fh * 512 + 512],
                              start=(j == 0), stop=False)
                    em.op("tensor", "matmul",
                          out=ps_b[:, sb:sb + 512], lhsT=onesb_s[:],
                          rhs=bprow[:, fh * 512:fh * 512 + 512],
                          start=False, stop=True)
                    w = []
                    if ci >= 3:
                        w.append((DV, dv_clip[ci - 3]))
                    for j in range(NCH):
                        em.op("tensor", "matmul", waits=w if j == 0 else (),
                              out=ps_a[:, gb:gb + 512],
                              lhsT=xn2[:, p * D + j * 128:p * D + (j + 1) * 128],
                              rhs=ws_bf[:, j * D + fh * 512:
                                        j * D + fh * 512 + 512],
                              start=(j == 0), stop=False)
                    em.op("tensor", "matmul", inc=PE,
                          out=ps_a[:, gb:gb + 512], lhsT=onesb_s[:],
                          rhs=fbrow[:, fh * 512:fh * 512 + 512],
                          start=False, stop=True)
                    pe_sg = PE.n
                    em.op("vector", "tensor_scalar", waits=[(PE, pe_sg)],
                          inc=DV,
                          out=gclip[:, fh * 512:(fh + 1) * 512],
                          in0=ps_a[:, gb:gb + 512], scalar1=-100.0,
                          scalar2=100.0, op0=ALU.max, op1=ALU.min)
                    dv_clip[ci] = DV.n
                    w = [(DV, dv_clip[ci])]
                    if ci >= 2:
                        w.append((STm, st_done[ci - 2]))
                    em.op("scalar", "activation", waits=w, inc=AC,
                          out=gact[:, fh * 512:(fh + 1) * 512],
                          in_=gclip[:, fh * 512:(fh + 1) * 512], func=AF.Silu)
                    em.op("vector", "scalar_tensor_tensor", waits=[(AC, AC.n)],
                          inc=DV,
                          out=gclip[:, fh * 512:(fh + 1) * 512],
                          in0=ps_b[:, sb:sb + 512], scalar=1.0,
                          in1=gact[:, fh * 512:(fh + 1) * 512],
                          op0=ALU.mult, op1=ALU.mult)
                    em.op("vector", "tensor_tensor", waits=[(DV, DV.n)], inc=DV,
                          out=gact[:, fh * 512:(fh + 1) * 512],
                          in0=gclip[:, fh * 512:(fh + 1) * 512],
                          in1=xt[:, fh * 512:(fh + 1) * 512], op=ALU.add)
                    dv_epi[ci] = DV.n
                    em.op("sync", "dma_start", waits=[(DV, dv_epi[ci])],
                          inc=STm, inck=16,
                          out=y_d[(s * NT + t) * 128:(s * NT + t + 1) * 128,
                                  fh * 512:(fh + 1) * 512],
                          in_=gact[:, fh * 512:(fh + 1) * 512])
                    st_done[ci] = STm.n
                    st_total[0] = STm.n

            em.op("sync", "nop", waits=[(STm, st_total[0])], inc=BAR)
            em.op("gpsimd", "nop", waits=[(GP, GP.n)], inc=BAR)
            em.op("tensor", "nop", inc=BAR)
            em.op("vector", "nop", inc=BAR)
            em.op("scalar", "nop", inc=BAR)

        em.op("sync", "nop", waits=[(STm, st_total[0])])

        @block.sync
        def _(sync):
            for fn in em.events["sync"]:
                fn(sync)

        @block.gpsimd
        def _(gpsimd):
            for fn in em.events["gpsimd"]:
                fn(gpsimd)

        @block.tensor
        def _(tensor):
            for fn in em.events["tensor"]:
                fn(tensor)

        @block.vector
        def _(vector):
            for fn in em.events["vector"]:
                fn(vector)

        @block.scalar
        def _(scalar):
            for fn in em.events["scalar"]:
                fn(scalar)

    return nc


_NC_CACHE = None


def _get_nc():
    global _NC_CACHE
    if _NC_CACHE is None:
        _NC_CACHE = build_nc()
    return _NC_CACHE


def kernel(L0_attn_x, L0_attn_Wqkv, L0_attn_Wproj, L0_attn_bproj,
           L0_attn_q_scale, L0_attn_k_scale, L0_ffn_weights, L0_ffn_scale,
           L0_ffn_bias):
    x = np.asarray(L0_attn_x, dtype=np.float32)
    wqkv = np.asarray(L0_attn_Wqkv, dtype=np.float32)
    wproj = np.asarray(L0_attn_Wproj, dtype=np.float32)
    bproj = np.asarray(L0_attn_bproj, dtype=np.float32)
    qs = np.asarray(L0_attn_q_scale, dtype=np.float32)
    ks = np.asarray(L0_attn_k_scale, dtype=np.float32)
    wf = np.asarray(L0_ffn_weights, dtype=np.float32)
    fsc = np.asarray(L0_ffn_scale, dtype=np.float32)
    fbi = np.asarray(L0_ffn_bias, dtype=np.float32)

    wq = wqkv[:, 0:D].reshape(D, H, HD)
    wk = wqkv[:, D:2 * D].reshape(D, H, HD)
    wv = wqkv[:, 2 * D:3 * D].reshape(D, H, HD)
    cols = []
    for part in (wq, wk, wv):
        for j in range(8):
            cols.append(part[:, j, :])
            cols.append(part[:, j + 8, :])
    wqkv_p = np.ascontiguousarray(np.concatenate(cols, axis=1))
    wp = wproj.reshape(H, HD, D)
    rows = []
    for j in range(8):
        rows.append(wp[j])
        rows.append(wp[j + 8])
    wproj_p = np.ascontiguousarray(np.concatenate(rows, axis=0))
    wt = np.ascontiguousarray(wf.T)
    qspat = np.ascontiguousarray(np.tile(qs, 2)[:, None])
    kspat = np.ascontiguousarray(np.tile(ks, 2)[:, None])
    fscale_b = np.ascontiguousarray(np.broadcast_to(fsc[None, :], (128, D)))

    bf = ml_dtypes.bfloat16
    ident = np.eye(128, dtype=np.float32).astype(bf)
    sstat = np.zeros((128, 32), np.float32)
    for i in range(4):
        sstat[0:64, 8 * i + 2 * i] = 1.0
        sstat[64:128, 8 * i + 2 * i + 1] = 1.0
    sstat = sstat.astype(bf)
    fstat = np.zeros((8, 512), np.float32)
    for i in range(4):
        fstat[2 * i, 128 * i:128 * i + 64] = 1.0
        fstat[2 * i + 1, 128 * i + 64:128 * i + 128] = 1.0
    fstat = fstat.astype(bf)
    dsel = np.zeros((16, 1024), np.float32)
    for j in range(8):
        dsel[j, j * 128:j * 128 + 64] = 1.0
        dsel[j + 8, j * 128 + 64:j * 128 + 128] = 1.0
    dsel = dsel.astype(bf)
    onesb = np.ones((1, 128), np.float32).astype(bf)

    nc = _get_nc()
    xr = x.reshape(32, N, D)
    in_maps = []
    for c in range(NCORES):
        in_maps.append({
            "x": np.ascontiguousarray(
                xr[SEQS * c:SEQS * (c + 1)].reshape(SEQS * N, D)),
            "wqkv": wqkv_p, "wproj": wproj_p, "wt": wt,
            "bproj": bproj, "fbias": fbi,
            "qs": qs[None, :], "ks": ks[None, :],
            "qspat": qspat, "kspat": kspat, "fscale": fscale_b,
            "ident": ident, "sstat": sstat, "fstat": fstat,
            "dsel": dsel, "onesb": onesb,
        })
    res = run_bass_kernel_spmd(nc, in_maps, core_ids=list(range(NCORES)))
    out = np.concatenate([r["y"].reshape(SEQS, N, D) for r in res.results],
                         axis=0)
    return out


# version 07
# speedup vs baseline: 1.4353x; 1.1142x over previous
import numpy as np
import ml_dtypes
import concourse.bass as bass
import concourse.mybir as mybir
from concourse.bass_utils import run_bass_kernel_spmd

F32 = mybir.dt.float32
BF16 = mybir.dt.bfloat16
F32R = mybir.dt.float32r
U8 = mybir.dt.uint8
AF = mybir.ActivationFunctionType
ALU = mybir.AluOpType
AX = mybir.AxisListType

NCORES = 8
SEQS = 4
N = 2048
D = 1024
H = 16
HD = 64
NT = N // 128
NKT = N // 128
NCH = D // 128
EPS_LN = 1e-5
EPS_RMS = 1e-8
ENGS = ("sync", "gpsimd", "tensor", "vector", "scalar")
ARENA = 22528
DEBUG = False


class Sem:
    def __init__(self, handle):
        self.h = handle
        self.n = 0


class Emitter:
    def __init__(self):
        self.events = {e: [] for e in ENGS}
        self.lastw = {e: {} for e in ENGS}

    def op(self, eng, method, *args, waits=(), inc=None, inck=1, **kw):
        real = []
        for sem, val in waits:
            if val <= 0:
                continue
            if self.lastw[eng].get(id(sem), -1) >= val:
                continue
            self.lastw[eng][id(sem)] = val
            real.append((sem.h, val))
        tgt = None
        if inc is not None:
            inc.n += inck
            tgt = (inc.h, inck)

        def fn(e, _m=method, _a=args, _k=kw, _w=real, _t=tgt):
            for h, v in _w:
                e.wait_ge(h, v)
            ins = getattr(e, _m)(*_a, **_k)
            if _t is not None:
                ins.then_inc(_t[0], _t[1])
        self.events[eng].append(fn)


def build_nc():
    nc = bass.Bass("TRN2", target_bir_lowering=False)

    x_d = nc.dram_tensor("x", [SEQS * N, D], F32, kind="ExternalInput").ap()
    wqkv_d = nc.dram_tensor("wqkv", [D, 3 * D], F32, kind="ExternalInput").ap()
    wproj_d = nc.dram_tensor("wproj", [D, D], F32, kind="ExternalInput").ap()
    wt_d = nc.dram_tensor("wt", [D, D], F32, kind="ExternalInput").ap()
    bproj_d = nc.dram_tensor("bproj", [D], F32, kind="ExternalInput").ap()
    fbias_d = nc.dram_tensor("fbias", [D], F32, kind="ExternalInput").ap()
    qs_d = nc.dram_tensor("qs", [1, HD], F32, kind="ExternalInput").ap()
    ks_d = nc.dram_tensor("ks", [1, HD], F32, kind="ExternalInput").ap()
    qspat_d = nc.dram_tensor("qspat", [128, 1], F32, kind="ExternalInput").ap()
    kspat_d = nc.dram_tensor("kspat", [128, 1], F32, kind="ExternalInput").ap()
    fscale_d = nc.dram_tensor("fscale", [128, D], F32, kind="ExternalInput").ap()
    ident_d = nc.dram_tensor("ident", [128, 128], BF16, kind="ExternalInput").ap()
    sstat_d = nc.dram_tensor("sstat", [128, 32], BF16, kind="ExternalInput").ap()
    fstat_d = nc.dram_tensor("fstat", [8, 512], BF16, kind="ExternalInput").ap()
    dsel_d = nc.dram_tensor("dsel", [16, 1024], BF16, kind="ExternalInput").ap()
    onesb_d = nc.dram_tensor("onesb", [1, 128], BF16, kind="ExternalInput").ap()
    y_d = nc.dram_tensor("y", [SEQS * N, D], F32, kind="ExternalOutput").ap()
    if DEBUG:
        dbg_xnf = nc.dram_tensor("dbg_xnf", [128, NCH * N], BF16, kind="ExternalOutput").ap()
        dbg_qh = nc.dram_tensor("dbg_qh", [128, NCH * N], BF16, kind="ExternalOutput").ap()
        dbg_kh = nc.dram_tensor("dbg_kh", [128, NCH * N], BF16, kind="ExternalOutput").ap()
        dbg_vt = nc.dram_tensor("dbg_vt", [128, NKT * 1040], BF16, kind="ExternalOutput").ap()
        dbg_dn = nc.dram_tensor("dbg_dn", [16, N], BF16, kind="ExternalOutput").ap()
        dbg_oun = nc.dram_tensor("dbg_oun", [128, NCH * N], BF16, kind="ExternalOutput").ap()
        dbg_onr = nc.dram_tensor("dbg_onr", [128, NCH * N], BF16, kind="ExternalOutput").ap()
        dbg_wbf = nc.dram_tensor("dbg_wbf", [128, D], BF16, kind="ExternalOutput").ap()
        dbg_q0 = nc.dram_tensor("dbg_q0", [128, N], BF16, kind="ExternalOutput").ap()

    import contextlib
    ctx = contextlib.ExitStack()
    with ctx:
        wproj_bf = ctx.enter_context(nc.sbuf_tensor("wproj_bf", [128, 8 * D], BF16))
        ws_bf = ctx.enter_context(nc.sbuf_tensor("ws_bf", [128, 8 * D], BF16))
        ident_bf = ctx.enter_context(nc.sbuf_tensor("ident_bf", [128, 128], BF16))
        sstat_s = ctx.enter_context(nc.sbuf_tensor("sstat_s", [128, 32], BF16))
        fstat_b = ctx.enter_context(nc.sbuf_tensor("fstat_b", [8, 512], BF16))
        dsel_s = ctx.enter_context(nc.sbuf_tensor("dsel_s", [16, 1024], BF16))
        onesb_s = ctx.enter_context(nc.sbuf_tensor("onesb_s", [1, 128], BF16))
        qspat_s = ctx.enter_context(nc.sbuf_tensor("qspat_s", [128, 1], F32))
        kspat_s = ctx.enter_context(nc.sbuf_tensor("kspat_s", [128, 1], F32))
        bprow = ctx.enter_context(nc.sbuf_tensor("bprow", [1, D], BF16))
        fbrow = ctx.enter_context(nc.sbuf_tensor("fbrow", [1, D], BF16))
        qsr = ctx.enter_context(nc.sbuf_tensor("qsr", [1, HD], F32))
        ksr = ctx.enter_context(nc.sbuf_tensor("ksr", [1, HD], F32))
        negb = ctx.enter_context(nc.sbuf_tensor("negb", [128, 1], F32))
        small = ctx.enter_context(nc.sbuf_tensor("small", [128, 16], F32))
        smallb = ctx.enter_context(nc.sbuf_tensor("smallb", [1, 16], BF16))
        qh = ctx.enter_context(nc.sbuf_tensor("qh", [128, NCH * N], BF16))
        kh = ctx.enter_context(nc.sbuf_tensor("kh", [128, NCH * N], BF16))
        vt = ctx.enter_context(nc.sbuf_tensor("vt", [128, NKT * 1040], BF16))
        xnf = ctx.enter_context(nc.sbuf_tensor("xnf", [128, NCH * N], BF16))
        dn = ctx.enter_context(nc.sbuf_tensor("dn", [16, N], BF16))
        xio = ctx.enter_context(nc.sbuf_tensor("xio", [128, 2 * D], F32))
        wbf = ctx.enter_context(nc.sbuf_tensor("wbf", [128, 2 * D], BF16))
        arena = ctx.enter_context(nc.sbuf_tensor("arena", [128, ARENA], U8))
        capm = ctx.enter_context(nc.sbuf_tensor("capm", [128, 1], F32))
        ps_a = ctx.enter_context(nc.psum_tensor("ps_a", [128, 2048], F32))
        ps_b = ctx.enter_context(nc.psum_tensor("ps_b", [128, 2048], F32))
        _SY = ctx.enter_context(nc.semaphore("SY"))
        _GP = ctx.enter_context(nc.semaphore("GP"))
        _PE = ctx.enter_context(nc.semaphore("PE"))
        _AC = ctx.enter_context(nc.semaphore("AC"))
        _DV = ctx.enter_context(nc.semaphore("DV"))
        _ST = ctx.enter_context(nc.semaphore("ST"))
        _BAR = ctx.enter_context(nc.semaphore("BAR"))
        block = ctx.enter_context(nc.Block())
        SY, GP, PE, AC, DV, STm, BAR = (Sem(_SY), Sem(_GP), Sem(_PE),
                                        Sem(_AC), Sem(_DV), Sem(_ST), Sem(_BAR))
        em = Emitter()
        nc.const_aps.register(F32, EPS_LN, capm[:])
        psa_bf = ps_a[:].bitcast(BF16)
        psb_bf = ps_b[:].bitcast(BF16)

        def av(off, nbytes, dt, parts=128):
            a = arena.ap()[:, off:off + nbytes].bitcast(dt)
            if parts != 128:
                a = a[0:parts, :]
            return a

        xntm = av(0, 4096, BF16)
        sqscrA = av(4096, 4096, F32)
        q2b = av(8192, 2048, BF16)
        fqsb = av(10240, 4096, F32)
        vfm = av(8192, 4096, BF16)
        fsb = av(14336, 2048, F32, 8)
        sqf = av(16384, 2048, F32, 8)
        fse = av(18432, 2048, F32, 8)
        fhi = av(20480, 1024, BF16, 8)
        flo = av(21504, 1024, BF16, 8)
        ptb = av(0, 8192, BF16)
        dscr = av(8192, 4096, BF16, 128)
        rdnf = av(0, 8192, F32, 16)
        rdnb = av(12288, 4096, BF16, 16)
        xntmE = av(0, 4096, BF16)
        xn2 = av(4096, 4096, BF16)
        gclip = av(8192, 4096, F32)
        gact = av(12288, 4096, F32)
        sqscrE = av(16384, 4096, F32)
        wstage = av(0, 4096, F32)
        wmask = av(4096, 1024, U8)
        scb = av(8192, 4096, F32)
        nscb = av(12288, 4096, F32)

        em.op("vector", "memset", inc=DV, ap=capm[:], constant=EPS_LN)
        for dst, src in ((ident_bf[:], ident_d), (sstat_s[:], sstat_d),
                         (fstat_b[:], fstat_d), (dsel_s[:], dsel_d),
                         (onesb_s[:], onesb_d), (qspat_s[:], qspat_d),
                         (kspat_s[:], kspat_d), (qsr[:], qs_d),
                         (ksr[:], ks_d), (scb, fscale_d)):
            em.op("sync", "dma_start", inc=SY, inck=16, out=dst, in_=src)
        ld_consts = SY.n
        for j in range(NCH):
            em.op("gpsimd", "dma_start", inc=GP, inck=16,
                  out=wproj_bf[:, j * D:(j + 1) * D],
                  in_=wproj_d[j * 128:(j + 1) * 128, :])
        em.op("gpsimd", "dma_start", inc=GP, inck=16,
              out=bprow[:], in_=bproj_d.rearrange("(a b) -> a b", a=1))
        em.op("gpsimd", "dma_start", inc=GP, inck=16,
              out=fbrow[:], in_=fbias_d.rearrange("(a b) -> a b", a=1))
        gp_wfix = GP.n

        dv_sel_prev = 0
        for j in range(NCH):
            em.op("sync", "dma_start", inc=SY, inck=16,
                  out=wstage, in_=wt_d[j * 128:(j + 1) * 128, :],
                  waits=[(DV, dv_sel_prev)])
            w = [(SY, SY.n), (SY, ld_consts)]
            em.op("vector", "tensor_single_scalar", waits=w, inc=DV,
                  out=wmask, in_=wstage, scalar=0.0, op=ALU.is_ge)
            if j == 0:
                em.op("vector", "tensor_scalar_mul", out=nscb,
                      in0=scb, scalar1=-1.0)
            em.op("vector", "select", inc=DV,
                  out=ws_bf[:, j * D:(j + 1) * D], mask=wmask,
                  on_true=scb, on_false=nscb)
            dv_sel_prev = DV.n
        dv_ws = DV.n
        em.op("vector", "reduce_max", waits=[(SY, ld_consts)],
              out=small[0:1, 0:1], in_=qsr[:], axis=AX.X,
              apply_absolute_value=True)
        em.op("vector", "reduce_max", inc=DV, out=small[0:1, 1:2], in_=ksr[:],
              axis=AX.X, apply_absolute_value=True)
        em.op("vector", "tensor_mul", waits=[(DV, DV.n)], inc=DV,
              out=small[0:1, 2:3],
              in0=small[0:1, 0:1], in1=small[0:1, 1:2])
        em.op("vector", "tensor_scalar_mul", waits=[(DV, DV.n)], inc=DV,
              out=small[0:1, 3:4],
              in0=small[0:1, 2:3], scalar1=-8.0)
        em.op("vector", "tensor_copy", waits=[(DV, DV.n)], inc=DV,
              out=smallb[0:1, 0:1], in_=small[0:1, 3:4])
        em.op("tensor", "matmul", waits=[(DV, DV.n)], inc=PE,
              out=ps_b[0:128, 0:1], lhsT=onesb_s[:],
              rhs=smallb[0:1, 0:1], start=True, stop=True)
        em.op("vector", "tensor_copy", waits=[(PE, PE.n)], inc=DV,
              out=negb[:], in_=ps_b[0:128, 0:1])
        dv_negb = DV.n

        st_total = [0]

        def emit_ln(t, s, sy_val, xt, out_tile, dv_var_prev, sqdump, extraw=()):
            def dve(*a, waits=(), **kw):
                em.op("vector", *a, waits=list(waits) + [(DV, DV.n)], inc=DV, **kw)
            w0 = list(extraw) + [(SY, sy_val)]
            em.op("vector", "reduce_sum", waits=w0, inc=DV, out=small[:, 0:1],
                  in_=xt, axis=AX.X)
            wa = list(w0)
            if dv_var_prev:
                wa.append((DV, dv_var_prev))
            em.op("scalar", "activation", waits=wa, inc=AC,
                  out=sqdump, in_=xt, func=AF.Square, accum_out=small[:, 1:2])
            ac_sq = AC.n
            dve("tensor_scalar_mul", waits=[(AC, ac_sq)],
                out=small[:, 2:3], in0=small[:, 0:1], scalar1=1.0 / D)
            dve("tensor_mul", out=small[:, 3:4],
                in0=small[:, 2:3], in1=small[:, 2:3])
            dve("scalar_tensor_tensor", out=small[:, 4:5], in0=small[:, 1:2],
                scalar=1.0 / D, in1=small[:, 3:4], op0=ALU.mult,
                op1=ALU.subtract)
            dve("tensor_scalar_add", out=small[:, 4:5], in0=small[:, 4:5],
                scalar1=EPS_LN)
            dv_var = DV.n
            em.op("scalar", "activation", waits=[(DV, dv_var)], inc=AC,
                  out=small[:, 5:6], in_=small[:, 4:5], func=AF.Sqrt,
                  bias=0.0, scale=1.0)
            dve("reciprocal", waits=[(AC, AC.n)],
                out=small[:, 6:7], in_=small[:, 5:6])
            dve("tensor_mul", out=small[:, 8:9],
                in0=small[:, 6:7], in1=small[:, 6:7])
            dve("tensor_mul", out=small[:, 8:9],
                in0=small[:, 8:9], in1=small[:, 4:5])
            dve("tensor_scalar", out=small[:, 8:9], in0=small[:, 8:9],
                scalar1=-0.5, scalar2=1.5, op0=ALU.mult, op1=ALU.add)
            dve("tensor_mul", out=small[:, 6:7],
                in0=small[:, 6:7], in1=small[:, 8:9])
            dve("scalar_tensor_tensor", out=small[:, 7:8], in0=small[:, 2:3],
                scalar=-1.0, in1=small[:, 6:7], op0=ALU.mult, op1=ALU.mult)
            dve("tensor_scalar", out=out_tile, in0=xt,
                scalar1=small[:, 6:7], scalar2=small[:, 7:8],
                op0=ALU.mult, op1=ALU.add)
            return dv_var, DV.n

        for s in range(SEQS):
            barw = [(BAR, 5 * s)]

            dv_apply = {}
            dv_trc = {}
            dv_varA = 0
            for tp in range(NT // 2):
                t0, t1 = 2 * tp, 2 * tp + 1
                sy_ld = {}
                for t in (t0, t1):
                    p = t % 2
                    w = list(barw)
                    if t >= 2:
                        w.append((DV, dv_apply[t - 2]))
                    em.op("sync", "dma_start", waits=w, inc=SY, inck=16,
                          out=xio[:, p * D:(p + 1) * D],
                          in_=x_d[(s * NT + t) * 128:(s * NT + t + 1) * 128, :])
                    sy_ld[t] = SY.n

                def dve(*a, waits=(), **kw):
                    em.op("vector", *a, waits=list(waits) + [(DV, DV.n)],
                          inc=DV, **kw)
                for i, t in enumerate((t0, t1)):
                    xt = xio[:, (t % 2) * D:(t % 2 + 1) * D]
                    em.op("vector", "reduce_sum", waits=barw + [(SY, sy_ld[t])],
                          inc=DV, out=small[:, 0 + i:1 + i], in_=xt, axis=AX.X)
                    wa = barw + [(SY, sy_ld[t])]
                    if dv_varA:
                        wa.append((DV, dv_varA))
                    em.op("scalar", "activation", waits=wa, inc=AC,
                          out=sqscrA, in_=xt, func=AF.Square,
                          accum_out=small[:, 2 + i:3 + i])
                ac_sq = AC.n
                dve("tensor_scalar_mul", waits=[(AC, ac_sq)],
                    out=small[:, 4:6], in0=small[:, 0:2], scalar1=1.0 / D)
                dve("tensor_mul", out=small[:, 6:8],
                    in0=small[:, 4:6], in1=small[:, 4:6])
                dve("scalar_tensor_tensor", out=small[:, 8:10],
                    in0=small[:, 2:4], scalar=1.0 / D, in1=small[:, 6:8],
                    op0=ALU.mult, op1=ALU.subtract)
                dve("tensor_scalar_add", out=small[:, 8:10],
                    in0=small[:, 8:10], scalar1=EPS_LN)
                dv_varA = DV.n
                em.op("scalar", "activation", waits=[(DV, dv_varA)], inc=AC,
                      out=small[:, 10:12], in_=small[:, 8:10], func=AF.Sqrt,
                      bias=0.0, scale=1.0)
                dve("reciprocal", waits=[(AC, AC.n)],
                    out=small[:, 12:14], in_=small[:, 10:12])
                dve("tensor_mul", out=small[:, 10:12],
                    in0=small[:, 12:14], in1=small[:, 12:14])
                dve("tensor_mul", out=small[:, 10:12],
                    in0=small[:, 10:12], in1=small[:, 8:10])
                dve("tensor_scalar", out=small[:, 10:12], in0=small[:, 10:12],
                    scalar1=-0.5, scalar2=1.5, op0=ALU.mult, op1=ALU.add)
                dve("tensor_mul", out=small[:, 12:14],
                    in0=small[:, 12:14], in1=small[:, 10:12])
                dve("scalar_tensor_tensor", out=small[:, 14:16],
                    in0=small[:, 4:6], scalar=-1.0, in1=small[:, 12:14],
                    op0=ALU.mult, op1=ALU.mult)
                for i, t in enumerate((t0, t1)):
                    p = t % 2
                    xt = xio[:, p * D:(p + 1) * D]
                    dve("tensor_scalar", out=xntm[:, p * D:(p + 1) * D],
                        in0=xt, scalar1=small[:, 12 + i:13 + i],
                        scalar2=small[:, 14 + i:15 + i],
                        op0=ALU.mult, op1=ALU.add)
                    dv_apply[t] = DV.n
                    w = [(DV, dv_apply[t])]
                    if t >= 1:
                        w.append((DV, dv_trc[t - 1]))
                    for j in range(NCH):
                        em.op("tensor", "transpose", waits=w if j == 0 else (),
                              inc=(PE if j == NCH - 1 else None),
                              out=psa_bf[:, j * 128:(j + 1) * 128],
                              in_=xntm[:, p * D + j * 128:p * D + (j + 1) * 128],
                              identity=ident_bf[:])
                    em.op("vector", "tensor_copy", waits=[(PE, PE.n)], inc=DV,
                          out=xnf.ap().rearrange("p (c n) -> p c n", c=NCH)[
                              :, :, t * 128:(t + 1) * 128],
                          in_=psa_bf[:, 0:D].rearrange("p (c n) -> p c n", c=NCH))
                    dv_trc[t] = DV.n
            dv_xnf = DV.n
            gp_w = {}
            pe_last_wbf = {}
            dv_raw = {}
            dv_stt = {}
            pe_qk = {}
            pe_ssm = {}
            ac_sqr = {}
            ac_sqrt = {}
            dv_f = {}
            pe_fm = {}
            dv_fev = {}
            pe_vm = {}
            dv_vev = {}
            dv_vtc = {}

            def load_wchunk(m):
                w = list(barw)
                if m >= 2:
                    w.append((PE, pe_last_wbf[m - 2]))
                em.op("gpsimd", "dma_start", waits=w, inc=GP, inck=16,
                      out=wbf[:, (m % 2) * D:(m % 2 + 1) * D].rearrange(
                          "p (j c) -> p j c", j=NCH),
                      in_=wqkv_d[:, m * 128:(m + 1) * 128].rearrange(
                          "(j p) c -> p j c", j=NCH))
                gp_w[m] = GP.n

            load_wchunk(0)
            load_wchunk(1)

            for m in range(16):
                isq = m < 8
                dst = qh if isq else kh
                ch = m % 8
                pat = qspat_s if isq else kspat_s
                for qtr in range(4):
                    w = [(GP, gp_w[m]), (DV, dv_xnf)]
                    if m >= 1:
                        w.append((DV, dv_raw[m - 1]))
                    for j in range(NCH):
                        em.op("tensor", "matmul", waits=w if j == 0 else (),
                              inc=(PE if j == NCH - 1 else None),
                              out=ps_a[:, qtr * 512:(qtr + 1) * 512],
                              lhsT=wbf[:, (m % 2) * D + j * 128:
                                       (m % 2) * D + (j + 1) * 128],
                              rhs=xnf[:, j * N + qtr * 512:
                                      j * N + (qtr + 1) * 512],
                              start=(j == 0), stop=(j == NCH - 1))
                    pe_qk[(m, qtr)] = PE.n
                em.op("vector", "tensor_copy", waits=[(PE, pe_qk[(m, 3)])],
                      inc=DV, out=dst[:, ch * N:(ch + 1) * N], in_=ps_a[:])
                dv_raw[m] = DV.n
                pe_last_wbf[m] = pe_qk[(m, 3)]
                if m + 2 < 24:
                    load_wchunk(m + 2)
                for qtr in range(4):
                    w = [(DV, dv_raw[m])]
                    if (m, qtr) != (0, 0):
                        pm, pq = (m, qtr - 2) if qtr >= 2 else (m - 1, qtr + 2)
                        if (pm, pq) in pe_ssm:
                            w.append((PE, pe_ssm[(pm, pq)]))
                    em.op("scalar", "activation", waits=w, inc=AC,
                          out=q2b[:, (qtr % 2) * 512:(qtr % 2 + 1) * 512],
                          in_=dst[:, ch * N + qtr * 512:ch * N + (qtr + 1) * 512],
                          func=AF.Square)
                    ac_sqr[(m, qtr)] = AC.n
                    w = [(AC, ac_sqr[(m, qtr)])]
                    if qtr == 0 and m >= 1:
                        w.append((AC, ac_sqrt[m - 1]))
                    em.op("tensor", "matmul", waits=w, inc=PE,
                          out=ps_b[0:8, 0:512],
                          lhsT=sstat_s[:, qtr * 8:(qtr + 1) * 8],
                          rhs=q2b[:, (qtr % 2) * 512:(qtr % 2 + 1) * 512],
                          start=(qtr == 0), stop=(qtr == 3))
                    pe_ssm[(m, qtr)] = PE.n
                w = [(PE, pe_ssm[(m, 3)])]
                if m >= 1:
                    w.append((PE, pe_fm[(m - 1, 3)]))
                em.op("scalar", "activation", waits=w, inc=AC,
                      out=sqf, in_=ps_b[0:8, 0:512], func=AF.Sqrt,
                      bias=0.0, scale=1.0 / 64.0)
                em.op("scalar", "activation", inc=AC,
                      out=fse, in_=ps_b[0:8, 0:512], func=AF.Copy,
                      bias=0.0, scale=1.0 / 64.0)
                ac_sqrt[m] = AC.n
                w = [(AC, ac_sqrt[m])]
                if m >= 1:
                    w.append((PE, pe_fm[(m - 1, 3)]))
                def dvb(*a, waits=(), **kw):
                    em.op("vector", *a, waits=list(waits) + [(DV, DV.n)],
                          inc=DV, **kw)
                dvb("reciprocal", waits=w, out=fsb, in_=sqf)
                dvb("tensor_mul", out=fsb, in0=fsb, in1=fse)
                dvb("tensor_add", out=fsb, in0=fsb, in1=sqf)
                dvb("tensor_scalar", out=fsb, in0=fsb, scalar1=0.5,
                    scalar2=EPS_RMS, op0=ALU.mult, op1=ALU.add)
                dvb("reciprocal", out=sqf, in_=fsb)
                dvb("tensor_copy", out=fhi, in_=sqf)
                dvb("tensor_sub", out=fse, in0=sqf, in1=fhi)
                dvb("tensor_copy", out=flo, in_=fse)
                dv_f[m] = DV.n
                for qtr in range(4):
                    fb = 512 + (qtr % 2) * 512
                    w = [(DV, dv_f[m])]
                    if qtr >= 2:
                        w.append((DV, dv_fev[(m, qtr - 2)]))
                    elif m >= 1:
                        w.append((DV, dv_fev[(m - 1, qtr + 2)]))
                    em.op("tensor", "matmul", waits=w,
                          out=ps_b[:, fb:fb + 512],
                          lhsT=fstat_b[:, qtr * 128:(qtr + 1) * 128],
                          rhs=fhi, start=True, stop=False)
                    em.op("tensor", "matmul", inc=PE,
                          out=ps_b[:, fb:fb + 512],
                          lhsT=fstat_b[:, qtr * 128:(qtr + 1) * 128],
                          rhs=flo, start=False, stop=True)
                    pe_fm[(m, qtr)] = PE.n
                    em.op("vector", "tensor_copy", waits=[(PE, pe_fm[(m, qtr)])],
                          inc=DV,
                          out=fqsb[:, (qtr % 2) * 512:(qtr % 2 + 1) * 512],
                          in_=ps_b[:, fb:fb + 512])
                    dv_fev[(m, qtr)] = DV.n
                    em.op("vector", "scalar_tensor_tensor", inc=DV,
                          waits=[(DV, dv_fev[(m, qtr)])],
                          out=dst[:, ch * N + qtr * 512:ch * N + (qtr + 1) * 512],
                          in0=dst[:, ch * N + qtr * 512:ch * N + (qtr + 1) * 512],
                          scalar=pat[:],
                          in1=fqsb[:, (qtr % 2) * 512:(qtr % 2 + 1) * 512],
                          op0=ALU.mult, op1=ALU.mult)
                    dv_stt[(m, qtr)] = DV.n

            if s == 0:
                em.op("vector", "memset", inc=DV, ap=vt[:], constant=1.0)
            dv_vinit = DV.n
            for vm in range(8):
                m = 16 + vm
                for qtr in range(4):
                    w = [(GP, gp_w[m]), (DV, dv_raw[15]), (DV, dv_vinit)]
                    if vm >= 1:
                        w.append((DV, dv_vev[vm - 1]))
                    for j in range(NCH):
                        em.op("tensor", "matmul", waits=w if j == 0 else (),
                              inc=(PE if j == NCH - 1 else None),
                              out=ps_a[:, qtr * 512:(qtr + 1) * 512],
                              lhsT=wbf[:, (m % 2) * D + j * 128:
                                       (m % 2) * D + (j + 1) * 128],
                              rhs=xnf[:, j * N + qtr * 512:
                                      j * N + (qtr + 1) * 512],
                              start=(j == 0), stop=(j == NCH - 1))
                pe_vm[m] = PE.n
                pe_last_wbf[m] = pe_vm[m]
                if m + 2 < 24:
                    load_wchunk(m + 2)
                em.op("vector", "tensor_copy", waits=[(PE, pe_vm[m])], inc=DV,
                      out=vfm, in_=ps_a[:])
                dv_vev[vm] = DV.n
                for grp in range(4):
                    w = [(DV, dv_vev[vm])]
                    if (vm, grp) != (0, 0):
                        pv, pg = (vm, grp - 1) if grp >= 1 else (vm - 1, 3)
                        w.append((DV, dv_vtc[(pv, pg)]))
                    for kk in range(4):
                        kt = grp * 4 + kk
                        em.op("tensor", "transpose", waits=w if kk == 0 else (),
                              inc=(PE if kk == 3 else None),
                              out=psb_bf[:, 3072 + kk * 128:3072 + (kk + 1) * 128],
                              in_=vfm[:, kt * 128:(kt + 1) * 128],
                              identity=ident_bf[:])
                    em.op("vector", "tensor_copy", waits=[(PE, PE.n)], inc=DV,
                          out=vt.ap().rearrange("p (kt c) -> p kt c", kt=NKT)[
                              :, grp * 4:(grp + 1) * 4,
                              130 * vm:130 * vm + 130].rearrange(
                              "p kt (h2 c) -> p kt h2 c", c=65)[:, :, :, 0:64],
                          in_=psb_bf[:, 3072:3584].rearrange(
                              "p (kt c) -> p kt c", kt=4).rearrange(
                              "p kt (h2 c) -> p kt h2 c", c=64))
                    dv_vtc[(vm, grp)] = DV.n
            dv_vdone = DV.n

            if DEBUG and s == 0:
                em.op("gpsimd", "dma_start", waits=[(DV, dv_vdone)], inc=GP,
                      inck=16, out=dbg_vt, in_=vt[:])
            order = [jj // 2 + 8 * (jj % 2) for jj in range(16)]
            pe_s = {}
            ac_e = {}
            pe_pv = {}
            dv_oe = {}
            gp_dn = {}
            units = 2 * NKT
            npair = NKT

            for hi, h in enumerate(order):
                ch = h % 8
                odd = h >= 8
                rowo = 64 if odd else 0
                drow = 0 if odd else 64

                def s_unit(u, hi=hi, ch=ch, rowo=rowo):
                    qha, kt = u // NKT, u % NKT
                    w = []
                    if hi == 0 and u == 0:
                        w += [(DV, dv_vdone), (DV, dv_negb)]
                    if u >= 2:
                        w.append((AC, ac_e.get((hi, u - 2), 0)))
                    elif hi > 0:
                        w.append((AC, ac_e[(hi - 1, units - 2 + u)]))
                    for qtr in range(2):
                        em.op("tensor", "matmul", waits=w if qtr == 0 else (),
                              inc=(PE if qtr == 1 else None),
                              out=ps_a[:, (u % 2) * 1024 + qtr * 512:
                                       (u % 2) * 1024 + (qtr + 1) * 512],
                              lhsT=kh[rowo:rowo + 64,
                                      ch * N + kt * 128:ch * N + (kt + 1) * 128],
                              rhs=qh[rowo:rowo + 64,
                                     ch * N + qha * 1024 + qtr * 512:
                                     ch * N + qha * 1024 + (qtr + 1) * 512],
                              start=True, stop=True)
                    pe_s[(hi, u)] = PE.n

                def e_unit(u, hi=hi):
                    w = [(PE, pe_s[(hi, u)])]
                    if u >= 2:
                        w.append((PE, pe_pv[(hi, u - 2)]))
                    elif hi > 0:
                        w.append((PE, pe_pv[(hi - 1, units - 2 + u)]))
                    em.op("scalar", "activation", waits=w, inc=AC,
                          out=ptb[:, (u % 2) * 1024:(u % 2) * 1024 + 1024],
                          in_=ps_a[:, (u % 2) * 1024:(u % 2) * 1024 + 1024],
                          func=AF.Exp, bias=negb[:], scale=0.125)
                    ac_e[(hi, u)] = AC.n

                def pv_unit(u, hi=hi, h=h, odd=odd):
                    qha, kt = u // NKT, u % NKT
                    w = [(AC, ac_e[(hi, u)])]
                    if kt == 0 and hi > 0:
                        w.append((DV, dv_oe[(hi - 1, qha)]))
                    vcol = kt * 1040 + 130 * (h % 8) + (65 if odd else 0)
                    for qtr in range(2):
                        ob = qha * 1024 + qtr * 512
                        pcol = (u % 2) * 1024 + qtr * 512
                        if not odd:
                            em.op("tensor", "matmul", waits=w if qtr == 0 else (),
                                  inc=(PE if qtr == 1 else None),
                                  out=ps_b[0:65, ob:ob + 512],
                                  lhsT=vt[:, vcol:vcol + 65],
                                  rhs=ptb[:, pcol:pcol + 512],
                                  start=(kt == 0), stop=(kt == NKT - 1))
                        else:
                            em.op("tensor", "matmul", waits=w if qtr == 0 else (),
                                  out=ps_b[64:128, ob:ob + 512],
                                  lhsT=vt[:, vcol:vcol + 64],
                                  rhs=ptb[:, pcol:pcol + 512],
                                  start=(kt == 0), stop=(kt == NKT - 1),
                                  tile_position=(0, 64))
                            em.op("tensor", "matmul",
                                  inc=(PE if qtr == 1 else None),
                                  out=ps_b[0:1, ob:ob + 512],
                                  lhsT=vt[:, vcol + 64:vcol + 65],
                                  rhs=ptb[:, pcol:pcol + 512],
                                  start=(kt == 0), stop=(kt == NKT - 1),
                                  tile_position=(0, 0))
                    pe_pv[(hi, u)] = PE.n

                def evac_region(qha, hi=hi, h=h, odd=odd, rowo=rowo, drow=drow):
                    w = [(PE, pe_pv[(hi, qha * NKT + NKT - 1)])]
                    em.op("vector", "tensor_copy", waits=w, inc=DV,
                          out=qh[rowo:rowo + 64,
                                 (h % 8) * N + qha * 1024:(h % 8) * N + (qha + 1) * 1024],
                          in_=ps_b[rowo:rowo + 64, qha * 1024:(qha + 1) * 1024])
                    w2 = []
                    if hi >= 2:
                        w2 = [(GP, gp_dn[hi - 2])]
                    em.op("vector", "tensor_copy", waits=w2, inc=DV,
                          out=dscr[drow:drow + 1, qha * 1024:(qha + 1) * 1024],
                          in_=ps_b[drow:drow + 1, qha * 1024:(qha + 1) * 1024])
                    dv_oe[(hi, qha)] = DV.n

                s_unit(0)
                s_unit(1)
                for u in range(units):
                    e_unit(u)
                    pv_unit(u)
                    if u + 2 < units:
                        s_unit(u + 2)
                    if u == NKT - 1:
                        evac_region(0)
                evac_region(1)
                em.op("gpsimd", "dma_start",
                      waits=[(DV, dv_oe[(hi, 1)])], inc=GP, inck=16,
                      out=dn[h:h + 1, :], in_=dscr[drow:drow + 1, :])
                gp_dn[hi] = GP.n

            if DEBUG and s == 0:
                em.op("gpsimd", "dma_start", waits=[(GP, gp_dn[15]), (DV, dv_oe[15])],
                      inc=GP, inck=16, out=dbg_dn, in_=dn[:])
                em.op("gpsimd", "dma_start", waits=[(DV, dv_oe[15])], inc=GP,
                      inck=16, out=dbg_oun, in_=qh[:])
            em.op("vector", "reciprocal", waits=[(GP, gp_dn[15])], inc=DV,
                  out=rdnf, in_=dn[:])
            em.op("vector", "tensor_copy", waits=[(DV, DV.n)], inc=DV,
                  out=rdnb, in_=rdnf)
            dv_rdn = DV.n
            pe_fd = {}
            dv_on = {}
            for j in range(NCH):
                for ha in range(2):
                    w = [(DV, dv_rdn), (AC, ac_e[(15, units - 1)])]
                    if (j, ha) != (0, 0):
                        pj, ph = (j, 0) if ha == 1 else (j - 1, 1)
                        w.append((DV, dv_on[(pj, ph)]))
                    for qtr in range(2):
                        em.op("tensor", "matmul", waits=w if qtr == 0 else (),
                              inc=(PE if qtr == 1 else None),
                              out=ps_a[:, ha * 1024 + qtr * 512:
                                       ha * 1024 + (qtr + 1) * 512],
                              lhsT=dsel_s[:, j * 128:(j + 1) * 128],
                              rhs=rdnb[:, ha * 1024 + qtr * 512:
                                       ha * 1024 + (qtr + 1) * 512],
                              start=True, stop=True)
                    pe_fd[(j, ha)] = PE.n
                    em.op("vector", "tensor_tensor",
                          waits=[(PE, pe_fd[(j, ha)])], inc=DV,
                          out=kh[:, j * N + ha * 1024:j * N + (ha + 1) * 1024],
                          in0=qh[:, j * N + ha * 1024:j * N + (ha + 1) * 1024],
                          in1=ps_a[:, ha * 1024:(ha + 1) * 1024], op=ALU.mult)
                    dv_on[(j, ha)] = DV.n
            dv_onorm = DV.n

            if DEBUG and s == 0:
                em.op("gpsimd", "dma_start", waits=[(DV, dv_onorm)], inc=GP,
                      inck=16, out=dbg_onr, in_=kh[:])
            dv_clip = {}
            dv_epi = {}
            st_done = {}
            for t in range(NT):
                p = t % 2
                w = [(DV, dv_onorm)]
                if t >= 2:
                    w.append((DV, dv_epi[2 * (t - 2) + 1]))
                em.op("sync", "dma_start", waits=w, inc=SY, inck=16,
                      out=xio[:, p * D:(p + 1) * D],
                      in_=x_d[(s * NT + t) * 128:(s * NT + t + 1) * 128, :])
                sy_x2 = SY.n
                xt = xio[:, p * D:(p + 1) * D]
                for fh in range(2):
                    ci = 2 * t + fh
                    sb = (ci % 4) * 512
                    gb = (ci % 4) * 512
                    w = [(DV, dv_onorm), (GP, gp_wfix)]
                    if ci >= 4:
                        w.append((DV, dv_epi[ci - 4]))
                    for j in range(NCH):
                        em.op("tensor", "matmul", waits=w if j == 0 else (),
                              out=ps_b[:, sb:sb + 512],
                              lhsT=kh[:, j * N + t * 128:j * N + (t + 1) * 128],
                              rhs=wproj_bf[:, j * D + fh * 512:
                                           j * D + fh * 512 + 512],
                              start=(j == 0), stop=False)
                    em.op("tensor", "matmul",
                          out=ps_b[:, sb:sb + 512], lhsT=onesb_s[:],
                          rhs=bprow[:, fh * 512:fh * 512 + 512],
                          start=False, stop=True)
                    w = []
                    if ci >= 4:
                        w.append((DV, dv_clip[ci - 4]))
                    for j in range(NCH):
                        em.op("tensor", "matmul", waits=w if j == 0 else (),
                              out=ps_a[:, gb:gb + 512],
                              lhsT=xnf[:, j * N + t * 128:j * N + (t + 1) * 128],
                              rhs=ws_bf[:, j * D + fh * 512:
                                        j * D + fh * 512 + 512],
                              start=(j == 0), stop=False)
                    em.op("tensor", "matmul", inc=PE,
                          out=ps_a[:, gb:gb + 512], lhsT=onesb_s[:],
                          rhs=fbrow[:, fh * 512:fh * 512 + 512],
                          start=False, stop=True)
                    pe_sg = PE.n
                    em.op("vector", "tensor_scalar", waits=[(PE, pe_sg)],
                          inc=DV,
                          out=gclip[:, fh * 512:(fh + 1) * 512],
                          in0=ps_a[:, gb:gb + 512], scalar1=-100.0,
                          scalar2=100.0, op0=ALU.max, op1=ALU.min)
                    dv_clip[ci] = DV.n
                    w = [(DV, dv_clip[ci])]
                    if ci >= 2:
                        w.append((STm, st_done[ci - 2]))
                    em.op("scalar", "activation", waits=w, inc=AC,
                          out=gact[:, fh * 512:(fh + 1) * 512],
                          in_=gclip[:, fh * 512:(fh + 1) * 512], func=AF.Silu)
                    em.op("vector", "scalar_tensor_tensor", waits=[(AC, AC.n)],
                          inc=DV,
                          out=gclip[:, fh * 512:(fh + 1) * 512],
                          in0=ps_b[:, sb:sb + 512], scalar=1.0,
                          in1=gact[:, fh * 512:(fh + 1) * 512],
                          op0=ALU.mult, op1=ALU.mult)
                    em.op("vector", "tensor_tensor",
                          waits=[(DV, DV.n), (SY, sy_x2)], inc=DV,
                          out=gact[:, fh * 512:(fh + 1) * 512],
                          in0=gclip[:, fh * 512:(fh + 1) * 512],
                          in1=xt[:, fh * 512:(fh + 1) * 512], op=ALU.add)
                    dv_epi[ci] = DV.n
                    em.op("sync", "dma_start", waits=[(DV, dv_epi[ci])],
                          inc=STm, inck=16,
                          out=y_d[(s * NT + t) * 128:(s * NT + t + 1) * 128,
                                  fh * 512:(fh + 1) * 512],
                          in_=gact[:, fh * 512:(fh + 1) * 512])
                    st_done[ci] = STm.n
                    st_total[0] = STm.n

            em.op("sync", "nop", waits=[(STm, st_total[0])], inc=BAR)
            em.op("gpsimd", "nop", waits=[(GP, GP.n)], inc=BAR)
            em.op("tensor", "nop", inc=BAR)
            em.op("vector", "nop", inc=BAR)
            em.op("scalar", "nop", inc=BAR)

        em.op("sync", "nop", waits=[(STm, st_total[0])])

        @block.sync
        def _(sync):
            for fn in em.events["sync"]:
                fn(sync)

        @block.gpsimd
        def _(gpsimd):
            for fn in em.events["gpsimd"]:
                fn(gpsimd)

        @block.tensor
        def _(tensor):
            for fn in em.events["tensor"]:
                fn(tensor)

        @block.vector
        def _(vector):
            for fn in em.events["vector"]:
                fn(vector)

        @block.scalar
        def _(scalar):
            for fn in em.events["scalar"]:
                fn(scalar)

    return nc


_NC_CACHE = None


def _get_nc():
    global _NC_CACHE
    if _NC_CACHE is None:
        _NC_CACHE = build_nc()
    return _NC_CACHE


def kernel(L0_attn_x, L0_attn_Wqkv, L0_attn_Wproj, L0_attn_bproj,
           L0_attn_q_scale, L0_attn_k_scale, L0_ffn_weights, L0_ffn_scale,
           L0_ffn_bias):
    x = np.asarray(L0_attn_x, dtype=np.float32)
    wqkv = np.asarray(L0_attn_Wqkv, dtype=np.float32)
    wproj = np.asarray(L0_attn_Wproj, dtype=np.float32)
    bproj = np.asarray(L0_attn_bproj, dtype=np.float32)
    qs = np.asarray(L0_attn_q_scale, dtype=np.float32)
    ks = np.asarray(L0_attn_k_scale, dtype=np.float32)
    wf = np.asarray(L0_ffn_weights, dtype=np.float32)
    fsc = np.asarray(L0_ffn_scale, dtype=np.float32)
    fbi = np.asarray(L0_ffn_bias, dtype=np.float32)

    wq = wqkv[:, 0:D].reshape(D, H, HD)
    wk = wqkv[:, D:2 * D].reshape(D, H, HD)
    wv = wqkv[:, 2 * D:3 * D].reshape(D, H, HD)
    cols = []
    for part in (wq, wk, wv):
        for j in range(8):
            cols.append(part[:, j, :])
            cols.append(part[:, j + 8, :])
    wqkv_p = np.ascontiguousarray(np.concatenate(cols, axis=1))
    wp = wproj.reshape(H, HD, D)
    rows = []
    for j in range(8):
        rows.append(wp[j])
        rows.append(wp[j + 8])
    wproj_p = np.ascontiguousarray(np.concatenate(rows, axis=0))
    wt = np.ascontiguousarray(wf.T)
    qspat = np.ascontiguousarray(np.tile(qs, 2)[:, None])
    kspat = np.ascontiguousarray(np.tile(ks, 2)[:, None])
    fscale_b = np.ascontiguousarray(np.broadcast_to(fsc[None, :], (128, D)))

    bf = ml_dtypes.bfloat16
    ident = np.eye(128, dtype=np.float32).astype(bf)
    sstat = np.zeros((128, 32), np.float32)
    for i in range(4):
        sstat[0:64, 8 * i + 2 * i] = 1.0
        sstat[64:128, 8 * i + 2 * i + 1] = 1.0
    sstat = sstat.astype(bf)
    fstat = np.zeros((8, 512), np.float32)
    for i in range(4):
        fstat[2 * i, 128 * i:128 * i + 64] = 1.0
        fstat[2 * i + 1, 128 * i + 64:128 * i + 128] = 1.0
    fstat = fstat.astype(bf)
    dsel = np.zeros((16, 1024), np.float32)
    for j in range(8):
        dsel[j, j * 128:j * 128 + 64] = 1.0
        dsel[j + 8, j * 128 + 64:j * 128 + 128] = 1.0
    dsel = dsel.astype(bf)
    onesb = np.ones((1, 128), np.float32).astype(bf)

    nc = _get_nc()
    xr = x.reshape(32, N, D)
    in_maps = []
    for c in range(NCORES):
        in_maps.append({
            "x": np.ascontiguousarray(
                xr[SEQS * c:SEQS * (c + 1)].reshape(SEQS * N, D)),
            "wqkv": wqkv_p, "wproj": wproj_p, "wt": wt,
            "bproj": bproj, "fbias": fbi,
            "qs": qs[None, :], "ks": ks[None, :],
            "qspat": qspat, "kspat": kspat, "fscale": fscale_b,
            "ident": ident, "sstat": sstat, "fstat": fstat,
            "dsel": dsel, "onesb": onesb,
        })
    res = run_bass_kernel_spmd(nc, in_maps, core_ids=list(range(NCORES)))
    out = np.concatenate([r["y"].reshape(SEQS, N, D) for r in res.results],
                         axis=0)
    return out


# version 08
# speedup vs baseline: 1.5751x; 1.0974x over previous
import numpy as np
import ml_dtypes
import concourse.bass as bass
import concourse.mybir as mybir
from concourse.bass_utils import run_bass_kernel_spmd

F32 = mybir.dt.float32
BF16 = mybir.dt.bfloat16
F32R = mybir.dt.float32r
U8 = mybir.dt.uint8
AF = mybir.ActivationFunctionType
ALU = mybir.AluOpType
AX = mybir.AxisListType

NCORES = 8
SEQS = 4
N = 2048
D = 1024
H = 16
HD = 64
NT = N // 128
NKT = N // 128
NCH = D // 128
EPS_LN = 1e-5
EPS_RMS = 1e-8
ENGS = ("sync", "gpsimd", "tensor", "vector", "scalar")
ARENA = 22528
DEBUG = False


class Sem:
    def __init__(self, handle):
        self.h = handle
        self.n = 0


class Emitter:
    def __init__(self):
        self.events = {e: [] for e in ENGS}
        self.lastw = {e: {} for e in ENGS}

    def op(self, eng, method, *args, waits=(), inc=None, inck=1, **kw):
        real = []
        for sem, val in waits:
            if val <= 0:
                continue
            if self.lastw[eng].get(id(sem), -1) >= val:
                continue
            self.lastw[eng][id(sem)] = val
            real.append((sem.h, val))
        tgt = None
        if inc is not None:
            inc.n += inck
            tgt = (inc.h, inck)

        def fn(e, _m=method, _a=args, _k=kw, _w=real, _t=tgt):
            for h, v in _w:
                e.wait_ge(h, v)
            ins = getattr(e, _m)(*_a, **_k)
            if _t is not None:
                ins.then_inc(_t[0], _t[1])
        self.events[eng].append(fn)


def build_nc():
    nc = bass.Bass("TRN2", target_bir_lowering=False)

    x_d = nc.dram_tensor("x", [SEQS * N, D], F32, kind="ExternalInput").ap()
    wqkv_d = nc.dram_tensor("wqkv", [D, 3 * D], F32, kind="ExternalInput").ap()
    wproj_d = nc.dram_tensor("wproj", [D, D], F32, kind="ExternalInput").ap()
    wt_d = nc.dram_tensor("wt", [D, D], F32, kind="ExternalInput").ap()
    bproj_d = nc.dram_tensor("bproj", [D], F32, kind="ExternalInput").ap()
    fbias_d = nc.dram_tensor("fbias", [D], F32, kind="ExternalInput").ap()
    qs_d = nc.dram_tensor("qs", [1, HD], F32, kind="ExternalInput").ap()
    ks_d = nc.dram_tensor("ks", [1, HD], F32, kind="ExternalInput").ap()
    qspat_d = nc.dram_tensor("qspat", [128, 1], F32, kind="ExternalInput").ap()
    kspat_d = nc.dram_tensor("kspat", [128, 1], F32, kind="ExternalInput").ap()
    fscale_d = nc.dram_tensor("fscale", [128, D], F32, kind="ExternalInput").ap()
    ident_d = nc.dram_tensor("ident", [128, 128], BF16, kind="ExternalInput").ap()
    sstat_d = nc.dram_tensor("sstat", [128, 32], BF16, kind="ExternalInput").ap()
    fstat_d = nc.dram_tensor("fstat", [8, 512], BF16, kind="ExternalInput").ap()
    dsel_d = nc.dram_tensor("dsel", [16, 1024], BF16, kind="ExternalInput").ap()
    onesb_d = nc.dram_tensor("onesb", [1, 128], BF16, kind="ExternalInput").ap()
    y_d = nc.dram_tensor("y", [SEQS * N, D], F32, kind="ExternalOutput").ap()
    if DEBUG:
        dbg_xnf = nc.dram_tensor("dbg_xnf", [128, NCH * N], BF16, kind="ExternalOutput").ap()
        dbg_qh = nc.dram_tensor("dbg_qh", [128, NCH * N], BF16, kind="ExternalOutput").ap()
        dbg_kh = nc.dram_tensor("dbg_kh", [128, NCH * N], BF16, kind="ExternalOutput").ap()
        dbg_vt = nc.dram_tensor("dbg_vt", [128, NKT * 1040], BF16, kind="ExternalOutput").ap()
        dbg_dn = nc.dram_tensor("dbg_dn", [16, N], BF16, kind="ExternalOutput").ap()
        dbg_oun = nc.dram_tensor("dbg_oun", [128, NCH * N], BF16, kind="ExternalOutput").ap()
        dbg_onr = nc.dram_tensor("dbg_onr", [128, NCH * N], BF16, kind="ExternalOutput").ap()
        dbg_wbf = nc.dram_tensor("dbg_wbf", [128, D], BF16, kind="ExternalOutput").ap()
        dbg_q0 = nc.dram_tensor("dbg_q0", [128, N], BF16, kind="ExternalOutput").ap()

    import contextlib
    ctx = contextlib.ExitStack()
    with ctx:
        wproj_bf = ctx.enter_context(nc.sbuf_tensor("wproj_bf", [128, 8 * D], BF16))
        ws_bf = ctx.enter_context(nc.sbuf_tensor("ws_bf", [128, 8 * D], BF16))
        ident_bf = ctx.enter_context(nc.sbuf_tensor("ident_bf", [128, 128], BF16))
        sstat_s = ctx.enter_context(nc.sbuf_tensor("sstat_s", [128, 32], BF16))
        fstat_b = ctx.enter_context(nc.sbuf_tensor("fstat_b", [8, 512], BF16))
        dsel_s = ctx.enter_context(nc.sbuf_tensor("dsel_s", [16, 1024], BF16))
        onesb_s = ctx.enter_context(nc.sbuf_tensor("onesb_s", [1, 128], BF16))
        qspat_s = ctx.enter_context(nc.sbuf_tensor("qspat_s", [128, 1], F32))
        kspat_s = ctx.enter_context(nc.sbuf_tensor("kspat_s", [128, 1], F32))
        bprow = ctx.enter_context(nc.sbuf_tensor("bprow", [1, D], BF16))
        fbrow = ctx.enter_context(nc.sbuf_tensor("fbrow", [1, D], BF16))
        qsr = ctx.enter_context(nc.sbuf_tensor("qsr", [1, HD], F32))
        ksr = ctx.enter_context(nc.sbuf_tensor("ksr", [1, HD], F32))
        negb = ctx.enter_context(nc.sbuf_tensor("negb", [128, 1], F32))
        small = ctx.enter_context(nc.sbuf_tensor("small", [128, 16], F32))
        smallb = ctx.enter_context(nc.sbuf_tensor("smallb", [1, 16], BF16))
        qh = ctx.enter_context(nc.sbuf_tensor("qh", [128, NCH * N], BF16))
        kh = ctx.enter_context(nc.sbuf_tensor("kh", [128, NCH * N], BF16))
        vt = ctx.enter_context(nc.sbuf_tensor("vt", [128, NKT * 1040], BF16))
        xnf = ctx.enter_context(nc.sbuf_tensor("xnf", [128, NCH * N], BF16))
        dn = ctx.enter_context(nc.sbuf_tensor("dn", [16, N], BF16))
        xio = ctx.enter_context(nc.sbuf_tensor("xio", [128, 2 * D], F32))
        wbf = ctx.enter_context(nc.sbuf_tensor("wbf", [128, 2 * D], BF16))
        arena = ctx.enter_context(nc.sbuf_tensor("arena", [128, ARENA], U8))
        capm = ctx.enter_context(nc.sbuf_tensor("capm", [128, 1], F32))
        ps_a = ctx.enter_context(nc.psum_tensor("ps_a", [128, 2048], F32))
        ps_b = ctx.enter_context(nc.psum_tensor("ps_b", [128, 2048], F32))
        _SY = ctx.enter_context(nc.semaphore("SY"))
        _GP = ctx.enter_context(nc.semaphore("GP"))
        _PE = ctx.enter_context(nc.semaphore("PE"))
        _AC = ctx.enter_context(nc.semaphore("AC"))
        _DV = ctx.enter_context(nc.semaphore("DV"))
        _ST = ctx.enter_context(nc.semaphore("ST"))
        _BAR = ctx.enter_context(nc.semaphore("BAR"))
        block = ctx.enter_context(nc.Block())
        SY, GP, PE, AC, DV, STm, BAR = (Sem(_SY), Sem(_GP), Sem(_PE),
                                        Sem(_AC), Sem(_DV), Sem(_ST), Sem(_BAR))
        em = Emitter()
        nc.const_aps.register(F32, EPS_LN, capm[:])
        psa_bf = ps_a[:].bitcast(BF16)
        psb_bf = ps_b[:].bitcast(BF16)

        def av(off, nbytes, dt, parts=128):
            a = arena.ap()[:, off:off + nbytes].bitcast(dt)
            if parts != 128:
                a = a[0:parts, :]
            return a

        xntm = av(0, 4096, BF16)
        sqscrA = av(4096, 4096, F32)
        q2b = av(8192, 2048, BF16)
        fqsb = av(10240, 4096, F32)
        vfm = av(8192, 4096, BF16)
        fsb = av(14336, 2048, F32, 8)
        sqf = av(16384, 2048, F32, 8)
        fse = av(18432, 2048, F32, 8)
        fhi = av(20480, 1024, BF16, 8)
        flo = av(21504, 1024, BF16, 8)
        ptb = av(0, 8192, BF16)
        dscr = av(8192, 4096, BF16, 128)
        rdnf = av(0, 8192, F32, 16)
        rdnb = av(12288, 4096, BF16, 16)
        xntmE = av(0, 4096, BF16)
        xn2 = av(4096, 4096, BF16)
        gclip = av(8192, 4096, F32)
        gact = av(12288, 4096, F32)
        sqscrE = av(16384, 4096, F32)
        wstage = av(0, 4096, F32)
        wmask = av(4096, 1024, U8)
        scb = av(8192, 4096, F32)
        nscb = av(12288, 4096, F32)

        em.op("vector", "memset", inc=DV, ap=capm[:], constant=EPS_LN)
        for dst, src in ((ident_bf[:], ident_d), (sstat_s[:], sstat_d),
                         (fstat_b[:], fstat_d), (dsel_s[:], dsel_d),
                         (onesb_s[:], onesb_d), (qspat_s[:], qspat_d),
                         (kspat_s[:], kspat_d), (qsr[:], qs_d),
                         (ksr[:], ks_d), (scb, fscale_d)):
            em.op("sync", "dma_start", inc=SY, inck=16, out=dst, in_=src)
        ld_consts = SY.n
        for j in range(NCH):
            em.op("gpsimd", "dma_start", inc=GP, inck=16,
                  out=wproj_bf[:, j * D:(j + 1) * D],
                  in_=wproj_d[j * 128:(j + 1) * 128, :])
        em.op("gpsimd", "dma_start", inc=GP, inck=16,
              out=bprow[:], in_=bproj_d.rearrange("(a b) -> a b", a=1))
        em.op("gpsimd", "dma_start", inc=GP, inck=16,
              out=fbrow[:], in_=fbias_d.rearrange("(a b) -> a b", a=1))
        gp_wfix = GP.n

        dv_sel_prev = 0
        for j in range(NCH):
            em.op("sync", "dma_start", inc=SY, inck=16,
                  out=wstage, in_=wt_d[j * 128:(j + 1) * 128, :],
                  waits=[(DV, dv_sel_prev)])
            w = [(SY, SY.n), (SY, ld_consts)]
            em.op("vector", "tensor_single_scalar", waits=w, inc=DV,
                  out=wmask, in_=wstage, scalar=0.0, op=ALU.is_ge)
            if j == 0:
                em.op("vector", "tensor_scalar_mul", out=nscb,
                      in0=scb, scalar1=-1.0)
            em.op("vector", "select", inc=DV,
                  out=ws_bf[:, j * D:(j + 1) * D], mask=wmask,
                  on_true=scb, on_false=nscb)
            dv_sel_prev = DV.n
        dv_ws = DV.n
        em.op("vector", "reduce_max", waits=[(SY, ld_consts)],
              out=small[0:1, 0:1], in_=qsr[:], axis=AX.X,
              apply_absolute_value=True)
        em.op("vector", "reduce_max", inc=DV, out=small[0:1, 1:2], in_=ksr[:],
              axis=AX.X, apply_absolute_value=True)
        em.op("vector", "tensor_mul", waits=[(DV, DV.n)], inc=DV,
              out=small[0:1, 2:3],
              in0=small[0:1, 0:1], in1=small[0:1, 1:2])
        em.op("vector", "tensor_scalar_mul", waits=[(DV, DV.n)], inc=DV,
              out=small[0:1, 3:4],
              in0=small[0:1, 2:3], scalar1=-8.0)
        em.op("vector", "tensor_copy", waits=[(DV, DV.n)], inc=DV,
              out=smallb[0:1, 0:1], in_=small[0:1, 3:4])
        em.op("tensor", "matmul", waits=[(DV, DV.n)], inc=PE,
              out=ps_b[0:128, 0:1], lhsT=onesb_s[:],
              rhs=smallb[0:1, 0:1], start=True, stop=True)
        em.op("vector", "tensor_copy", waits=[(PE, PE.n)], inc=DV,
              out=negb[:], in_=ps_b[0:128, 0:1])
        dv_negb = DV.n

        st_total = [0]

        def emit_ln(t, s, sy_val, xt, out_tile, dv_var_prev, sqdump, extraw=()):
            def dve(*a, waits=(), **kw):
                em.op("vector", *a, waits=list(waits) + [(DV, DV.n)], inc=DV, **kw)
            w0 = list(extraw) + [(SY, sy_val)]
            em.op("vector", "reduce_sum", waits=w0, inc=DV, out=small[:, 0:1],
                  in_=xt, axis=AX.X)
            wa = list(w0)
            if dv_var_prev:
                wa.append((DV, dv_var_prev))
            em.op("scalar", "activation", waits=wa, inc=AC,
                  out=sqdump, in_=xt, func=AF.Square, accum_out=small[:, 1:2])
            ac_sq = AC.n
            dve("tensor_scalar_mul", waits=[(AC, ac_sq)],
                out=small[:, 2:3], in0=small[:, 0:1], scalar1=1.0 / D)
            dve("tensor_mul", out=small[:, 3:4],
                in0=small[:, 2:3], in1=small[:, 2:3])
            dve("scalar_tensor_tensor", out=small[:, 4:5], in0=small[:, 1:2],
                scalar=1.0 / D, in1=small[:, 3:4], op0=ALU.mult,
                op1=ALU.subtract)
            dve("tensor_scalar_add", out=small[:, 4:5], in0=small[:, 4:5],
                scalar1=EPS_LN)
            dv_var = DV.n
            em.op("scalar", "activation", waits=[(DV, dv_var)], inc=AC,
                  out=small[:, 5:6], in_=small[:, 4:5], func=AF.Sqrt,
                  bias=0.0, scale=1.0)
            dve("reciprocal", waits=[(AC, AC.n)],
                out=small[:, 6:7], in_=small[:, 5:6])
            dve("tensor_mul", out=small[:, 8:9],
                in0=small[:, 6:7], in1=small[:, 6:7])
            dve("tensor_mul", out=small[:, 8:9],
                in0=small[:, 8:9], in1=small[:, 4:5])
            dve("tensor_scalar", out=small[:, 8:9], in0=small[:, 8:9],
                scalar1=-0.5, scalar2=1.5, op0=ALU.mult, op1=ALU.add)
            dve("tensor_mul", out=small[:, 6:7],
                in0=small[:, 6:7], in1=small[:, 8:9])
            dve("scalar_tensor_tensor", out=small[:, 7:8], in0=small[:, 2:3],
                scalar=-1.0, in1=small[:, 6:7], op0=ALU.mult, op1=ALU.mult)
            dve("tensor_scalar", out=out_tile, in0=xt,
                scalar1=small[:, 6:7], scalar2=small[:, 7:8],
                op0=ALU.mult, op1=ALU.add)
            return dv_var, DV.n

        for s in range(SEQS):
            barw = [(BAR, 5 * s)]

            dv_apply = {}
            dv_trc = {}
            dv_varA = 0
            for tp in range(NT // 2):
                t0, t1 = 2 * tp, 2 * tp + 1
                sy_ld = {}
                for t in (t0, t1):
                    p = t % 2
                    w = list(barw)
                    if t >= 2:
                        w.append((DV, dv_apply[t - 2]))
                    em.op("sync", "dma_start", waits=w, inc=SY, inck=16,
                          out=xio[:, p * D:(p + 1) * D],
                          in_=x_d[(s * NT + t) * 128:(s * NT + t + 1) * 128, :])
                    sy_ld[t] = SY.n

                def dve(*a, waits=(), **kw):
                    em.op("vector", *a, waits=list(waits) + [(DV, DV.n)],
                          inc=DV, **kw)
                for i, t in enumerate((t0, t1)):
                    xt = xio[:, (t % 2) * D:(t % 2 + 1) * D]
                    em.op("vector", "reduce_sum", waits=barw + [(SY, sy_ld[t])],
                          inc=DV, out=small[:, 0 + i:1 + i], in_=xt, axis=AX.X)
                    wa = barw + [(SY, sy_ld[t])]
                    if dv_varA:
                        wa.append((DV, dv_varA))
                    em.op("scalar", "activation", waits=wa, inc=AC,
                          out=sqscrA, in_=xt, func=AF.Square,
                          accum_out=small[:, 2 + i:3 + i])
                ac_sq = AC.n
                dve("tensor_scalar_mul", waits=[(AC, ac_sq)],
                    out=small[:, 4:6], in0=small[:, 0:2], scalar1=1.0 / D)
                dve("tensor_mul", out=small[:, 6:8],
                    in0=small[:, 4:6], in1=small[:, 4:6])
                dve("scalar_tensor_tensor", out=small[:, 8:10],
                    in0=small[:, 2:4], scalar=1.0 / D, in1=small[:, 6:8],
                    op0=ALU.mult, op1=ALU.subtract)
                dve("tensor_scalar_add", out=small[:, 8:10],
                    in0=small[:, 8:10], scalar1=EPS_LN)
                dv_varA = DV.n
                em.op("scalar", "activation", waits=[(DV, dv_varA)], inc=AC,
                      out=small[:, 10:12], in_=small[:, 8:10], func=AF.Sqrt,
                      bias=0.0, scale=1.0)
                dve("reciprocal", waits=[(AC, AC.n)],
                    out=small[:, 12:14], in_=small[:, 10:12])
                dve("tensor_mul", out=small[:, 10:12],
                    in0=small[:, 12:14], in1=small[:, 12:14])
                dve("tensor_mul", out=small[:, 10:12],
                    in0=small[:, 10:12], in1=small[:, 8:10])
                dve("tensor_scalar", out=small[:, 10:12], in0=small[:, 10:12],
                    scalar1=-0.5, scalar2=1.5, op0=ALU.mult, op1=ALU.add)
                dve("tensor_mul", out=small[:, 12:14],
                    in0=small[:, 12:14], in1=small[:, 10:12])
                dve("scalar_tensor_tensor", out=small[:, 14:16],
                    in0=small[:, 4:6], scalar=-1.0, in1=small[:, 12:14],
                    op0=ALU.mult, op1=ALU.mult)
                for i, t in enumerate((t0, t1)):
                    p = t % 2
                    xt = xio[:, p * D:(p + 1) * D]
                    dve("tensor_scalar", out=xntm[:, p * D:(p + 1) * D],
                        in0=xt, scalar1=small[:, 12 + i:13 + i],
                        scalar2=small[:, 14 + i:15 + i],
                        op0=ALU.mult, op1=ALU.add)
                    dv_apply[t] = DV.n
                    w = [(DV, dv_apply[t])]
                    if t >= 1:
                        w.append((DV, dv_trc[t - 1]))
                    for j in range(NCH):
                        em.op("tensor", "transpose", waits=w if j == 0 else (),
                              inc=(PE if j == NCH - 1 else None),
                              out=psa_bf[:, j * 128:(j + 1) * 128],
                              in_=xntm[:, p * D + j * 128:p * D + (j + 1) * 128],
                              identity=ident_bf[:])
                    em.op("vector", "tensor_copy", waits=[(PE, PE.n)], inc=DV,
                          out=xnf.ap().rearrange("p (c n) -> p c n", c=NCH)[
                              :, :, t * 128:(t + 1) * 128],
                          in_=psa_bf[:, 0:D].rearrange("p (c n) -> p c n", c=NCH))
                    dv_trc[t] = DV.n
            dv_xnf = DV.n
            gp_w = {}
            pe_last_wbf = {}
            dv_raw = {}
            dv_stt = {}
            pe_qk = {}
            pe_ssm = {}
            ac_sqr = {}
            ac_sqrt = {}
            dv_f = {}
            pe_fm = {}
            dv_fev = {}
            pe_vm = {}
            dv_vev = {}
            dv_vtc = {}

            def load_wchunk(m):
                w = list(barw)
                if m >= 2:
                    w.append((PE, pe_last_wbf[m - 2]))
                em.op("gpsimd", "dma_start", waits=w, inc=GP, inck=16,
                      out=wbf[:, (m % 2) * D:(m % 2 + 1) * D].rearrange(
                          "p (j c) -> p j c", j=NCH),
                      in_=wqkv_d[:, m * 128:(m + 1) * 128].rearrange(
                          "(j p) c -> p j c", j=NCH))
                gp_w[m] = GP.n

            load_wchunk(0)
            load_wchunk(1)

            def emit_rmspipe(m):
                isq = m < 8
                dst = qh if isq else kh
                ch = m % 8
                pat = qspat_s if isq else kspat_s
                for qtr in range(4):
                    w = [(DV, dv_raw[m])]
                    if (m, qtr) != (0, 0):
                        pm, pq = (m, qtr - 2) if qtr >= 2 else (m - 1, qtr + 2)
                        if (pm, pq) in pe_ssm:
                            w.append((PE, pe_ssm[(pm, pq)]))
                    em.op("scalar", "activation", waits=w, inc=AC,
                          out=q2b[:, (qtr % 2) * 512:(qtr % 2 + 1) * 512],
                          in_=dst[:, ch * N + qtr * 512:ch * N + (qtr + 1) * 512],
                          func=AF.Square)
                    ac_sqr[(m, qtr)] = AC.n
                    w = [(AC, ac_sqr[(m, qtr)])]
                    if qtr == 0 and m >= 1:
                        w.append((AC, ac_sqrt[m - 1]))
                    em.op("tensor", "matmul", waits=w, inc=PE,
                          out=ps_b[0:8, 0:512],
                          lhsT=sstat_s[:, qtr * 8:(qtr + 1) * 8],
                          rhs=q2b[:, (qtr % 2) * 512:(qtr % 2 + 1) * 512],
                          start=(qtr == 0), stop=(qtr == 3))
                    pe_ssm[(m, qtr)] = PE.n
                w = [(PE, pe_ssm[(m, 3)])]
                if m >= 1:
                    w.append((PE, pe_fm[(m - 1, 3)]))
                em.op("scalar", "activation", waits=w, inc=AC,
                      out=sqf, in_=ps_b[0:8, 0:512], func=AF.Sqrt,
                      bias=0.0, scale=1.0 / 64.0)
                em.op("scalar", "activation", inc=AC,
                      out=fse, in_=ps_b[0:8, 0:512], func=AF.Copy,
                      bias=0.0, scale=1.0 / 64.0)
                ac_sqrt[m] = AC.n
                w = [(AC, ac_sqrt[m])]
                if m >= 1:
                    w.append((PE, pe_fm[(m - 1, 3)]))
                def dvb(*a, waits=(), **kw):
                    em.op("vector", *a, waits=list(waits) + [(DV, DV.n)],
                          inc=DV, **kw)
                dvb("reciprocal", waits=w, out=fsb, in_=sqf)
                dvb("tensor_mul", out=fsb, in0=fsb, in1=fse)
                dvb("tensor_add", out=fsb, in0=fsb, in1=sqf)
                dvb("tensor_scalar", out=fsb, in0=fsb, scalar1=0.5,
                    scalar2=EPS_RMS, op0=ALU.mult, op1=ALU.add)
                dvb("reciprocal", out=sqf, in_=fsb)
                dvb("tensor_copy", out=fhi, in_=sqf)
                dvb("tensor_sub", out=fse, in0=sqf, in1=fhi)
                dvb("tensor_copy", out=flo, in_=fse)
                dv_f[m] = DV.n
                for qtr in range(4):
                    fb = 512 + (qtr % 2) * 512
                    w = [(DV, dv_f[m])]
                    if qtr >= 2:
                        w.append((DV, dv_fev[(m, qtr - 2)]))
                    elif m >= 1:
                        w.append((DV, dv_fev[(m - 1, qtr + 2)]))
                    em.op("tensor", "matmul", waits=w,
                          out=ps_b[:, fb:fb + 512],
                          lhsT=fstat_b[:, qtr * 128:(qtr + 1) * 128],
                          rhs=fhi, start=True, stop=False)
                    em.op("tensor", "matmul", inc=PE,
                          out=ps_b[:, fb:fb + 512],
                          lhsT=fstat_b[:, qtr * 128:(qtr + 1) * 128],
                          rhs=flo, start=False, stop=True)
                    pe_fm[(m, qtr)] = PE.n
                    em.op("vector", "tensor_copy", waits=[(PE, pe_fm[(m, qtr)])],
                          inc=DV,
                          out=fqsb[:, (qtr % 2) * 512:(qtr % 2 + 1) * 512],
                          in_=ps_b[:, fb:fb + 512])
                    dv_fev[(m, qtr)] = DV.n
                    em.op("vector", "scalar_tensor_tensor", inc=DV,
                          waits=[(DV, dv_fev[(m, qtr)])],
                          out=dst[:, ch * N + qtr * 512:ch * N + (qtr + 1) * 512],
                          in0=dst[:, ch * N + qtr * 512:ch * N + (qtr + 1) * 512],
                          scalar=pat[:],
                          in1=fqsb[:, (qtr % 2) * 512:(qtr % 2 + 1) * 512],
                          op0=ALU.mult, op1=ALU.mult)
                    dv_stt[(m, qtr)] = DV.n

            for m in range(16):
                ch = m % 8
                dst = qh if m < 8 else kh
                for qtr in range(4):
                    w = [(GP, gp_w[m]), (DV, dv_xnf)]
                    if m >= 1:
                        w.append((DV, dv_raw[m - 1]))
                    for j in range(NCH):
                        em.op("tensor", "matmul", waits=w if j == 0 else (),
                              inc=(PE if j == NCH - 1 else None),
                              out=ps_a[:, qtr * 512:(qtr + 1) * 512],
                              lhsT=wbf[:, (m % 2) * D + j * 128:
                                       (m % 2) * D + (j + 1) * 128],
                              rhs=xnf[:, j * N + qtr * 512:
                                      j * N + (qtr + 1) * 512],
                              start=(j == 0), stop=(j == NCH - 1))
                    pe_qk[(m, qtr)] = PE.n
                em.op("vector", "tensor_copy", waits=[(PE, pe_qk[(m, 3)])],
                      inc=DV, out=dst[:, ch * N:(ch + 1) * N], in_=ps_a[:])
                dv_raw[m] = DV.n
                pe_last_wbf[m] = pe_qk[(m, 3)]
                if m + 2 < 24:
                    load_wchunk(m + 2)
                if m >= 1:
                    emit_rmspipe(m - 1)
            emit_rmspipe(15)
            if s == 0:
                em.op("vector", "memset", inc=DV, ap=vt[:], constant=1.0)
            dv_vinit = DV.n
            for vm in range(8):
                m = 16 + vm
                for qtr in range(4):
                    w = [(GP, gp_w[m]), (DV, dv_raw[15]), (DV, dv_vinit)]
                    if vm >= 1:
                        w.append((DV, dv_vev[vm - 1]))
                    for j in range(NCH):
                        em.op("tensor", "matmul", waits=w if j == 0 else (),
                              inc=(PE if j == NCH - 1 else None),
                              out=ps_a[:, qtr * 512:(qtr + 1) * 512],
                              lhsT=wbf[:, (m % 2) * D + j * 128:
                                       (m % 2) * D + (j + 1) * 128],
                              rhs=xnf[:, j * N + qtr * 512:
                                      j * N + (qtr + 1) * 512],
                              start=(j == 0), stop=(j == NCH - 1))
                pe_vm[m] = PE.n
                pe_last_wbf[m] = pe_vm[m]
                if m + 2 < 24:
                    load_wchunk(m + 2)
                em.op("vector", "tensor_copy", waits=[(PE, pe_vm[m])], inc=DV,
                      out=vfm, in_=ps_a[:])
                dv_vev[vm] = DV.n
                for grp in range(4):
                    w = [(DV, dv_vev[vm])]
                    if (vm, grp) != (0, 0):
                        pv, pg = (vm, grp - 1) if grp >= 1 else (vm - 1, 3)
                        w.append((DV, dv_vtc[(pv, pg)]))
                    for kk in range(4):
                        kt = grp * 4 + kk
                        em.op("tensor", "transpose", waits=w if kk == 0 else (),
                              inc=(PE if kk == 3 else None),
                              out=psb_bf[:, 3072 + kk * 128:3072 + (kk + 1) * 128],
                              in_=vfm[:, kt * 128:(kt + 1) * 128],
                              identity=ident_bf[:])
                    em.op("vector", "tensor_copy", waits=[(PE, PE.n)], inc=DV,
                          out=vt.ap().rearrange("p (kt c) -> p kt c", kt=NKT)[
                              :, grp * 4:(grp + 1) * 4,
                              130 * vm:130 * vm + 130].rearrange(
                              "p kt (h2 c) -> p kt h2 c", c=65)[:, :, :, 0:64],
                          in_=psb_bf[:, 3072:3584].rearrange(
                              "p (kt c) -> p kt c", kt=4).rearrange(
                              "p kt (h2 c) -> p kt h2 c", c=64))
                    dv_vtc[(vm, grp)] = DV.n
            dv_vdone = DV.n

            if DEBUG and s == 0:
                em.op("gpsimd", "dma_start", waits=[(DV, dv_vdone)], inc=GP,
                      inck=16, out=dbg_vt, in_=vt[:])
            order = [jj // 2 + 8 * (jj % 2) for jj in range(16)]
            pe_s = {}
            ac_e = {}
            pe_pv = {}
            dv_oe = {}
            gp_dn = {}
            units = 2 * NKT
            npair = NKT

            for hi, h in enumerate(order):
                ch = h % 8
                odd = h >= 8
                rowo = 64 if odd else 0
                drow = 0 if odd else 64

                def s_unit(u, hi=hi, ch=ch, rowo=rowo):
                    qha, kt = u // NKT, u % NKT
                    w = []
                    if hi == 0 and u == 0:
                        w += [(DV, dv_vdone), (DV, dv_negb)]
                    if u >= 2:
                        w.append((AC, ac_e.get((hi, u - 2), 0)))
                    elif hi > 0:
                        w.append((AC, ac_e[(hi - 1, units - 2 + u)]))
                    for qtr in range(2):
                        em.op("tensor", "matmul", waits=w if qtr == 0 else (),
                              inc=(PE if qtr == 1 else None),
                              out=ps_a[:, (u % 2) * 1024 + qtr * 512:
                                       (u % 2) * 1024 + (qtr + 1) * 512],
                              lhsT=kh[rowo:rowo + 64,
                                      ch * N + kt * 128:ch * N + (kt + 1) * 128],
                              rhs=qh[rowo:rowo + 64,
                                     ch * N + qha * 1024 + qtr * 512:
                                     ch * N + qha * 1024 + (qtr + 1) * 512],
                              start=True, stop=True)
                    pe_s[(hi, u)] = PE.n

                def e_unit(u, hi=hi):
                    w = [(PE, pe_s[(hi, u)])]
                    if u >= 4:
                        w.append((PE, pe_pv[(hi, u - 4)]))
                    elif hi > 0:
                        w.append((PE, pe_pv[(hi - 1, units - 4 + u)]))
                    em.op("scalar", "activation", waits=w, inc=AC,
                          out=ptb[:, (u % 4) * 1024:(u % 4) * 1024 + 1024],
                          in_=ps_a[:, (u % 2) * 1024:(u % 2) * 1024 + 1024],
                          func=AF.Exp, bias=negb[:], scale=0.125)
                    ac_e[(hi, u)] = AC.n

                def pv_unit(u, hi=hi, h=h, odd=odd):
                    qha, kt = u // NKT, u % NKT
                    w = [(AC, ac_e[(hi, u)])]
                    if kt == 0 and hi > 0:
                        w.append((DV, dv_oe[(hi - 1, qha)]))
                    vcol = kt * 1040 + 130 * (h % 8) + (65 if odd else 0)
                    for qtr in range(2):
                        ob = qha * 1024 + qtr * 512
                        pcol = (u % 4) * 1024 + qtr * 512
                        em.op("tensor", "matmul", waits=w if qtr == 0 else (),
                              inc=(PE if qtr == 1 else None),
                              out=ps_b[0:65, ob:ob + 512],
                              lhsT=vt[:, vcol:vcol + 65],
                              rhs=ptb[:, pcol:pcol + 512],
                              start=(kt == 0), stop=(kt == NKT - 1))
                    pe_pv[(hi, u)] = PE.n

                def evac_region(qha, hi=hi, h=h, odd=odd):
                    w = [(PE, pe_pv[(hi, qha * NKT + NKT - 1)])]
                    w2 = []
                    if hi >= 1:
                        w2 = [(GP, gp_dn[hi - 1])]
                    if not odd:
                        em.op("vector", "tensor_copy", waits=w, inc=DV,
                              out=qh[0:64, (h % 8) * N + qha * 1024:
                                     (h % 8) * N + (qha + 1) * 1024],
                              in_=ps_b[0:64, qha * 1024:(qha + 1) * 1024])
                        em.op("vector", "tensor_copy", waits=w2, inc=DV,
                              out=dscr[64:65, qha * 1024:(qha + 1) * 1024],
                              in_=ps_b[64:65, qha * 1024:(qha + 1) * 1024])
                    else:
                        em.op("vector", "tensor_copy", waits=w + w2, inc=DV,
                              out=dscr[0:65, qha * 1024:(qha + 1) * 1024],
                              in_=ps_b[0:65, qha * 1024:(qha + 1) * 1024])
                    dv_oe[(hi, qha)] = DV.n

                s_unit(0)
                s_unit(1)
                for u in range(units):
                    e_unit(u)
                    pv_unit(u)
                    if u + 2 < units:
                        s_unit(u + 2)
                    if u == NKT - 1:
                        evac_region(0)
                evac_region(1)
                if odd:
                    em.op("gpsimd", "dma_start",
                          waits=[(DV, dv_oe[(hi, 1)])], inc=GP, inck=16,
                          out=qh[64:128, (h % 8) * N:(h % 8 + 1) * N],
                          in_=dscr[0:64, :])
                em.op("gpsimd", "dma_start",
                      waits=[(DV, dv_oe[(hi, 1)])], inc=GP, inck=16,
                      out=dn[h:h + 1, :], in_=dscr[64:65, :])
                gp_dn[hi] = GP.n

            if DEBUG and s == 0:
                em.op("gpsimd", "dma_start", waits=[(GP, gp_dn[15]), (DV, dv_oe[15])],
                      inc=GP, inck=16, out=dbg_dn, in_=dn[:])
                em.op("gpsimd", "dma_start", waits=[(DV, dv_oe[15])], inc=GP,
                      inck=16, out=dbg_oun, in_=qh[:])
            em.op("vector", "reciprocal", waits=[(GP, gp_dn[15])], inc=DV,
                  out=rdnf, in_=dn[:])
            em.op("vector", "tensor_copy", waits=[(DV, DV.n)], inc=DV,
                  out=rdnb, in_=rdnf)
            dv_rdn = DV.n
            pe_fd = {}
            dv_on = {}
            for j in range(NCH):
                for ha in range(2):
                    w = [(DV, dv_rdn), (AC, ac_e[(15, units - 1)])]
                    if (j, ha) != (0, 0):
                        pj, ph = (j, 0) if ha == 1 else (j - 1, 1)
                        w.append((DV, dv_on[(pj, ph)]))
                    for qtr in range(2):
                        em.op("tensor", "matmul", waits=w if qtr == 0 else (),
                              inc=(PE if qtr == 1 else None),
                              out=ps_a[:, ha * 1024 + qtr * 512:
                                       ha * 1024 + (qtr + 1) * 512],
                              lhsT=dsel_s[:, j * 128:(j + 1) * 128],
                              rhs=rdnb[:, ha * 1024 + qtr * 512:
                                       ha * 1024 + (qtr + 1) * 512],
                              start=True, stop=True)
                    pe_fd[(j, ha)] = PE.n
                    em.op("vector", "tensor_tensor",
                          waits=[(PE, pe_fd[(j, ha)])], inc=DV,
                          out=kh[:, j * N + ha * 1024:j * N + (ha + 1) * 1024],
                          in0=qh[:, j * N + ha * 1024:j * N + (ha + 1) * 1024],
                          in1=ps_a[:, ha * 1024:(ha + 1) * 1024], op=ALU.mult)
                    dv_on[(j, ha)] = DV.n
            dv_onorm = DV.n

            if DEBUG and s == 0:
                em.op("gpsimd", "dma_start", waits=[(DV, dv_onorm)], inc=GP,
                      inck=16, out=dbg_onr, in_=kh[:])
            dv_clip = {}
            dv_epi = {}
            st_done = {}
            for t in range(NT):
                p = t % 2
                w = [(DV, dv_onorm)]
                if t >= 2:
                    w.append((DV, dv_epi[2 * (t - 2) + 1]))
                em.op("sync", "dma_start", waits=w, inc=SY, inck=16,
                      out=xio[:, p * D:(p + 1) * D],
                      in_=x_d[(s * NT + t) * 128:(s * NT + t + 1) * 128, :])
                sy_x2 = SY.n
                xt = xio[:, p * D:(p + 1) * D]
                for fh in range(2):
                    ci = 2 * t + fh
                    sb = (ci % 4) * 512
                    gb = (ci % 4) * 512
                    w = [(DV, dv_onorm), (GP, gp_wfix)]
                    if ci >= 4:
                        w.append((DV, dv_epi[ci - 4]))
                    for j in range(NCH):
                        em.op("tensor", "matmul", waits=w if j == 0 else (),
                              out=ps_b[:, sb:sb + 512],
                              lhsT=kh[:, j * N + t * 128:j * N + (t + 1) * 128],
                              rhs=wproj_bf[:, j * D + fh * 512:
                                           j * D + fh * 512 + 512],
                              start=(j == 0), stop=False)
                    em.op("tensor", "matmul",
                          out=ps_b[:, sb:sb + 512], lhsT=onesb_s[:],
                          rhs=bprow[:, fh * 512:fh * 512 + 512],
                          start=False, stop=True)
                    w = []
                    if ci >= 4:
                        w.append((DV, dv_clip[ci - 4]))
                    for j in range(NCH):
                        em.op("tensor", "matmul", waits=w if j == 0 else (),
                              out=ps_a[:, gb:gb + 512],
                              lhsT=xnf[:, j * N + t * 128:j * N + (t + 1) * 128],
                              rhs=ws_bf[:, j * D + fh * 512:
                                        j * D + fh * 512 + 512],
                              start=(j == 0), stop=False)
                    em.op("tensor", "matmul", inc=PE,
                          out=ps_a[:, gb:gb + 512], lhsT=onesb_s[:],
                          rhs=fbrow[:, fh * 512:fh * 512 + 512],
                          start=False, stop=True)
                    pe_sg = PE.n
                    em.op("vector", "tensor_scalar", waits=[(PE, pe_sg)],
                          inc=DV,
                          out=gclip[:, fh * 512:(fh + 1) * 512],
                          in0=ps_a[:, gb:gb + 512], scalar1=-100.0,
                          scalar2=100.0, op0=ALU.max, op1=ALU.min)
                    dv_clip[ci] = DV.n
                    w = [(DV, dv_clip[ci])]
                    if ci >= 2:
                        w.append((STm, st_done[ci - 2]))
                    em.op("scalar", "activation", waits=w, inc=AC,
                          out=gact[:, fh * 512:(fh + 1) * 512],
                          in_=gclip[:, fh * 512:(fh + 1) * 512], func=AF.Silu)
                    em.op("vector", "scalar_tensor_tensor", waits=[(AC, AC.n)],
                          inc=DV,
                          out=gclip[:, fh * 512:(fh + 1) * 512],
                          in0=ps_b[:, sb:sb + 512], scalar=1.0,
                          in1=gact[:, fh * 512:(fh + 1) * 512],
                          op0=ALU.mult, op1=ALU.mult)
                    em.op("vector", "tensor_tensor",
                          waits=[(DV, DV.n), (SY, sy_x2)], inc=DV,
                          out=gact[:, fh * 512:(fh + 1) * 512],
                          in0=gclip[:, fh * 512:(fh + 1) * 512],
                          in1=xt[:, fh * 512:(fh + 1) * 512], op=ALU.add)
                    dv_epi[ci] = DV.n
                    em.op("sync", "dma_start", waits=[(DV, dv_epi[ci])],
                          inc=STm, inck=16,
                          out=y_d[(s * NT + t) * 128:(s * NT + t + 1) * 128,
                                  fh * 512:(fh + 1) * 512],
                          in_=gact[:, fh * 512:(fh + 1) * 512])
                    st_done[ci] = STm.n
                    st_total[0] = STm.n

            em.op("sync", "nop", waits=[(STm, st_total[0])], inc=BAR)
            em.op("gpsimd", "nop", waits=[(GP, GP.n)], inc=BAR)
            em.op("tensor", "nop", inc=BAR)
            em.op("vector", "nop", inc=BAR)
            em.op("scalar", "nop", inc=BAR)

        em.op("sync", "nop", waits=[(STm, st_total[0])])

        @block.sync
        def _(sync):
            for fn in em.events["sync"]:
                fn(sync)

        @block.gpsimd
        def _(gpsimd):
            for fn in em.events["gpsimd"]:
                fn(gpsimd)

        @block.tensor
        def _(tensor):
            for fn in em.events["tensor"]:
                fn(tensor)

        @block.vector
        def _(vector):
            for fn in em.events["vector"]:
                fn(vector)

        @block.scalar
        def _(scalar):
            for fn in em.events["scalar"]:
                fn(scalar)

    return nc


_NC_CACHE = None


def _get_nc():
    global _NC_CACHE
    if _NC_CACHE is None:
        _NC_CACHE = build_nc()
    return _NC_CACHE


def kernel(L0_attn_x, L0_attn_Wqkv, L0_attn_Wproj, L0_attn_bproj,
           L0_attn_q_scale, L0_attn_k_scale, L0_ffn_weights, L0_ffn_scale,
           L0_ffn_bias):
    x = np.asarray(L0_attn_x, dtype=np.float32)
    wqkv = np.asarray(L0_attn_Wqkv, dtype=np.float32)
    wproj = np.asarray(L0_attn_Wproj, dtype=np.float32)
    bproj = np.asarray(L0_attn_bproj, dtype=np.float32)
    qs = np.asarray(L0_attn_q_scale, dtype=np.float32)
    ks = np.asarray(L0_attn_k_scale, dtype=np.float32)
    wf = np.asarray(L0_ffn_weights, dtype=np.float32)
    fsc = np.asarray(L0_ffn_scale, dtype=np.float32)
    fbi = np.asarray(L0_ffn_bias, dtype=np.float32)

    wq = wqkv[:, 0:D].reshape(D, H, HD)
    wk = wqkv[:, D:2 * D].reshape(D, H, HD)
    wv = wqkv[:, 2 * D:3 * D].reshape(D, H, HD)
    cols = []
    for part in (wq, wk, wv):
        for j in range(8):
            cols.append(part[:, j, :])
            cols.append(part[:, j + 8, :])
    wqkv_p = np.ascontiguousarray(np.concatenate(cols, axis=1))
    wp = wproj.reshape(H, HD, D)
    rows = []
    for j in range(8):
        rows.append(wp[j])
        rows.append(wp[j + 8])
    wproj_p = np.ascontiguousarray(np.concatenate(rows, axis=0))
    wt = np.ascontiguousarray(wf.T)
    qspat = np.ascontiguousarray(np.tile(qs, 2)[:, None])
    kspat = np.ascontiguousarray(np.tile(ks, 2)[:, None])
    fscale_b = np.ascontiguousarray(np.broadcast_to(fsc[None, :], (128, D)))

    bf = ml_dtypes.bfloat16
    ident = np.eye(128, dtype=np.float32).astype(bf)
    sstat = np.zeros((128, 32), np.float32)
    for i in range(4):
        sstat[0:64, 8 * i + 2 * i] = 1.0
        sstat[64:128, 8 * i + 2 * i + 1] = 1.0
    sstat = sstat.astype(bf)
    fstat = np.zeros((8, 512), np.float32)
    for i in range(4):
        fstat[2 * i, 128 * i:128 * i + 64] = 1.0
        fstat[2 * i + 1, 128 * i + 64:128 * i + 128] = 1.0
    fstat = fstat.astype(bf)
    dsel = np.zeros((16, 1024), np.float32)
    for j in range(8):
        dsel[j, j * 128:j * 128 + 64] = 1.0
        dsel[j + 8, j * 128 + 64:j * 128 + 128] = 1.0
    dsel = dsel.astype(bf)
    onesb = np.ones((1, 128), np.float32).astype(bf)

    nc = _get_nc()
    xr = x.reshape(32, N, D)
    in_maps = []
    for c in range(NCORES):
        in_maps.append({
            "x": np.ascontiguousarray(
                xr[SEQS * c:SEQS * (c + 1)].reshape(SEQS * N, D)),
            "wqkv": wqkv_p, "wproj": wproj_p, "wt": wt,
            "bproj": bproj, "fbias": fbi,
            "qs": qs[None, :], "ks": ks[None, :],
            "qspat": qspat, "kspat": kspat, "fscale": fscale_b,
            "ident": ident, "sstat": sstat, "fstat": fstat,
            "dsel": dsel, "onesb": onesb,
        })
    res = run_bass_kernel_spmd(nc, in_maps, core_ids=list(range(NCORES)))
    out = np.concatenate([r["y"].reshape(SEQS, N, D) for r in res.results],
                         axis=0)
    return out


# version 13
# speedup vs baseline: 1.9242x; 1.2216x over previous
import numpy as np
import ml_dtypes
import concourse.bass as bass
import concourse.mybir as mybir
from concourse.bass_utils import run_bass_kernel_spmd

F32 = mybir.dt.float32
BF16 = mybir.dt.bfloat16
F32R = mybir.dt.float32r
U8 = mybir.dt.uint8
AF = mybir.ActivationFunctionType
ALU = mybir.AluOpType
AX = mybir.AxisListType

NCORES = 8
SEQS = 4
N = 2048
D = 1024
H = 16
HD = 64
NT = N // 128
NKT = N // 128
NCH = D // 128
EPS_LN = 1e-5
EPS_RMS = 1e-8
ENGS = ("sync", "gpsimd", "tensor", "vector", "scalar")
ARENA = 22528
DEBUG = False


class Sem:
    def __init__(self, handle):
        self.h = handle
        self.n = 0


class Emitter:
    def __init__(self):
        self.events = {e: [] for e in ENGS}
        self.lastw = {e: {} for e in ENGS}

    def op(self, eng, method, *args, waits=(), inc=None, inck=1, **kw):
        real = []
        for sem, val in waits:
            if val <= 0:
                continue
            if self.lastw[eng].get(id(sem), -1) >= val:
                continue
            self.lastw[eng][id(sem)] = val
            real.append((sem.h, val))
        tgt = None
        if inc is not None:
            inc.n += inck
            tgt = (inc.h, inck)

        def fn(e, _m=method, _a=args, _k=kw, _w=real, _t=tgt):
            for h, v in _w:
                e.wait_ge(h, v)
            ins = getattr(e, _m)(*_a, **_k)
            if _t is not None:
                ins.then_inc(_t[0], _t[1])
        self.events[eng].append(fn)


def build_nc():
    nc = bass.Bass("TRN2", target_bir_lowering=False)

    x_d = nc.dram_tensor("x", [SEQS * N, D], F32, kind="ExternalInput").ap()
    wqkv_d = nc.dram_tensor("wqkv", [D, 3 * D], F32, kind="ExternalInput").ap()
    wproj_d = nc.dram_tensor("wproj", [D, D], F32, kind="ExternalInput").ap()
    wt_d = nc.dram_tensor("wt", [D, D], F32, kind="ExternalInput").ap()
    bproj_d = nc.dram_tensor("bproj", [D], F32, kind="ExternalInput").ap()
    fbias_d = nc.dram_tensor("fbias", [D], F32, kind="ExternalInput").ap()
    qs_d = nc.dram_tensor("qs", [1, HD], F32, kind="ExternalInput").ap()
    ks_d = nc.dram_tensor("ks", [1, HD], F32, kind="ExternalInput").ap()
    qspat_d = nc.dram_tensor("qspat", [128, 1], F32, kind="ExternalInput").ap()
    kspat_d = nc.dram_tensor("kspat", [128, 1], F32, kind="ExternalInput").ap()
    fscale_d = nc.dram_tensor("fscale", [128, D], F32, kind="ExternalInput").ap()
    ident_d = nc.dram_tensor("ident", [128, 128], BF16, kind="ExternalInput").ap()
    sstat_d = nc.dram_tensor("sstat", [128, 32], BF16, kind="ExternalInput").ap()
    fstat_d = nc.dram_tensor("fstat", [8, 512], BF16, kind="ExternalInput").ap()
    dsel_d = nc.dram_tensor("dsel", [16, 1024], BF16, kind="ExternalInput").ap()
    onesb_d = nc.dram_tensor("onesb", [1, 128], BF16, kind="ExternalInput").ap()
    y_d = nc.dram_tensor("y", [SEQS * N, D], F32, kind="ExternalOutput").ap()
    xnsave_d = nc.dram_tensor("xnsave", [128, NCH * N], BF16).ap()
    if DEBUG:
        dbg_xnf = nc.dram_tensor("dbg_xnf", [128, NCH * N], BF16, kind="ExternalOutput").ap()
        dbg_qh = nc.dram_tensor("dbg_qh", [128, NCH * N], BF16, kind="ExternalOutput").ap()
        dbg_kh = nc.dram_tensor("dbg_kh", [128, NCH * N], BF16, kind="ExternalOutput").ap()
        dbg_vt = nc.dram_tensor("dbg_vt", [128, NKT * 1040], BF16, kind="ExternalOutput").ap()
        dbg_dn = nc.dram_tensor("dbg_dn", [16, N], BF16, kind="ExternalOutput").ap()
        dbg_oun = nc.dram_tensor("dbg_oun", [128, NCH * N], BF16, kind="ExternalOutput").ap()
        dbg_onr = nc.dram_tensor("dbg_onr", [128, NCH * N], BF16, kind="ExternalOutput").ap()
        dbg_wbf = nc.dram_tensor("dbg_wbf", [128, D], BF16, kind="ExternalOutput").ap()
        dbg_q0 = nc.dram_tensor("dbg_q0", [128, N], BF16, kind="ExternalOutput").ap()

    import contextlib
    ctx = contextlib.ExitStack()
    with ctx:
        wproj_bf = ctx.enter_context(nc.sbuf_tensor("wproj_bf", [128, 8 * D], BF16))
        ws_bf = ctx.enter_context(nc.sbuf_tensor("ws_bf", [128, 8 * D], BF16))
        ident_bf = ctx.enter_context(nc.sbuf_tensor("ident_bf", [128, 128], BF16))
        sstat_s = ctx.enter_context(nc.sbuf_tensor("sstat_s", [128, 32], BF16))
        fstat_b = ctx.enter_context(nc.sbuf_tensor("fstat_b", [8, 512], BF16))
        dsel_s = ctx.enter_context(nc.sbuf_tensor("dsel_s", [16, 1024], BF16))
        onesb_s = ctx.enter_context(nc.sbuf_tensor("onesb_s", [1, 128], BF16))
        qspat_s = ctx.enter_context(nc.sbuf_tensor("qspat_s", [128, 1], F32))
        kspat_s = ctx.enter_context(nc.sbuf_tensor("kspat_s", [128, 1], F32))
        bprow = ctx.enter_context(nc.sbuf_tensor("bprow", [1, D], BF16))
        fbrow = ctx.enter_context(nc.sbuf_tensor("fbrow", [1, D], BF16))
        qsr = ctx.enter_context(nc.sbuf_tensor("qsr", [1, HD], F32))
        ksr = ctx.enter_context(nc.sbuf_tensor("ksr", [1, HD], F32))
        negb = ctx.enter_context(nc.sbuf_tensor("negb", [128, 1], F32))
        small = ctx.enter_context(nc.sbuf_tensor("small", [128, 16], F32))
        smallb = ctx.enter_context(nc.sbuf_tensor("smallb", [1, 16], BF16))
        qh = ctx.enter_context(nc.sbuf_tensor("qh", [128, NCH * N], BF16))
        kh = ctx.enter_context(nc.sbuf_tensor("kh", [128, NCH * N], BF16))
        vt = ctx.enter_context(nc.sbuf_tensor("vt", [128, NKT * 1040], BF16))
        xnf = ctx.enter_context(nc.sbuf_tensor("xnf", [128, NCH * N], BF16))
        dn = ctx.enter_context(nc.sbuf_tensor("dn", [16, N], BF16))
        xio = ctx.enter_context(nc.sbuf_tensor("xio", [128, 2 * D], F32))
        wbf = ctx.enter_context(nc.sbuf_tensor("wbf", [128, 2 * D], BF16))
        arena = ctx.enter_context(nc.sbuf_tensor("arena", [128, ARENA], U8))
        capm = ctx.enter_context(nc.sbuf_tensor("capm", [128, 1], F32))
        capm2 = ctx.enter_context(nc.sbuf_tensor("capm2", [128, 1], F32))
        ps_a = ctx.enter_context(nc.psum_tensor("ps_a", [128, 2048], F32))
        ps_b = ctx.enter_context(nc.psum_tensor("ps_b", [128, 2048], F32))
        _SY = ctx.enter_context(nc.semaphore("SY"))
        _GP = ctx.enter_context(nc.semaphore("GP"))
        _PE = ctx.enter_context(nc.semaphore("PE"))
        _AC = ctx.enter_context(nc.semaphore("AC"))
        _DV = ctx.enter_context(nc.semaphore("DV"))
        _ST = ctx.enter_context(nc.semaphore("ST"))
        _BAR = ctx.enter_context(nc.semaphore("BAR"))
        block = ctx.enter_context(nc.Block())
        SY, GP, PE, AC, DV, STm, BAR = (Sem(_SY), Sem(_GP), Sem(_PE),
                                        Sem(_AC), Sem(_DV), Sem(_ST), Sem(_BAR))
        em = Emitter()
        nc.const_aps.register(F32, EPS_LN, capm[:])
        nc.const_aps.register(F32, EPS_RMS, capm2[:])
        psa_bf = ps_a[:].bitcast(BF16)
        psb_bf = ps_b[:].bitcast(BF16)

        def av(off, nbytes, dt, parts=128):
            a = arena.ap()[:, off:off + nbytes].bitcast(dt)
            if parts != 128:
                a = a[0:parts, :]
            return a

        xntm = av(0, 4096, BF16)
        sqscrA = av(4096, 4096, F32)
        q2b = av(8192, 2048, BF16)
        fqsb = av(10240, 4096, F32)
        vfm = av(8192, 4096, BF16)
        fsb = av(14336, 2048, F32, 8)
        sqf = av(16384, 2048, F32, 8)
        fse = av(4096, 2048, F32, 8)
        fhi = av(18432, 2048, BF16, 8)
        flo = av(20480, 2048, BF16, 8)
        ptb = av(0, 8192, BF16)
        dscr = av(8192, 4096, BF16, 128)
        rdnf = av(0, 8192, F32, 16)
        rdnb = av(12288, 4096, BF16, 16)
        xn2t = av(0, 4096, BF16)
        xn2 = av(4096, 4096, BF16)
        gclip = av(8192, 4096, F32)
        gact = av(12288, 4096, F32)
        sqscrE = av(16384, 4096, F32)
        wstage = av(0, 4096, F32)
        wmask = av(4096, 1024, U8)
        scb = av(8192, 4096, F32)
        nscb = av(12288, 4096, F32)

        em.op("vector", "memset", ap=capm[:], constant=EPS_LN)
        em.op("vector", "memset", inc=DV, ap=capm2[:], constant=EPS_RMS)
        for dst, src in ((ident_bf[:], ident_d), (sstat_s[:], sstat_d),
                         (fstat_b[:], fstat_d), (dsel_s[:], dsel_d),
                         (onesb_s[:], onesb_d), (qspat_s[:], qspat_d),
                         (kspat_s[:], kspat_d), (qsr[:], qs_d),
                         (ksr[:], ks_d), (scb, fscale_d)):
            em.op("sync", "dma_start", inc=SY, inck=16, out=dst, in_=src)
        ld_consts = SY.n
        for j in range(NCH):
            em.op("gpsimd", "dma_start", inc=GP, inck=16,
                  out=wproj_bf[:, j * D:(j + 1) * D],
                  in_=wproj_d[j * 128:(j + 1) * 128, :])
        em.op("gpsimd", "dma_start", inc=GP, inck=16,
              out=bprow[:], in_=bproj_d.rearrange("(a b) -> a b", a=1))
        em.op("gpsimd", "dma_start", inc=GP, inck=16,
              out=fbrow[:], in_=fbias_d.rearrange("(a b) -> a b", a=1))
        gp_wfix = GP.n

        dv_sel_prev = 0
        for j in range(NCH):
            em.op("sync", "dma_start", inc=SY, inck=16,
                  out=wstage, in_=wt_d[j * 128:(j + 1) * 128, :],
                  waits=[(DV, dv_sel_prev)])
            w = [(SY, SY.n), (SY, ld_consts)]
            em.op("vector", "tensor_single_scalar", waits=w, inc=DV,
                  out=wmask, in_=wstage, scalar=0.0, op=ALU.is_ge)
            if j == 0:
                em.op("vector", "tensor_scalar_mul", out=nscb,
                      in0=scb, scalar1=-1.0)
            em.op("vector", "select", inc=DV,
                  out=ws_bf[:, j * D:(j + 1) * D], mask=wmask,
                  on_true=scb, on_false=nscb)
            dv_sel_prev = DV.n
        dv_ws = DV.n
        em.op("vector", "reduce_max", waits=[(SY, ld_consts)],
              out=small[0:1, 0:1], in_=qsr[:], axis=AX.X,
              apply_absolute_value=True)
        em.op("vector", "reduce_max", inc=DV, out=small[0:1, 1:2], in_=ksr[:],
              axis=AX.X, apply_absolute_value=True)
        em.op("vector", "tensor_mul", waits=[(DV, DV.n)], inc=DV,
              out=small[0:1, 2:3],
              in0=small[0:1, 0:1], in1=small[0:1, 1:2])
        em.op("vector", "tensor_scalar_mul", waits=[(DV, DV.n)], inc=DV,
              out=small[0:1, 3:4],
              in0=small[0:1, 2:3], scalar1=-8.0)
        em.op("vector", "tensor_copy", waits=[(DV, DV.n)], inc=DV,
              out=smallb[0:1, 0:1], in_=small[0:1, 3:4])
        em.op("tensor", "matmul", waits=[(DV, DV.n)], inc=PE,
              out=ps_b[0:128, 0:1], lhsT=onesb_s[:],
              rhs=smallb[0:1, 0:1], start=True, stop=True)
        em.op("vector", "tensor_copy", waits=[(PE, PE.n)], inc=DV,
              out=negb[:], in_=ps_b[0:128, 0:1])
        dv_negb = DV.n

        st_total = [0]

        def emit_ln(t, s, sy_val, xt, out_tile, dv_var_prev, sqdump, extraw=()):
            def dve(*a, waits=(), **kw):
                em.op("vector", *a, waits=list(waits) + [(DV, DV.n)], inc=DV, **kw)
            w0 = list(extraw) + [(SY, sy_val)]
            em.op("vector", "reduce_sum", waits=w0, inc=DV, out=small[:, 0:1],
                  in_=xt, axis=AX.X)
            wa = list(w0)
            if dv_var_prev:
                wa.append((DV, dv_var_prev))
            em.op("scalar", "activation", waits=wa, inc=AC,
                  out=sqdump, in_=xt, func=AF.Square, accum_out=small[:, 1:2])
            ac_sq = AC.n
            dve("tensor_scalar_mul", waits=[(AC, ac_sq)],
                out=small[:, 2:3], in0=small[:, 0:1], scalar1=1.0 / D)
            dve("tensor_mul", out=small[:, 3:4],
                in0=small[:, 2:3], in1=small[:, 2:3])
            dve("scalar_tensor_tensor", out=small[:, 4:5], in0=small[:, 1:2],
                scalar=1.0 / D, in1=small[:, 3:4], op0=ALU.mult,
                op1=ALU.subtract)
            dve("tensor_scalar_add", out=small[:, 4:5], in0=small[:, 4:5],
                scalar1=EPS_LN)
            dv_var = DV.n
            em.op("scalar", "activation", waits=[(DV, dv_var)], inc=AC,
                  out=small[:, 5:6], in_=small[:, 4:5], func=AF.Sqrt,
                  bias=0.0, scale=1.0)
            dve("reciprocal", waits=[(AC, AC.n)],
                out=small[:, 6:7], in_=small[:, 5:6])
            dve("tensor_mul", out=small[:, 8:9],
                in0=small[:, 6:7], in1=small[:, 6:7])
            dve("tensor_mul", out=small[:, 8:9],
                in0=small[:, 8:9], in1=small[:, 4:5])
            dve("tensor_scalar", out=small[:, 8:9], in0=small[:, 8:9],
                scalar1=-0.5, scalar2=1.5, op0=ALU.mult, op1=ALU.add)
            dve("tensor_mul", out=small[:, 6:7],
                in0=small[:, 6:7], in1=small[:, 8:9])
            dve("scalar_tensor_tensor", out=small[:, 7:8], in0=small[:, 2:3],
                scalar=-1.0, in1=small[:, 6:7], op0=ALU.mult, op1=ALU.mult)
            dve("tensor_scalar", out=out_tile, in0=xt,
                scalar1=small[:, 6:7], scalar2=small[:, 7:8],
                op0=ALU.mult, op1=ALU.add)
            return dv_var, DV.n

        for s in range(SEQS):
            barw = [(BAR, 5 * s)]

            dv_apply = {}
            dv_trc = {}
            dv_varA = 0
            for tp in range(NT // 2):
                t0, t1 = 2 * tp, 2 * tp + 1
                sy_ld = {}
                for t in (t0, t1):
                    p = t % 2
                    w = list(barw)
                    if t >= 2:
                        w.append((DV, dv_apply[t - 2]))
                    em.op("sync", "dma_start", waits=w, inc=SY, inck=16,
                          out=xio[:, p * D:(p + 1) * D],
                          in_=x_d[(s * NT + t) * 128:(s * NT + t + 1) * 128, :])
                    sy_ld[t] = SY.n

                def dve(*a, waits=(), **kw):
                    em.op("vector", *a, waits=list(waits) + [(DV, DV.n)],
                          inc=DV, **kw)
                for i, t in enumerate((t0, t1)):
                    xt = xio[:, (t % 2) * D:(t % 2 + 1) * D]
                    em.op("vector", "reduce_sum", waits=barw + [(SY, sy_ld[t])],
                          inc=DV, out=small[:, 0 + i:1 + i], in_=xt, axis=AX.X)
                    wa = barw + [(SY, sy_ld[t])]
                    if dv_varA:
                        wa.append((DV, dv_varA))
                    em.op("scalar", "activation", waits=wa, inc=AC,
                          out=sqscrA, in_=xt, func=AF.Square,
                          accum_out=small[:, 2 + i:3 + i])
                ac_sq = AC.n
                dve("tensor_scalar_mul", waits=[(AC, ac_sq)],
                    out=small[:, 4:6], in0=small[:, 0:2], scalar1=1.0 / D)
                dve("tensor_mul", out=small[:, 6:8],
                    in0=small[:, 4:6], in1=small[:, 4:6])
                dve("scalar_tensor_tensor", out=small[:, 8:10],
                    in0=small[:, 2:4], scalar=1.0 / D, in1=small[:, 6:8],
                    op0=ALU.mult, op1=ALU.subtract)
                dv_varA = DV.n
                em.op("scalar", "activation", waits=[(DV, dv_varA)],
                      out=small[:, 10:12], in_=small[:, 8:10], func=AF.Ln,
                      bias=EPS_LN, scale=1.0)
                em.op("scalar", "activation", inc=AC,
                      out=small[:, 12:14], in_=small[:, 10:12], func=AF.Exp,
                      bias=0.0, scale=-0.5)
                dve("scalar_tensor_tensor", waits=[(AC, AC.n)],
                    out=small[:, 14:16],
                    in0=small[:, 4:6], scalar=-1.0, in1=small[:, 12:14],
                    op0=ALU.mult, op1=ALU.mult)
                for i, t in enumerate((t0, t1)):
                    p = t % 2
                    xt = xio[:, p * D:(p + 1) * D]
                    dve("tensor_scalar", out=xntm[:, p * D:(p + 1) * D],
                        in0=xt, scalar1=small[:, 12 + i:13 + i],
                        scalar2=small[:, 14 + i:15 + i],
                        op0=ALU.mult, op1=ALU.add)
                    dv_apply[t] = DV.n
                    w = [(DV, dv_apply[t])]
                    if t >= 1:
                        w.append((DV, dv_trc[t - 1]))
                    for j in range(NCH):
                        em.op("tensor", "transpose", waits=w if j == 0 else (),
                              inc=(PE if j == NCH - 1 else None),
                              out=psa_bf[:, j * 128:(j + 1) * 128],
                              in_=xntm[:, p * D + j * 128:p * D + (j + 1) * 128],
                              identity=ident_bf[:])
                    em.op("vector", "tensor_copy", waits=[(PE, PE.n)], inc=DV,
                          out=xnf.ap().rearrange("p (c n) -> p c n", c=NCH)[
                              :, :, t * 128:(t + 1) * 128],
                          in_=psa_bf[:, 0:D].rearrange("p (c n) -> p c n", c=NCH))
                    dv_trc[t] = DV.n
            dv_xnf = DV.n
            em.op("sync", "dma_start", waits=[(DV, dv_xnf)], inc=SY, inck=16,
                  out=xnsave_d, in_=xnf[:])
            sy_xnsave = SY.n
            gp_w = {}
            pe_last_wbf = {}
            dv_raw = {}
            dv_raw_half = {}
            dv_stt = {}
            pe_qk = {}
            pe_ssm = {}
            ac_sqr = {}
            ac_sqrt = {}
            dv_f = {}
            pe_fm = {}
            dv_fev = {}
            pe_vm = {}
            dv_vev = {}
            dv_vtc = {}

            def load_wchunk(m):
                w = list(barw)
                if m >= 2:
                    w.append((PE, pe_last_wbf[m - 2]))
                em.op("gpsimd", "dma_start", waits=w, inc=GP, inck=16,
                      out=wbf[:, (m % 2) * D:(m % 2 + 1) * D].rearrange(
                          "p (j c) -> p j c", j=NCH),
                      in_=wqkv_d[:, m * 128:(m + 1) * 128].rearrange(
                          "(j p) c -> p j c", j=NCH))
                gp_w[m] = GP.n

            load_wchunk(0)
            load_wchunk(1)

            def rmspipe1(m):
                isq = m < 8
                dst = qh if isq else kh
                ch = m % 8
                for qtr in range(4):
                    w = [(DV, dv_raw[m])]
                    if (m, qtr) != (0, 0):
                        pm, pq = (m, qtr - 2) if qtr >= 2 else (m - 1, qtr + 2)
                        if (pm, pq) in pe_ssm:
                            w.append((PE, pe_ssm[(pm, pq)]))
                    em.op("scalar", "activation", waits=w, inc=AC,
                          out=q2b[:, (qtr % 2) * 512:(qtr % 2 + 1) * 512],
                          in_=dst[:, ch * N + qtr * 512:ch * N + (qtr + 1) * 512],
                          func=AF.Square)
                    ac_sqr[(m, qtr)] = AC.n
                    w = [(AC, ac_sqr[(m, qtr)])]
                    if qtr == 0 and m >= 1:
                        w.append((AC, ac_sqrt[m - 1]))
                    em.op("tensor", "matmul", waits=w, inc=PE,
                          out=ps_b[0:8, 0:512],
                          lhsT=sstat_s[:, qtr * 8:(qtr + 1) * 8],
                          rhs=q2b[:, (qtr % 2) * 512:(qtr % 2 + 1) * 512],
                          start=(qtr == 0), stop=(qtr == 3))
                    pe_ssm[(m, qtr)] = PE.n
                w = [(PE, pe_ssm[(m, 3)])]
                if m >= 1:
                    w.append((DV, dv_f[m - 1]))
                em.op("scalar", "activation", waits=w,
                      out=sqf, in_=ps_b[0:8, 0:512], func=AF.Ln,
                      bias=0.0, scale=1.0 / 64.0)
                em.op("scalar", "activation",
                      out=fsb, in_=sqf, func=AF.Exp, bias=0.0, scale=0.5)
                em.op("scalar", "activation",
                      out=sqf, in_=fsb, func=AF.Ln, bias=EPS_RMS, scale=1.0)
                em.op("scalar", "activation", inc=AC,
                      out=fsb, in_=sqf, func=AF.Exp, bias=0.0, scale=-1.0)
                ac_sqrt[m] = AC.n
                mb = (m % 2) * 512
                def dvb(*a, waits=(), **kw):
                    em.op("vector", *a, waits=list(waits) + [(DV, DV.n)],
                          inc=DV, **kw)
                wcast = [(AC, ac_sqrt[m])]
                if m >= 2:
                    wcast.append((PE, pe_fm.get((m - 2, 3), 0)))
                dvb("tensor_copy", waits=wcast,
                    out=fhi[:, mb:mb + 512], in_=fsb)
                dvb("tensor_sub", out=fse, in0=fsb, in1=fhi[:, mb:mb + 512])
                dvb("tensor_copy", out=flo[:, mb:mb + 512], in_=fse)
                dv_f[m] = DV.n

            def rmspipe2(m):
                isq = m < 8
                dst = qh if isq else kh
                ch = m % 8
                pat = qspat_s if isq else kspat_s
                for qtr in range(4):
                    fb = 512 + (qtr % 2) * 512
                    w = [(DV, dv_f[m])]
                    if qtr >= 2:
                        w.append((DV, dv_fev[(m, qtr - 2)]))
                    elif m >= 1:
                        w.append((DV, dv_fev[(m - 1, qtr + 2)]))
                    em.op("tensor", "matmul", waits=w,
                          out=ps_b[:, fb:fb + 512],
                          lhsT=fstat_b[:, qtr * 128:(qtr + 1) * 128],
                          rhs=fhi[:, (m % 2) * 512:(m % 2) * 512 + 512],
                          start=True, stop=False)
                    em.op("tensor", "matmul", inc=PE,
                          out=ps_b[:, fb:fb + 512],
                          lhsT=fstat_b[:, qtr * 128:(qtr + 1) * 128],
                          rhs=flo[:, (m % 2) * 512:(m % 2) * 512 + 512],
                          start=False, stop=True)
                    pe_fm[(m, qtr)] = PE.n
                    em.op("vector", "tensor_copy", waits=[(PE, pe_fm[(m, qtr)])],
                          inc=DV,
                          out=fqsb[:, (qtr % 2) * 512:(qtr % 2 + 1) * 512],
                          in_=ps_b[:, fb:fb + 512])
                    dv_fev[(m, qtr)] = DV.n
                    em.op("vector", "scalar_tensor_tensor", inc=DV,
                          waits=[(DV, dv_fev[(m, qtr)])],
                          out=dst[:, ch * N + qtr * 512:ch * N + (qtr + 1) * 512],
                          in0=dst[:, ch * N + qtr * 512:ch * N + (qtr + 1) * 512],
                          scalar=pat[:],
                          in1=fqsb[:, (qtr % 2) * 512:(qtr % 2 + 1) * 512],
                          op0=ALU.mult, op1=ALU.mult)
                    dv_stt[(m, qtr)] = DV.n

            for m in range(16):
                ch = m % 8
                dst = qh if m < 8 else kh
                for qtr in range(4):
                    w = [(GP, gp_w[m]), (DV, dv_xnf)]
                    if m >= 1:
                        w.append((DV, dv_raw_half[m - 1] if qtr < 2
                                  else dv_raw[m - 1]))
                    for j in range(NCH):
                        em.op("tensor", "matmul", waits=w if j == 0 else (),
                              inc=(PE if j == NCH - 1 else None),
                              out=ps_a[:, qtr * 512:(qtr + 1) * 512],
                              lhsT=wbf[:, (m % 2) * D + j * 128:
                                       (m % 2) * D + (j + 1) * 128],
                              rhs=xnf[:, j * N + qtr * 512:
                                      j * N + (qtr + 1) * 512],
                              start=(j == 0), stop=(j == NCH - 1))
                    pe_qk[(m, qtr)] = PE.n
                em.op("vector", "tensor_copy", waits=[(PE, pe_qk[(m, 1)])],
                      inc=DV, out=dst[:, ch * N:ch * N + 1024],
                      in_=ps_a[:, 0:1024])
                dv_rawa = DV.n
                em.op("vector", "tensor_copy", waits=[(PE, pe_qk[(m, 3)])],
                      inc=DV, out=dst[:, ch * N + 1024:(ch + 1) * N],
                      in_=ps_a[:, 1024:2048])
                dv_raw[m] = DV.n
                dv_raw_half[m] = dv_rawa
                pe_last_wbf[m] = pe_qk[(m, 3)]
                if m + 2 < 24:
                    load_wchunk(m + 2)
                if m >= 1:
                    rmspipe1(m - 1)
                if m >= 2:
                    rmspipe2(m - 2)
            rmspipe2(14)
            rmspipe1(15)
            rmspipe2(15)
            if s == 0:
                em.op("vector", "memset", inc=DV, ap=vt[:], constant=1.0)
            dv_vinit = DV.n
            for vm in range(8):
                m = 16 + vm
                for qtr in range(4):
                    w = [(GP, gp_w[m]), (DV, dv_raw[15]), (DV, dv_vinit)]
                    if vm >= 1:
                        w.append((DV, dv_vev[vm - 1]))
                    for j in range(NCH):
                        em.op("tensor", "matmul", waits=w if j == 0 else (),
                              inc=(PE if j == NCH - 1 else None),
                              out=ps_a[:, qtr * 512:(qtr + 1) * 512],
                              lhsT=wbf[:, (m % 2) * D + j * 128:
                                       (m % 2) * D + (j + 1) * 128],
                              rhs=xnf[:, j * N + qtr * 512:
                                      j * N + (qtr + 1) * 512],
                              start=(j == 0), stop=(j == NCH - 1))
                pe_vm[m] = PE.n
                pe_last_wbf[m] = pe_vm[m]
                if m + 2 < 24:
                    load_wchunk(m + 2)
                em.op("vector", "tensor_copy", waits=[(PE, pe_vm[m])], inc=DV,
                      out=vfm, in_=ps_a[:])
                dv_vev[vm] = DV.n
                for grp in range(4):
                    w = [(DV, dv_vev[vm])]
                    if (vm, grp) != (0, 0):
                        pv, pg = (vm, grp - 1) if grp >= 1 else (vm - 1, 3)
                        w.append((DV, dv_vtc[(pv, pg)]))
                    for kk in range(4):
                        kt = grp * 4 + kk
                        em.op("tensor", "transpose", waits=w if kk == 0 else (),
                              inc=(PE if kk == 3 else None),
                              out=psb_bf[:, 3072 + kk * 128:3072 + (kk + 1) * 128],
                              in_=vfm[:, kt * 128:(kt + 1) * 128],
                              identity=ident_bf[:])
                    em.op("vector", "tensor_copy", waits=[(PE, PE.n)], inc=DV,
                          out=vt.ap().rearrange("p (kt c) -> p kt c", kt=NKT)[
                              :, grp * 4:(grp + 1) * 4,
                              130 * vm:130 * vm + 130].rearrange(
                              "p kt (h2 c) -> p kt h2 c", c=65)[:, :, :, 0:64],
                          in_=psb_bf[:, 3072:3584].rearrange(
                              "p (kt c) -> p kt c", kt=4).rearrange(
                              "p kt (h2 c) -> p kt h2 c", c=64))
                    dv_vtc[(vm, grp)] = DV.n
            dv_vdone = DV.n
            khzn = xnf
            wkz = [(PE, pe_vm[23]), (SY, sy_xnsave)]
            em.op("vector", "memset", waits=wkz, inc=DV,
                  ap=khzn[64:128, :], constant=0.0)
            for ch_ in range(8):
                em.op("vector", "tensor_scalar_mul",
                      waits=[(DV, dv_stt[(8 + ch_, 3)])] + wkz,
                      inc=(DV if ch_ == 7 else None),
                      out=khzn[0:64, ch_ * N:(ch_ + 1) * N],
                      in0=kh[0:64, ch_ * N:(ch_ + 1) * N], scalar1=-1.0)
            dv_khzn = DV.n

            if DEBUG and s == 0:
                em.op("gpsimd", "dma_start", waits=[(DV, dv_vdone)], inc=GP,
                      inck=16, out=dbg_vt, in_=vt[:])
            order = [jj // 2 + 8 * (jj % 2) for jj in range(16)]
            pe_s = {}
            ac_e = {}
            pe_pv = {}
            dv_oe = {}
            gp_dn = {}
            units = 2 * NKT
            npair = NKT

            for hi, h in enumerate(order):
                ch = h % 8
                odd = h >= 8
                rowo = 64 if odd else 0
                drow = 0 if odd else 64

                def s_unit(u, hi=hi, ch=ch, odd=odd):
                    qha, kt = u // NKT, u % NKT
                    w = []
                    if hi == 0 and u == 0:
                        w += [(DV, dv_vdone), (DV, dv_negb), (DV, dv_khzn)]
                    if u >= 2:
                        w.append((AC, ac_e.get((hi, u - 2), 0)))
                    elif hi > 0:
                        w.append((AC, ac_e[(hi - 1, units - 2 + u)]))
                    for qtr in range(2):
                        ob = (u % 2) * 1024 + qtr * 512
                        qs_ = qh[:, ch * N + qha * 1024 + qtr * 512:
                                 ch * N + qha * 1024 + (qtr + 1) * 512]
                        ks_ = ch * N + kt * 128
                        if not odd:
                            em.op("tensor", "matmul", waits=w if qtr == 0 else (),
                                  inc=(PE if qtr == 1 else None),
                                  out=ps_a[:, ob:ob + 512],
                                  lhsT=khzn[:, ks_:ks_ + 128], rhs=qs_,
                                  start=True, stop=True)
                        else:
                            em.op("tensor", "matmul", waits=w if qtr == 0 else (),
                                  out=ps_a[:, ob:ob + 512],
                                  lhsT=kh[:, ks_:ks_ + 128], rhs=qs_,
                                  start=True, stop=False)
                            em.op("tensor", "matmul",
                                  inc=(PE if qtr == 1 else None),
                                  out=ps_a[:, ob:ob + 512],
                                  lhsT=khzn[:, ks_:ks_ + 128], rhs=qs_,
                                  start=False, stop=True)
                    pe_s[(hi, u)] = PE.n

                def e_unit(u, hi=hi, odd=odd):
                    w = [(PE, pe_s[(hi, u)])]
                    if u >= 4:
                        w.append((PE, pe_pv[(hi, u - 4)]))
                    elif hi > 0:
                        w.append((PE, pe_pv[(hi - 1, units - 4 + u)]))
                    em.op("scalar", "activation", waits=w, inc=AC,
                          out=ptb[:, (u % 4) * 1024:(u % 4) * 1024 + 1024],
                          in_=ps_a[:, (u % 2) * 1024:(u % 2) * 1024 + 1024],
                          func=AF.Exp, bias=negb[:],
                          scale=(0.125 if odd else -0.125))
                    ac_e[(hi, u)] = AC.n

                def pv_unit(u, first, hi=hi, h=h, odd=odd):
                    qha, kt = u // NKT, u % NKT
                    w = [(AC, ac_e[(hi, u)])]
                    if kt == 0 and hi > 0:
                        w.append((DV, dv_oe[(hi - 1, qha)]))
                    vcol = kt * 1040 + 130 * (h % 8) + (65 if odd else 0)
                    for qtr in range(2):
                        ob = qha * 1024 + qtr * 512
                        pcol = (u % 4) * 1024 + qtr * 512
                        em.op("tensor", "matmul", waits=w if qtr == 0 else (),
                              inc=(PE if qtr == 1 else None),
                              out=ps_b[0:65, ob:ob + 512],
                              lhsT=vt[:, vcol:vcol + 65],
                              rhs=ptb[:, pcol:pcol + 512],
                              start=(kt == 0), stop=(kt == NKT - 1))
                    pe_pv[(hi, u)] = PE.n

                def evac_region(qha, hi=hi, h=h, odd=odd):
                    w = [(PE, pe_pv[(hi, qha * NKT + NKT - 1)])]
                    w2 = []
                    if hi >= 1:
                        w2 = [(GP, gp_dn[hi - 1])]
                    if not odd:
                        em.op("vector", "tensor_copy", waits=w, inc=DV,
                              out=qh[0:64, (h % 8) * N + qha * 1024:
                                     (h % 8) * N + (qha + 1) * 1024],
                              in_=ps_b[0:64, qha * 1024:(qha + 1) * 1024])
                        em.op("vector", "tensor_copy", waits=w2, inc=DV,
                              out=dscr[64:65, qha * 1024:(qha + 1) * 1024],
                              in_=ps_b[64:65, qha * 1024:(qha + 1) * 1024])
                    else:
                        em.op("vector", "tensor_copy", waits=w + w2, inc=DV,
                              out=dscr[0:65, qha * 1024:(qha + 1) * 1024],
                              in_=ps_b[0:65, qha * 1024:(qha + 1) * 1024])
                    dv_oe[(hi, qha)] = DV.n

                s_unit(0)
                s_unit(1)
                for u in range(units):
                    e_unit(u)
                    pv_unit(u, True)
                    if u + 2 < units:
                        s_unit(u + 2)
                    if u == NKT - 1:
                        evac_region(0)
                evac_region(1)
                if odd:
                    em.op("gpsimd", "dma_start",
                          waits=[(DV, dv_oe[(hi, 1)])], inc=GP, inck=16,
                          out=qh[64:128, (h % 8) * N:(h % 8 + 1) * N],
                          in_=dscr[0:64, :])
                em.op("gpsimd", "dma_start",
                      waits=[(DV, dv_oe[(hi, 1)])], inc=GP, inck=16,
                      out=dn[h:h + 1, :], in_=dscr[64:65, :])
                gp_dn[hi] = GP.n

            if DEBUG and s == 0:
                em.op("gpsimd", "dma_start", waits=[(GP, gp_dn[15])],
                      inc=GP, inck=16, out=dbg_dn, in_=dn[:])
                em.op("gpsimd", "dma_start", waits=[(GP, gp_dn[15])], inc=GP,
                      inck=16, out=dbg_oun, in_=qh[:])
            em.op("scalar", "activation", waits=[(GP, gp_dn[15])],
                  out=rdnf, in_=dn[:], func=AF.Ln, bias=0.0, scale=1.0)
            em.op("scalar", "activation", inc=AC,
                  out=rdnb, in_=rdnf, func=AF.Exp, bias=0.0, scale=-1.0)
            dv_rdn = None
            ac_rdn = AC.n
            pe_fd = {}
            dv_on = {}
            for j in range(NCH):
                for ha in range(2):
                    w = [(AC, ac_rdn)]
                    if (j, ha) != (0, 0):
                        pj, ph = (j, 0) if ha == 1 else (j - 1, 1)
                        w.append((DV, dv_on[(pj, ph)]))
                    for qtr in range(2):
                        em.op("tensor", "matmul", waits=w if qtr == 0 else (),
                              inc=(PE if qtr == 1 else None),
                              out=ps_a[:, ha * 1024 + qtr * 512:
                                       ha * 1024 + (qtr + 1) * 512],
                              lhsT=dsel_s[:, j * 128:(j + 1) * 128],
                              rhs=rdnb[:, ha * 1024 + qtr * 512:
                                       ha * 1024 + (qtr + 1) * 512],
                              start=True, stop=True)
                    pe_fd[(j, ha)] = PE.n
                    em.op("vector", "tensor_tensor",
                          waits=[(PE, pe_fd[(j, ha)])], inc=DV,
                          out=kh[:, j * N + ha * 1024:j * N + (ha + 1) * 1024],
                          in0=qh[:, j * N + ha * 1024:j * N + (ha + 1) * 1024],
                          in1=ps_a[:, ha * 1024:(ha + 1) * 1024], op=ALU.mult)
                    dv_on[(j, ha)] = DV.n
            dv_onorm = DV.n

            if DEBUG and s == 0:
                em.op("gpsimd", "dma_start", waits=[(DV, dv_onorm)], inc=GP,
                      inck=16, out=dbg_onr, in_=kh[:])
            dv_clip = {}
            dv_epi = {}
            pe_ffn = {}
            st_done = {}
            for t in range(NT):
                p = t % 2
                w = [(DV, dv_onorm)]
                if t >= 2:
                    w.append((DV, dv_epi[2 * (t - 2) + 1]))
                em.op("sync", "dma_start", waits=w, inc=SY, inck=16,
                      out=xio[:, p * D:(p + 1) * D],
                      in_=x_d[(s * NT + t) * 128:(s * NT + t + 1) * 128, :])
                sy_x2 = SY.n
                w = []
                if t >= 2:
                    w.append((PE, pe_ffn[t - 2]))
                em.op("sync", "dma_start", waits=w, inc=SY, inck=16,
                      out=xn2t[:, p * D:(p + 1) * D].rearrange(
                          "p (c n) -> p c n", c=NCH),
                      in_=xnsave_d.rearrange("p (c n) -> p c n", c=NCH)[
                          :, :, t * 128:(t + 1) * 128])
                sy_xn2 = SY.n
                xt = xio[:, p * D:(p + 1) * D]
                for fh in range(2):
                    ci = 2 * t + fh
                    sb = (ci % 4) * 512
                    gb = (ci % 4) * 512
                    w = [(DV, dv_onorm), (GP, gp_wfix)]
                    if ci >= 4:
                        w.append((DV, dv_epi[ci - 4]))
                    for j in range(NCH):
                        em.op("tensor", "matmul", waits=w if j == 0 else (),
                              out=ps_b[:, sb:sb + 512],
                              lhsT=kh[:, j * N + t * 128:j * N + (t + 1) * 128],
                              rhs=wproj_bf[:, j * D + fh * 512:
                                           j * D + fh * 512 + 512],
                              start=(j == 0), stop=False)
                    em.op("tensor", "matmul",
                          out=ps_b[:, sb:sb + 512], lhsT=onesb_s[:],
                          rhs=bprow[:, fh * 512:fh * 512 + 512],
                          start=False, stop=True)
                    w = [(SY, sy_xn2)]
                    if ci >= 4:
                        w.append((DV, dv_clip[ci - 4]))
                    for j in range(NCH):
                        em.op("tensor", "matmul", waits=w if j == 0 else (),
                              out=ps_a[:, gb:gb + 512],
                              lhsT=xn2t[:, p * D + j * 128:
                                        p * D + (j + 1) * 128],
                              rhs=ws_bf[:, j * D + fh * 512:
                                        j * D + fh * 512 + 512],
                              start=(j == 0), stop=False)
                    em.op("tensor", "matmul", inc=PE,
                          out=ps_a[:, gb:gb + 512], lhsT=onesb_s[:],
                          rhs=fbrow[:, fh * 512:fh * 512 + 512],
                          start=False, stop=True)
                    pe_sg = PE.n
                    pe_ffn[t] = PE.n
                    em.op("vector", "tensor_scalar", waits=[(PE, pe_sg)],
                          inc=DV,
                          out=gclip[:, fh * 512:(fh + 1) * 512],
                          in0=ps_a[:, gb:gb + 512], scalar1=-100.0,
                          scalar2=100.0, op0=ALU.max, op1=ALU.min)
                    dv_clip[ci] = DV.n
                    w = [(DV, dv_clip[ci])]
                    if ci >= 2:
                        w.append((STm, st_done[ci - 2]))
                    em.op("scalar", "activation", waits=w, inc=AC,
                          out=gact[:, fh * 512:(fh + 1) * 512],
                          in_=gclip[:, fh * 512:(fh + 1) * 512], func=AF.Silu)
                    em.op("vector", "scalar_tensor_tensor", waits=[(AC, AC.n)],
                          inc=DV,
                          out=gclip[:, fh * 512:(fh + 1) * 512],
                          in0=ps_b[:, sb:sb + 512], scalar=1.0,
                          in1=gact[:, fh * 512:(fh + 1) * 512],
                          op0=ALU.mult, op1=ALU.mult)
                    em.op("vector", "tensor_tensor",
                          waits=[(DV, DV.n), (SY, sy_x2)], inc=DV,
                          out=gact[:, fh * 512:(fh + 1) * 512],
                          in0=gclip[:, fh * 512:(fh + 1) * 512],
                          in1=xt[:, fh * 512:(fh + 1) * 512], op=ALU.add)
                    dv_epi[ci] = DV.n
                    em.op("sync", "dma_start", waits=[(DV, dv_epi[ci])],
                          inc=STm, inck=16,
                          out=y_d[(s * NT + t) * 128:(s * NT + t + 1) * 128,
                                  fh * 512:(fh + 1) * 512],
                          in_=gact[:, fh * 512:(fh + 1) * 512])
                    st_done[ci] = STm.n
                    st_total[0] = STm.n

            em.op("sync", "nop", waits=[(STm, st_total[0])], inc=BAR)
            em.op("gpsimd", "nop", waits=[(GP, GP.n)], inc=BAR)
            em.op("tensor", "nop", inc=BAR)
            em.op("vector", "nop", inc=BAR)
            em.op("scalar", "nop", inc=BAR)

        em.op("sync", "nop", waits=[(STm, st_total[0])])

        @block.sync
        def _(sync):
            for fn in em.events["sync"]:
                fn(sync)

        @block.gpsimd
        def _(gpsimd):
            for fn in em.events["gpsimd"]:
                fn(gpsimd)

        @block.tensor
        def _(tensor):
            for fn in em.events["tensor"]:
                fn(tensor)

        @block.vector
        def _(vector):
            for fn in em.events["vector"]:
                fn(vector)

        @block.scalar
        def _(scalar):
            for fn in em.events["scalar"]:
                fn(scalar)

    return nc


_NC_CACHE = None


def _get_nc():
    global _NC_CACHE
    if _NC_CACHE is None:
        _NC_CACHE = build_nc()
    return _NC_CACHE


def kernel(L0_attn_x, L0_attn_Wqkv, L0_attn_Wproj, L0_attn_bproj,
           L0_attn_q_scale, L0_attn_k_scale, L0_ffn_weights, L0_ffn_scale,
           L0_ffn_bias):
    x = np.asarray(L0_attn_x, dtype=np.float32)
    wqkv = np.asarray(L0_attn_Wqkv, dtype=np.float32)
    wproj = np.asarray(L0_attn_Wproj, dtype=np.float32)
    bproj = np.asarray(L0_attn_bproj, dtype=np.float32)
    qs = np.asarray(L0_attn_q_scale, dtype=np.float32)
    ks = np.asarray(L0_attn_k_scale, dtype=np.float32)
    wf = np.asarray(L0_ffn_weights, dtype=np.float32)
    fsc = np.asarray(L0_ffn_scale, dtype=np.float32)
    fbi = np.asarray(L0_ffn_bias, dtype=np.float32)

    wq = wqkv[:, 0:D].reshape(D, H, HD)
    wk = wqkv[:, D:2 * D].reshape(D, H, HD)
    wv = wqkv[:, 2 * D:3 * D].reshape(D, H, HD)
    cols = []
    for part in (wq, wk, wv):
        for j in range(8):
            cols.append(part[:, j, :])
            cols.append(part[:, j + 8, :])
    wqkv_p = np.ascontiguousarray(np.concatenate(cols, axis=1))
    wp = wproj.reshape(H, HD, D)
    rows = []
    for j in range(8):
        rows.append(wp[j])
        rows.append(wp[j + 8])
    wproj_p = np.ascontiguousarray(np.concatenate(rows, axis=0))
    wt = np.ascontiguousarray(wf.T)
    qspat = np.ascontiguousarray(np.tile(qs, 2)[:, None])
    kspat = np.ascontiguousarray(np.tile(ks, 2)[:, None])
    fscale_b = np.ascontiguousarray(np.broadcast_to(fsc[None, :], (128, D)))

    bf = ml_dtypes.bfloat16
    ident = np.eye(128, dtype=np.float32).astype(bf)
    sstat = np.zeros((128, 32), np.float32)
    for i in range(4):
        sstat[0:64, 8 * i + 2 * i] = 1.0
        sstat[64:128, 8 * i + 2 * i + 1] = 1.0
    sstat = sstat.astype(bf)
    fstat = np.zeros((8, 512), np.float32)
    for i in range(4):
        fstat[2 * i, 128 * i:128 * i + 64] = 1.0
        fstat[2 * i + 1, 128 * i + 64:128 * i + 128] = 1.0
    fstat = fstat.astype(bf)
    dsel = np.zeros((16, 1024), np.float32)
    for j in range(8):
        dsel[j, j * 128:j * 128 + 64] = 1.0
        dsel[j + 8, j * 128 + 64:j * 128 + 128] = 1.0
    dsel = dsel.astype(bf)
    onesb = np.ones((1, 128), np.float32).astype(bf)

    nc = _get_nc()
    xr = x.reshape(32, N, D)
    in_maps = []
    for c in range(NCORES):
        in_maps.append({
            "x": np.ascontiguousarray(
                xr[SEQS * c:SEQS * (c + 1)].reshape(SEQS * N, D)),
            "wqkv": wqkv_p, "wproj": wproj_p, "wt": wt,
            "bproj": bproj, "fbias": fbi,
            "qs": qs[None, :], "ks": ks[None, :],
            "qspat": qspat, "kspat": kspat, "fscale": fscale_b,
            "ident": ident, "sstat": sstat, "fstat": fstat,
            "dsel": dsel, "onesb": onesb,
        })
    res = run_bass_kernel_spmd(nc, in_maps, core_ids=list(range(NCORES)))
    out = np.concatenate([r["y"].reshape(SEQS, N, D) for r in res.results],
                         axis=0)
    return out
